# version 06
# baseline (speedup 1.0000x reference)
import numpy as np
import ml_dtypes
import concourse.bass as bass
import concourse.mybir as mybir
from concourse.bass_utils import run_bass_kernel_spmd
from contextlib import ExitStack

F32 = mybir.dt.float32
BF16 = mybir.dt.bfloat16
I8 = mybir.dt.int8
I32 = mybir.dt.int32
U32 = mybir.dt.uint32
ALU = mybir.AluOpType
AF = mybir.ActivationFunctionType
AX = mybir.AxisListType

B, T, D = 4, 8192, 2048
PD, NCI = 12320, 97
TC, NCHUNK, THALF = 512, 8, 4096
SPIN_MODE = "while"
N_POLL = 1536
NCORES = 8
NSUB = 4
KCV = 4
CI_Z0, CI_B0, CI_C0, CI_DT = 16, 32, 64, 96


def build_program(debug=False, nchunk=NCHUNK, trunc=None):
    th = nchunk * TC
    nc = bass.Bass("TRN2", target_bir_lowering=False, debug=False,
                   num_devices=NCORES, detect_race_conditions=False)

    dri = lambda n, sh, dt: nc.dram_tensor(n, sh, dt, kind="ExternalInput").ap()
    xT = dri("xT", [D, th], F32)
    xh3 = dri("xh3", [D, 4], F32)
    win_t = dri("win_t", [NCI * 128, 2048], F32)
    wout_t = dri("wout_t", [16 * 128, 2048], F32)
    wg_t = dri("wg_t", [64 * 128, 2048], I8)
    wu_t = dri("wu_t", [64 * 128, 2048], I8)
    wd_t = dri("wd_t", [16 * 128, 8192], I8)
    conv_t = dri("conv_t", [80 * 128, 512], F32)
    alog = dri("alog", [32, 1], F32)
    dtb = dri("dtb", [32, 1], F32)
    sg_i = dri("sg_i", [128, 64], F32)
    su_i = dri("su_i", [128, 64], F32)
    sd_i = dri("sd_i", [128, 16], F32)
    ident_bf = dri("ident_bf", [128, 128], BF16)
    ident_f = dri("ident_f", [128, 128], F32)
    umask_i = dri("umask_i", [128, 128], BF16)
    onesr_i = dri("onesr_i", [1, 128], F32)
    onec_i = dri("onec_i", [128, 1], F32)
    onecb_i = dri("onecb_i", [128, 1], BF16)
    negonesr_i = dri("negonesr_i", [1, 128], F32)
    consts_i = dri("consts_i", [128, 8], F32)
    xidx_i = dri("xidx_i", [128, 1], I32)
    fidx_i = dri("fidx_i", [2, 1], I32)
    nonce_i = dri("nonce_i", [2, 1], U32)
    hmask_i = dri("hmask_i", [128, 1], F32)

    out_d = nc.dram_tensor("out_d", [D, th], F32, kind="ExternalOutput").ap()

    okind = "ExternalOutput" if debug else "Internal"
    drs = lambda n, sh, dt, kind="Internal": nc.dram_tensor(n, sh, dt, kind=kind).ap()
    winb = drs("winb", [NCI * 128, 2048], BF16)
    woutb = drs("woutb", [16 * 128, 2048], BF16)
    wgb = drs("wgb", [64 * 128, 2048], BF16)
    wub = drs("wub", [64 * 128, 2048], BF16)
    wdb = drs("wdb", [16 * 128, 8192], BF16)
    convb = drs("convb", [80 * 128, 512], BF16)
    ypd = drs("ypd", [th, 2048], BF16, okind)
    ccd = drs("ccd", [th, 4096], BF16, okind)
    zsd = drs("zsd", [th, 2048], BF16, okind)
    sdram = drs("sdram", [32, 520], F32)
    hdbg = drs("hdbg", [128, 2048], F32, okind) if debug else None
    xnd = drs("xnd", [128, 8192], BF16, okind) if debug else None
    flgd = drs("flgd", [1, 4], U32, okind) if debug else None
    drepd = drs("drepd", [128, 128], F32, okind) if debug else None
    dctd = drs("dctd", [128, 128], F32, okind) if debug else None
    dttd = drs("dttd", [128, 128], F32, okind) if debug else None
    sTd = drs("sTd", [128, 128], F32, okind) if debug else None
    mtxd = drs("mtxd", [128, 128], F32, okind) if debug else None
    gmd = drs("gmd", [128, 128], BF16, okind) if debug else None
    dxtd = drs("dxtd", [128, 64], BF16, okind) if debug else None
    b2td = drs("b2td", [128, 128], BF16, okind) if debug else None
    argd = drs("argd", [128, 128], F32, okind) if debug else None
    sfd2 = drs("sfd2", [1, 4128], F32, okind) if debug else None
    trad = drs("trad", [128, 128], F32, okind) if debug else None
    psd = drs("psd", [128, 128], F32, okind) if debug else None
    idd = drs("idd", [128, 128], F32, okind) if debug else None
    umd = drs("umd", [128, 128], F32, okind) if debug else None
    dtd = drs("dtd", [32, 512], F32, okind) if debug else None
    sfd = drs("sfd", [1, 4128], F32, okind) if debug else None
    xcd = drs("xcd", [128, 8192], BF16, okind) if debug else None
    hx = nc.dram_tensor("hx", [260, 2048], BF16, addr_space="Shared").ap()
    hxf = nc.dram_tensor("hxf", [4, 2], U32, addr_space="Shared").ap()

    stk = ExitStack()
    SB = lambda n, sh, dt: stk.enter_context(nc.sbuf_tensor(n, sh, dt))
    s_ident = SB("s_ident", [128, 128], BF16)
    s_identf = SB("s_identf", [128, 128], F32)
    s_umask = SB("s_umask", [128, 128], BF16)
    s_ones = SB("s_ones", [1, 128], F32)
    s_onec = SB("s_onec", [128, 1], F32)
    s_onecb = SB("s_onecb", [128, 1], BF16)
    s_nones = SB("s_nones", [1, 128], F32)
    s_consts = SB("s_consts", [128, 8], F32)
    s_alog = SB("s_alog", [32, 1], F32)
    s_A = SB("s_A", [32, 1], F32)
    s_dtbv = SB("s_dtbv", [32, 1], F32)
    s_sg = SB("s_sg", [128, 64], F32)
    s_su = SB("s_su", [128, 64], F32)
    s_sd = SB("s_sd", [128, 16], F32)
    s_sT = SB("s_sT", [128, 1024], F32)
    s_h = SB("s_h", [128, 2048], F32)
    s_hb = SB("s_hb", [128, 2048], BF16)
    s_sbuf = SB("s_sbuf", [32, 514], F32)
    SCR = SB("SCR", [128, 4160], F32)
    s_sflat = SCR[0:1, 0:4128]
    s_dtT = SB("s_dtT", [128, 128], F32)
    s_dCT = SB("s_dCT", [128, 128], F32)
    s_expsT = SB("s_expsT", [128, 128], F32)
    s_drep = SB("s_drep", [128, 128], F32)
    s_halo = SB("s_halo", [128, 240], BF16)
    s_xh3 = SB("s_xh3", [128, 64], BF16)
    s_x3f = SB("s_x3f", [128, 64], F32)
    s_xidx = SB("s_xidx", [128, 1], I32)
    s_fidx = SB("s_fidx", [2, 1], I32)
    s_nonce = SB("s_nonce", [2, 1], U32)
    s_hmask = SB("s_hmask", [128, 1], F32)
    s_flag = SB("s_flag", [1, 1], U32)
    s_row = SB("s_row", [1, 512], F32)
    s_row2 = SB("s_row2", [1, 512], F32)
    s_rowt = SB("s_rowt", [1, 512], F32)
    s_rms = SB("s_rms", [128, 32], F32)
    DT3 = SB("DT3", [128, 1536], F32)
    s_dt = DT3[0:32, 0:512]
    s_dt2 = DT3[0:32, 512:1024]
    s_dcr = DT3[0:32, 1024:1536]
    s_trA = SB("s_trA", [128, 128], BF16)
    s_trB = SB("s_trB", [128, 128], BF16)
    s_yis = SB("s_yis", [128, 512], BF16)
    s_otile1 = SB("s_otl1", [128, 512], F32)
    s_sq0 = SB("s_sq0", [128, 512], BF16)
    s_sq = [s_sq0, s_sq0]
    s_gm = [SB(f"s_gm{i}", [128, 128], BF16) for i in range(2)]
    s_mtx = [SB(f"s_mtx{i}", [128, 128], F32) for i in range(2)]
    s_mtm = [SB(f"s_mtm{i}", [128, 128], BF16) for i in range(2)]
    s_dxt = [SB(f"s_dxt{i}", [128, 64], BF16) for i in range(2)]
    s_b2t = [SB(f"s_b2t{i}", [128, 128], BF16) for i in range(2)]
    A1 = SB("A1", [128, 16 * TC], BF16)
    A2 = SB("A2", [128, 16 * TC], BF16)
    A3 = SB("A3", [128, 32 * TC], BF16)
    A4 = SB("A4", [128, 32 * TC], BF16)
    A5 = SB("A5", [128, 16 * TC], BF16)
    A6 = SB("A6", [128, 2048], BF16)
    A7 = [SB(f"A7_{i}", [128, 2048], BF16) for i in range(2)]
    A8 = [SB(f"A8_{i}", [128, 2048], BF16) for i in range(2)]
    A9 = SB("A9", [128, 4096], BF16)
    s_pc = [SB(f"s_pc{i}", [128, 4 + TC], BF16) for i in range(2)]
    s_hin = A6
    s_xf = [SCR[:, 3072:3584], SCR[:, 3584:4096]]
    s_otile = [DT3[:, 1024:1536], s_otile1]
    s_gt = s_pc[0][:, 0:512]

    pb = [stk.enter_context(nc.psum_tensor(f"pb{i}", [128, 512], F32)) for i in range(6)]
    pbf = [stk.enter_context(nc.psum_tensor(f"pbf{i}", [128, 1024], BF16)) for i in range(2)]

    sp_spin = stk.enter_context(nc.semaphore("sp_spin"))

    sems = {}
    hw = {}
    ops = {"pe": [], "dv": [], "ac": [], "sy": [], "gp": []}
    seqc = [0]
    ENGSEM = {"pe": "Epe", "dv": "Edv", "ac": "Eac"}

    def sem(name):
        if name not in sems:
            sems[name] = [stk.enter_context(nc.semaphore(name)), 0]
        return sems[name]

    for s in ENGSEM.values():
        sem(s)

    class Buf:
        __slots__ = ("w", "r")
        def __init__(self):
            self.w = None
            self.r = []
    bufs = {}

    def bget(name):
        if name not in bufs:
            bufs[name] = Buf()
        return bufs[name]

    def E(eng, reads=(), writes=(), deps=(), dma=None):
        def deco(fn):
            dd = list(deps)
            for r in reads:
                b = bget(r)
                if b.w is not None:
                    dd.append(b.w)
            for w in writes:
                b = bget(w)
                if b.w is not None:
                    dd.append(b.w)
                dd.extend(b.r)
            if dma is None:
                s = sem(ENGSEM[eng])
                s[1] += 1
                tok = (ENGSEM[eng], s[1])
            else:
                s = sem(dma)
                s[1] += 16
                tok = (dma, s[1])
            seqc[0] += 1
            ops[eng].append((fn, tuple(set(dd)), tok, seqc[0]))
            for r in reads:
                bget(r).r.append(tok)
            for w in writes:
                b = bget(w)
                b.w = tok
                b.r = []
            return tok
        return deco

    def all_tokens():
        return [(n, s[1]) for n, s in sems.items() if s[1] > 0]

    def barrier():
        toks = all_tokens()
        for eng in ops:
            if eng in ENGSEM:
                seqc[0] += 1
                ops[eng].append(((lambda e: e.nop()), tuple(toks), None, seqc[0]))
            else:
                seqc[0] += 1
                ops[eng].append(((lambda e: None), tuple(toks), None, seqc[0]))

    def replay(eng_obj, eng):
        for fn, dd, tok, sq in ops[eng]:
            if trunc is not None and sq > trunc:
                continue
            best = {}
            for sname, val in dd:
                if best.get(sname, 0) < val:
                    best[sname] = val
            for sname, val in sorted(best.items()):
                if eng == "pe" and sname == "Epe":
                    continue
                key = (eng, sname)
                if hw.get(key, 0) >= val:
                    continue
                eng_obj.wait_ge(sems[sname][0], val)
                hw[key] = val
            inst = fn(eng_obj)
            if inst is not None and tok is not None:
                inst.then_inc(sems[tok[0]][0], 16 if tok[0].startswith("d_") else 1)

    CL = lambda p=1: s_consts[0:p, 0:1]
    CR = lambda p=1: s_consts[0:p, 1:2]
    CH_ = lambda p=1: s_consts[0:p, 2:3]
    CD = lambda p=1: s_consts[0:p, 3:4]
    C64 = lambda p=1: s_consts[0:p, 4:5]
    C1 = lambda p=1: s_consts[0:p, 5:6]

    small = [(s_ident, ident_bf, "c_id"), (s_identf, ident_f, "c_idf"),
             (s_umask, umask_i, "c_um"), (s_ones, onesr_i, "c_on"),
             (s_nones, negonesr_i, "c_no"), (s_onec, onec_i, "c_oc"), (s_onecb, onecb_i, "c_ob"), (s_consts, consts_i, "c_ct"),
             (s_alog, alog, "c_al"), (s_dtbv, dtb, "c_db"),
             (s_sg, sg_i, "c_sg"), (s_su, su_i, "c_su"),
             (s_sd, sd_i, "c_sd"), (s_xidx, xidx_i, "c_xi"),
             (s_fidx, fidx_i, "c_fi"), (s_nonce, nonce_i, "c_nn"),
             (s_hmask, hmask_i, "c_hm")]
    for dst, src, nm in small:
        @E("sy", writes=(nm,), dma="d_small")
        def _(e, dst=dst, src=src):
            return e.dma_start(dst[:], src[:])
    for _, _, nm in small:
        bget(nm).w = ("d_small", sems["d_small"][1])
    SMALLT = all_tokens()

    @E("gp", writes=("xb",), dma="d_xb")
    def _(e):
        return e.dma_start(
            A1[:].rearrange("p (k t) -> p k t", t=TC),
            xT[:, 0:TC].rearrange("(k p) t -> p k t", p=128))

    @E("gp", writes=("x3f",), dma="d_x3")
    def _(e):
        return e.dma_start(
            s_x3f[:].rearrange("p (k f) -> p k f", f=4),
            xh3.rearrange("(k p) f -> p k f", p=128))

    def cast_stream(name, src, dst, nrows, pieces):
        wn = []
        rp = nrows // pieces
        for i in range(pieces):
            @E("gp", writes=(f"{name}cast{i}",), dma="d_" + name)
            def _(e, i=i):
                return e.dma_start(dst[i * rp:(i + 1) * rp, :], src[i * rp:(i + 1) * rp, :])
            wn.append(f"{name}cast{i}")
        for n_ in wn:
            bget(n_).w = ("d_" + name, sems["d_" + name][1])
        return tuple(wn)

    def cast_rows(name, src, dst, ranges):
        wn = []
        for i, (r0, r1) in enumerate(ranges):
            @E("gp", writes=(f"{name}cast{i}",), dma="d_" + name)
            def _(e, r0=r0, r1=r1):
                return e.dma_start(dst[r0:r1, :], src[r0:r1, :])
            wn.append(f"{name}cast{i}")
        for n_ in wn:
            bget(n_).w = ("d_" + name, sems["d_" + name][1])
        return tuple(wn)

    WINGRP = {}
    CVGRP = {}
    prev_tok = [()]

    def chain_cast(name, src, dst, ranges):
        toks = cast_rows_dep(name, src, dst, ranges, prev_tok[0])
        prev_tok[0] = toks
        return toks

    def cast_rows_dep(name, src, dst, ranges, dep):
        wn = []
        for i, (r0, r1) in enumerate(ranges):
            @E("gp", reads=dep if i == 0 else (), writes=(f"{name}cast{i}",), dma="d_" + name)
            def _(e, r0=r0, r1=r1):
                return e.dma_start(dst[r0:r1, :], src[r0:r1, :])
            wn.append(f"{name}cast{i}")
        for n_ in wn:
            bget(n_).w = ("d_" + name, sems["d_" + name][1])
        return tuple(wn)

    wa = chain_cast("wA", win_t, winb, [(96 * 128, 97 * 128), (16 * 128, 32 * 128)])
    for ci in [96] + list(range(16, 32)):
        WINGRP[ci] = wa
    cvt = chain_cast("cv0", conv_t, convb, [(0, 20 * 128)])
    for gi in range(5):
        CVGRP[gi] = cvt
    wb_ = chain_cast("wB", win_t, winb, [(0, 16 * 128)])
    for ci in range(16):
        WINGRP[ci] = wb_
    cvt = chain_cast("cv1", conv_t, convb, [(20 * 128, 40 * 128)])
    for gi in range(5, 10):
        CVGRP[gi] = cvt
    wc = chain_cast("wC0", win_t, winb, [(32 * 128, 48 * 128)])
    for ci in range(32, 48):
        WINGRP[ci] = wc
    cvt = chain_cast("cv2", conv_t, convb, [(40 * 128, 60 * 128)])
    for gi in range(10, 15):
        CVGRP[gi] = cvt
    wc = chain_cast("wC1", win_t, winb, [(48 * 128, 64 * 128)])
    for ci in range(48, 64):
        WINGRP[ci] = wc
    cvt = chain_cast("cv3", conv_t, convb, [(60 * 128, 80 * 128)])
    for gi in range(15, 20):
        CVGRP[gi] = cvt
    wc = chain_cast("wC2", win_t, winb, [(64 * 128, 80 * 128)])
    for ci in range(64, 80):
        WINGRP[ci] = wc
    wc = chain_cast("wC3", win_t, winb, [(80 * 128, 96 * 128)])
    for ci in range(80, 96):
        WINGRP[ci] = wc

    @E("ac", reads=("c_al",), writes=("A",), deps=SMALLT)
    def _(e):
        return e.activation(s_A[0:32, :], s_alog[0:32, :], AF.Exp)

    @E("dv", reads=("A",), writes=("A",))
    def _(e):
        return e.tensor_scalar(s_A[0:32, :], s_A[0:32, :], -1.0, None, ALU.mult)

    @E("dv", writes=("h",))
    def _(e):
        return e.memset(s_h[:], 0.0)

    @E("dv", writes=("hb",))
    def _(e):
        return e.memset(s_hb[:], 0.0)

    @E("dv", writes=("halo",))
    def _(e):
        return e.memset(s_halo[:], 0.0)

    @E("dv", writes=("s",))
    def _(e):
        return e.memset(s_sbuf[:, 0:1], 0.0)

    @E("dv", writes=("trA",))
    def _(e):
        return e.memset(s_trA[:], 0.0)

    @E("dv", writes=("trB",))
    def _(e):
        return e.memset(s_trB[:], 0.0)

    for k in range(16):
        @E("pe", reads=("x3f", "c_oc"), writes=("bk4",), deps=SMALLT)
        def _(e, k=k):
            return e.matmul(pb[4][0:1, 0:4], s_onec[:, 0:1], s_x3f[:, k * 4:(k + 1) * 4],
                            start=(k == 0), stop=(k == 15))

    @E("ac", reads=("x3f",), writes=("x3n",))
    def _(e):
        return e.square(s_xh3[:, :], s_x3f[:, :])

    for k in range(16):
        @E("pe", reads=("x3n", "c_ob"), writes=("bk5",), deps=SMALLT)
        def _(e, k=k):
            return e.matmul(pb[5][0:1, 0:4], s_onecb[:, 0:1], s_xh3[:, k * 4:(k + 1) * 4],
                            start=(k == 0), stop=(k == 15))

    @E("dv", reads=("bk4", "c_ct"), writes=("rowA",))
    def _(e):
        return e.tensor_scalar(s_row[0:1, 0:4], pb[4][0:1, 0:4], CD(), None, ALU.mult)

    @E("dv", reads=("bk5", "c_ct"), writes=("rowB",))
    def _(e):
        return e.tensor_scalar(s_row2[0:1, 0:4], pb[5][0:1, 0:4], CD(), None, ALU.mult)

    @E("dv", reads=("rowA",), writes=("rowT",))
    def _(e):
        return e.tensor_tensor(s_rowt[0:1, 0:4], s_row[0:1, 0:4], s_row[0:1, 0:4], ALU.mult)

    @E("dv", reads=("rowT", "rowB"), writes=("rowB",))
    def _(e):
        return e.tensor_tensor(s_row2[0:1, 0:4], s_row2[0:1, 0:4], s_rowt[0:1, 0:4], ALU.subtract)

    @E("ac", reads=("rowB", "c_ct"), writes=("rowB",))
    def _(e):
        return e.activation(s_row2[0:1, 0:4], s_row2[0:1, 0:4], AF.Ln, bias=CL())

    @E("ac", reads=("rowB", "c_ct"), writes=("rowB",))
    def _(e):
        return e.activation(s_row2[0:1, 0:4], s_row2[0:1, 0:4], AF.Exp, scale=CH_())

    @E("dv", reads=("rowA", "rowB"), writes=("rowA",))
    def _(e):
        return e.tensor_tensor(s_row[0:1, 0:4], s_row[0:1, 0:4], s_row2[0:1, 0:4], ALU.mult)

    @E("dv", reads=("rowA",), writes=("rowA",))
    def _(e):
        return e.tensor_scalar(s_row[0:1, 0:4], s_row[0:1, 0:4], -1.0, None, ALU.mult)

    @E("pe", reads=("rowB", "c_on"), writes=("bk4",))
    def _(e):
        return e.matmul(pb[4][:, 0:4], s_ones[0:1, :], s_row2[0:1, 0:4], start=True, stop=True)

    @E("pe", reads=("rowA", "c_on"), writes=("bk5",))
    def _(e):
        return e.matmul(pb[5][:, 4:8], s_ones[0:1, :], s_row[0:1, 0:4], start=True, stop=True)

    for k in range(16):
        @E("dv", reads=("x3f", "bk4", "x3n"), writes=("x3n",))
        def _(e, k=k):
            return e.tensor_tensor(s_xh3[:, k * 4:(k + 1) * 4], s_x3f[:, k * 4:(k + 1) * 4],
                                   pb[4][:, 0:4], ALU.mult)
    for k in range(16):
        @E("dv", reads=("bk5", "x3n"), writes=("x3n",))
        def _(e, k=k):
            return e.tensor_tensor(s_xh3[:, k * 4:(k + 1) * 4], s_xh3[:, k * 4:(k + 1) * 4],
                                   pb[5][:, 4:8], ALU.add)

    NWIN = 2
    NCV = 2

    def ln_stats_rows(src_name, src_tile_fn, n_tiles):
        for k in range(n_tiles):
            @E("pe", reads=(src_name, "c_ob"), writes=("bk4",), deps=SMALLT)
            def _(e, k=k):
                return e.matmul(pb[4][0:1, 0:512], s_onecb[:, 0:1], src_tile_fn(k),
                                start=(k == 0), stop=(k == n_tiles - 1))
        for k in range(n_tiles):
            @E("ac", reads=(src_name,), writes=("sq",))
            def _(e, k=k):
                return e.square(s_sq[k % 2][:, :], src_tile_fn(k))

            @E("pe", reads=("sq", "c_ob"), writes=("bk5",))
            def _(e, k=k):
                return e.matmul(pb[5][0:1, 0:512], s_onecb[:, 0:1], s_sq[k % 2][:, :],
                                start=(k == 0), stop=(k == n_tiles - 1))

        @E("dv", reads=("bk4", "c_ct"), writes=("rowA",))
        def _(e):
            return e.tensor_scalar(s_row[0:1, :], pb[4][0:1, 0:512], CD(), None, ALU.mult)

        @E("dv", reads=("bk5", "c_ct"), writes=("rowB",))
        def _(e):
            return e.tensor_scalar(s_row2[0:1, :], pb[5][0:1, 0:512], CD(), None, ALU.mult)

        @E("dv", reads=("rowA",), writes=("rowT",))
        def _(e):
            return e.tensor_tensor(s_rowt[0:1, :], s_row[0:1, :], s_row[0:1, :], ALU.mult)

        @E("dv", reads=("rowT", "rowB"), writes=("rowB",))
        def _(e):
            return e.tensor_tensor(s_row2[0:1, :], s_row2[0:1, :], s_rowt[0:1, :], ALU.subtract)

        @E("ac", reads=("rowB", "c_ct"), writes=("rowB",))
        def _(e):
            return e.activation(s_row2[0:1, :], s_row2[0:1, :], AF.Ln, bias=CL())

        @E("ac", reads=("rowB", "c_ct"), writes=("rowB",))
        def _(e):
            return e.activation(s_row2[0:1, :], s_row2[0:1, :], AF.Exp, scale=CH_())

        @E("dv", reads=("rowA", "rowB"), writes=("rowA",))
        def _(e):
            return e.tensor_tensor(s_row[0:1, :], s_row[0:1, :], s_row2[0:1, :], ALU.mult)

        @E("dv", reads=("rowA",), writes=("rowA",))
        def _(e):
            return e.tensor_scalar(s_row[0:1, :], s_row[0:1, :], -1.0, None, ALU.mult)

        @E("pe", reads=("rowB", "c_on"), writes=("bk4",))
        def _(e):
            return e.matmul(pb[4][:, 0:512], s_ones[0:1, :], s_row2[0:1, :], start=True, stop=True)

        @E("pe", reads=("rowA", "c_on"), writes=("bk5",))
        def _(e):
            return e.matmul(pb[5][:, 0:512], s_ones[0:1, :], s_row[0:1, :], start=True, stop=True)

    for ch in range(nchunk):
        c0 = ch * TC

        if ch > 0:
            @E("gp", writes=("xb",), dma="d_xb")
            def _(e, c0=c0):
                return e.dma_start(
                    A1[:].rearrange("p (k t) -> p k t", t=TC),
                    xT[:, c0:c0 + TC].rearrange("(k p) t -> p k t", p=128))

        ln_stats_rows("xb", lambda k: A1[:, k * TC:(k + 1) * TC], 16)
        for k in range(16):
            @E("dv", reads=("xb", "bk4"), writes=("xn",))
            def _(e, k=k):
                return e.tensor_tensor(A2[:, k * TC:(k + 1) * TC], A1[:, k * TC:(k + 1) * TC],
                                       pb[4][:, 0:512], ALU.mult)

            @E("dv", reads=("bk5",), writes=("xn",))
            def _(e, k=k):
                return e.tensor_tensor(A2[:, k * TC:(k + 1) * TC], A2[:, k * TC:(k + 1) * TC],
                                       pb[5][:, 0:512], ALU.add)

        if debug and ch == 0:
            @E("sy", reads=("xn",), writes=("xnd",), dma="d_dbg1")
            def _(e):
                return e.dma_start(xnd[:], A2[:])

        ci_order = [96] + list(range(16, 32)) + list(range(0, 16)) + list(range(32, 96))
        strips = [(ci,) for ci in ci_order]

        nb = 0
        cv_loaded = set()
        for si, pair in enumerate(strips):
            slot = si % NWIN

            @E("sy", reads=WINGRP[pair[0]], writes=(f"win{slot}",), dma=f"d_win{slot}")
            def _(e, pair=pair, slot=slot):
                return e.dma_start(A8[slot][:], winb[pair[0] * 128:(pair[0] + 1) * 128, :])

            for ci in pair:
                if ci == CI_DT or (CI_Z0 <= ci < CI_B0):
                    continue
                ti = ci if ci < 16 else ci - 16
                gi = ti // 4
                if gi not in cv_loaded:
                    cv_loaded.add(gi)
                    cslot = gi % NCV

                    @E("sy", reads=CVGRP[gi], writes=(f"cvl{cslot}",), dma=f"d_cvl{cslot}")
                    def _(e, gi=gi, cslot=cslot):
                        return e.dma_start(
                            A9[:, cslot * 2048:(cslot + 1) * 2048].rearrange("p (g m) -> p g m", g=4),
                            convb[gi * 4 * 128:(gi + 1) * 4 * 128, :]
                            .rearrange("(g p) m -> p g m", p=128))

            for j, ci in enumerate(pair):
                bank = nb % 3
                nb += 1
                for k in range(16):
                    @E("pe", reads=(f"win{slot}", "xn"), writes=(f"bk{bank}",))
                    def _(e, slot=slot, k=k, bank=bank):
                        return e.matmul(pb[bank][:, 0:512],
                                        A8[slot][:, k * 128:(k + 1) * 128],
                                        A2[:, k * TC:(k + 1) * TC],
                                        start=(k == 0), stop=(k == 15))
                if ch == 0 and (ci < 16 or CI_B0 <= ci < CI_DT):
                    ti = ci if ci < 16 else ci - 16
                    for k in range(16):
                        @E("pe", reads=(f"win{slot}", "x3n"), writes=("bk5",))
                        def _(e, slot=slot, k=k):
                            return e.matmul(pb[5][:, 8:12],
                                            A8[slot][:, k * 128:(k + 1) * 128],
                                            s_xh3[:, k * 4:(k + 1) * 4],
                                            start=(k == 0), stop=(k == 15))

                    @E("dv", reads=("bk5",), writes=("halo",))
                    def _(e, ti=ti):
                        return e.tensor_copy(s_halo[:, ti * 3:(ti + 1) * 3], pb[5][:, 8:11])

                if ci == CI_DT:
                    @E("dv", reads=(f"bk{bank}",), writes=("rdt",))
                    def _(e, bank=bank):
                        return e.tensor_scalar(s_dt[0:32, :], pb[bank][0:32, 0:512], 1.0, None, ALU.mult)
                elif CI_Z0 <= ci < CI_B0:
                    zi = ci - CI_Z0
                    hbuf, zo = (zi // 2) % 2, zi % 2

                    @E("ac", reads=(f"bk{bank}",), writes=(f"zs{hbuf}",))
                    def _(e, bank=bank, hbuf=hbuf, zo=zo):
                        return e.activation(A6[:, hbuf * 1024 + zo * TC:hbuf * 1024 + (zo + 1) * TC],
                                            pb[bank][:, 0:512], AF.Silu)

                    if zi % 2 == 1:
                        zp = zi // 2

                        for sb_ in range(NSUB):
                            @E("sy", reads=(f"zs{hbuf}",), writes=(f"zsdw{hbuf}",), dma=f"d_zs{hbuf}")
                            def _(e, ch=ch, zp=zp, hbuf=hbuf, sb_=sb_):
                                rb_ = (ch * 4 + sb_) * 128
                                return e.dma_start(
                                    zsd[rb_:rb_ + 128, zp * 256:(zp + 1) * 256]
                                    .rearrange("p (zo t) -> p zo t", t=128),
                                    A6[:, hbuf * 1024:(hbuf + 1) * 1024]
                                    .rearrange("p (zo tt) -> p zo tt", tt=TC)[:, :, sb_ * 128:(sb_ + 1) * 128])
                else:
                    ti = ci if ci < 16 else ci - 16
                    pcs = ti % 2

                    @E("dv", reads=("halo",), writes=(f"pc{pcs}",))
                    def _(e, ti=ti, pcs=pcs):
                        return e.tensor_copy(s_pc[pcs][:, 0:3], s_halo[:, ti * 3:(ti + 1) * 3])

                    @E("ac", reads=(f"bk{bank}",), writes=(f"pc{pcs}",))
                    def _(e, bank=bank, pcs=pcs):
                        return e.copy(s_pc[pcs][:, 3:3 + TC], pb[bank][:, 0:512])

                    @E("dv", reads=(f"pc{pcs}",), writes=("halo",))
                    def _(e, ti=ti, pcs=pcs):
                        return e.tensor_copy(s_halo[:, ti * 3:(ti + 1) * 3], s_pc[pcs][:, TC:TC + 3])

                    cslot = (ti // 4) % NCV
                    for kk in range(KCV):
                        @E("pe", reads=(f"cvl{cslot}", f"pc{pcs}"), writes=("bk3",))
                        def _(e, ti=ti, kk=kk, pcs=pcs, cslot=cslot):
                            return e.matmul(
                                pb[3][:, 0:512],
                                A9[:, cslot * 2048 + (ti % 4) * 512 + kk * 128:
                                   cslot * 2048 + (ti % 4) * 512 + (kk + 1) * 128],
                                s_pc[pcs][:, kk:kk + TC],
                                start=(kk == 0), stop=(kk == KCV - 1))
                    if ci < 16:
                        dstt, dsti, wname = A5, ci, "xc"
                    elif ci < CI_C0:
                        dstt, dsti, wname = A3, ci - CI_B0, "bc"
                    else:
                        dstt, dsti, wname = A4, ci - CI_C0, "cc"

                    @E("ac", reads=("bk3",), writes=(wname,))
                    def _(e, dstt=dstt, dsti=dsti):
                        return e.activation(dstt[:, dsti * TC:(dsti + 1) * TC],
                                            pb[3][:, 0:512], AF.Silu)

        @E("dv", reads=("rdt", "c_db"), writes=("rdt",))
        def _(e):
            return e.tensor_scalar(s_dt[0:32, :], s_dt[0:32, :], s_dtbv[0:32, :], None, ALU.add)

        @E("dv", reads=("rdt",), writes=("rdt2",))
        def _(e):
            return e.tensor_scalar(s_dt2[0:32, :], s_dt[0:32, :], -1.0, None, ALU.mult)

        @E("dv", reads=("rdt", "rdt2"), writes=("rdt2",))
        def _(e):
            return e.tensor_tensor(s_dt2[0:32, :], s_dt[0:32, :], s_dt2[0:32, :], ALU.min)

        @E("ac", reads=("rdt2",), writes=("rdt2",))
        def _(e):
            return e.activation(s_dt2[0:32, :], s_dt2[0:32, :], AF.Exp)

        @E("ac", reads=("rdt2", "c_ct"), writes=("rdt2",))
        def _(e):
            return e.activation(s_dt2[0:32, :], s_dt2[0:32, :], AF.Ln, bias=C1(32))

        @E("dv", reads=("rdt",), writes=("rdt",))
        def _(e):
            return e.tensor_scalar(s_dt[0:32, :], s_dt[0:32, :], 0.0, None, ALU.max)

        @E("dv", reads=("rdt", "rdt2"), writes=("rdt",))
        def _(e):
            return e.tensor_tensor(s_dt[0:32, :], s_dt[0:32, :], s_dt2[0:32, :], ALU.add)

        @E("dv", reads=("rdt", "A"), writes=("rdt2",))
        def _(e):
            return e.tensor_scalar(s_dt2[0:32, :], s_dt[0:32, :], s_A[0:32, :], None, ALU.mult)

        @E("dv", reads=("rdt2", "s", "c_ct"), writes=("s",))
        def _(e, c0=c0):
            return e.tensor_tensor_scan(
                s_sbuf[0:32, 1:1 + TC],
                s_consts[0:32, 5:6].broadcast_to([32, TC]),
                s_dt2[0:32, :], 0.0, ALU.mult, ALU.add)

        @E("dv", reads=("s",), writes=("s",))
        def _(e, c0=c0):
            return e.tensor_scalar(
                s_sbuf[0:32, 1:1 + TC], s_sbuf[0:32, 1:1 + TC],
                s_sbuf[0:32, 0:1], None, ALU.add)

        for sb_ in range(NSUB):
            @E("dv", reads=("s",), writes=("rdcr",))
            def _(e, c0=c0, sb_=sb_):
                return e.tensor_scalar(
                    s_dcr[0:32, sb_ * 128:(sb_ + 1) * 128],
                    s_sbuf[0:32, 1 + sb_ * 128:1 + (sb_ + 1) * 128],
                    s_sbuf[0:32, sb_ * 128:sb_ * 128 + 1], None, ALU.subtract)

        @E("ac", reads=("rdcr",), writes=("rdcr",))
        def _(e):
            return e.activation(s_dcr[0:32, :], s_dcr[0:32, :], AF.Exp)

        for sb_ in range(NSUB):
            @E("ac", reads=("s",), writes=("trA",))
            def _(e, sb_=sb_):
                return e.copy(s_trA[0:32, :], s_sbuf[0:32, 1 + sb_ * 128:1 + (sb_ + 1) * 128])

            @E("dv", reads=("s", "trA"), writes=("trB",))
            def _(e, sb_=sb_):
                return e.tensor_tensor(s_trB[0:32, :],
                                       s_sbuf[0:32, 1 + sb_ * 128:1 + (sb_ + 1) * 128],
                                       s_trA[0:32, :], ALU.subtract)

            if debug and ch == 0 and sb_ == 3:
                @E("sy", reads=("trA",), writes=("tradw",), dma="d_trad")
                def _(e):
                    return e.dma_start(trad[:].bitcast(BF16)[:, 0:128], s_trA[:])

            @E("pe", reads=("trA", "c_id"), writes=("bkf0",))
            def _(e):
                return e.transpose(pbf[0][:, 0:128], s_trA[:], s_ident[:])

            @E("dv", reads=("bkf0",), writes=("sT",))
            def _(e, ch=ch, sb_=sb_):
                return e.tensor_copy(s_sT[:, (ch * 4 + sb_) * 32:(ch * 4 + sb_ + 1) * 32],
                                     pbf[0][:, 0:32])

            if debug and ch == 0 and sb_ == 3:
                @E("dv", reads=("bkf0",), writes=("psdw0",))
                def _(e):
                    return e.tensor_copy(s_mtx[0][:], pbf[0][:, 0:128])

                @E("sy", reads=("psdw0",), writes=("psdw",), dma="d_psd")
                def _(e):
                    return e.dma_start(psd[:], s_mtx[0][:])

                @E("sy", reads=("c_id",), writes=("iddw",), dma="d_idd")
                def _(e):
                    return e.dma_start(idd[:].bitcast(BF16)[:, 0:128], s_ident[:])

                @E("sy", reads=("c_um",), writes=("umdw",), dma="d_umd")
                def _(e):
                    return e.dma_start(umd[:].bitcast(BF16)[:, 0:128], s_umask[:])

            @E("pe", reads=("trB", "c_id"), writes=("bkf0",))
            def _(e):
                return e.transpose(pbf[0][:, 0:128], s_trB[:], s_ident[:])

            @E("dv", reads=("bkf0", "sT"), writes=("sT",))
            def _(e, ch=ch, sb_=sb_):
                return e.tensor_tensor(s_sT[:, (ch * 4 + sb_) * 32:(ch * 4 + sb_ + 1) * 32],
                                       s_sT[:, (ch * 4 + sb_) * 32:(ch * 4 + sb_ + 1) * 32],
                                       pbf[0][:, 0:32], ALU.add)

            @E("ac", reads=("rdt",), writes=("trA",))
            def _(e, sb_=sb_):
                return e.copy(s_trA[0:32, :], s_dt[0:32, sb_ * 128:(sb_ + 1) * 128])

            @E("pe", reads=("trA", "c_id"), writes=("bkf0",))
            def _(e):
                return e.transpose(pbf[0][:, 0:128], s_trA[:], s_ident[:])

            @E("dv", reads=("bkf0",), writes=("dtT",))
            def _(e, sb_=sb_):
                return e.tensor_copy(s_dtT[:, sb_ * 32:(sb_ + 1) * 32], pbf[0][:, 0:32])

            @E("ac", reads=("rdcr",), writes=("trB",))
            def _(e, sb_=sb_):
                return e.copy(s_trB[0:32, :], s_dcr[0:32, sb_ * 128:(sb_ + 1) * 128])

            @E("pe", reads=("trB", "c_id"), writes=("bkf0",))
            def _(e):
                return e.transpose(pbf[0][:, 0:128], s_trB[:], s_ident[:])

            @E("dv", reads=("bkf0",), writes=("dCT",))
            def _(e, sb_=sb_):
                return e.tensor_copy(s_dCT[:, sb_ * 32:(sb_ + 1) * 32], pbf[0][:, 0:32])

        @E("sy", reads=("s",), writes=("sdram",), dma="d_sb")
        def _(e, c0=c0):
            return e.dma_start(sdram[:, 0:513], s_sbuf[0:32, 0:513])

        if debug and ch == 0:
            @E("sy", reads=("rdt",), writes=("dtd",), dma="d_dbg2")
            def _(e):
                return e.dma_start(dtd[:], s_dt[0:32, :])

            @E("sy", reads=("xc",), writes=("xcd",), dma="d_dbg4")
            def _(e):
                return e.dma_start(xcd[:], A5[:])

        @E("dv", reads=("s",), writes=("s",))
        def _(e):
            return e.tensor_copy(s_sbuf[0:32, 0:1], s_sbuf[0:32, 512:513])

        for sb_ in range(NSUB):
            toff = sb_ * 128
            gcol = (ch * 4 + sb_) * 32

            @E("sy", reads=("sdram",), writes=("sflat",), dma="d_sb2")
            def _(e, sb_=sb_):
                return e.dma_start(
                    s_sflat[0:1, 0:4128].rearrange("one (h t) -> one h t", t=129),
                    sdram[:, sb_ * 128:sb_ * 128 + 129].unsqueeze(0))

            @E("pe", reads=("sflat", "c_on", "c_no"), writes=("bk3",))
            def _(e):
                sf = s_sflat[0:1, :].rearrange("one (h t) -> one h t", t=129)
                e.matmul(pb[3][:, 0:32], s_ones[0:1, :],
                         sf[0:1, :, 128:129].rearrange("one h t -> one (h t)"),
                         start=True, stop=False)
                return e.matmul(pb[3][:, 0:32], s_nones[0:1, :],
                                sf[0:1, :, 0:1].rearrange("one h t -> one (h t)"),
                                start=False, stop=True)

            @E("ac", reads=("bk3",), writes=("drep",))
            def _(e, sb_=sb_):
                return e.activation(s_drep[:, sb_ * 32:(sb_ + 1) * 32], pb[3][:, 0:32], AF.Exp)

            if debug and ch == 0 and sb_ == 0:
                @E("sy", reads=("sflat",), writes=("sfd",), dma="d_dbg3")
                def _(e):
                    return e.dma_start(sfd[:], s_sflat[0:1, 0:4128])
            if debug and ch == 0 and sb_ == 3:
                @E("sy", reads=("sflat",), writes=("sfd2w",), dma="d_dbg6")
                def _(e):
                    return e.dma_start(sfd2[:], s_sflat[0:1, 0:4128])
            for grp in range(4):
                yb = 4 + (grp % 2)
                for hh in range(8):
                    h = grp * 8 + hh
                    m = h % 2
                    rb = m
                    gb = 2 + m

                    @E("pe", reads=("sflat", "c_on"), writes=(f"bk{rb}",))
                    def _(e, h=h, sb_=sb_, rb=rb):
                        sf = s_sflat[0:1, :].rearrange("one (h t) -> one h t", t=129)
                        return e.matmul(pb[rb][:, 0:128], s_ones[0:1, :],
                                        sf[0:1, h:h + 1, 1:129]
                                        .rearrange("one h t -> one (h t)"),
                                        start=True, stop=True)

                    @E("dv", reads=(f"bk{rb}", "sT"), writes=(f"mtx{m}",))
                    def _(e, h=h, gcol=gcol, m=m, rb=rb):
                        return e.tensor_scalar(s_mtx[m][:], pb[rb][:, 0:128],
                                               s_sT[:, gcol + h:gcol + h + 1],
                                               0.0, ALU.subtract, ALU.min)

                    if debug and ch == 0 and h == 31 and sb_ == 3:
                        @E("sy", reads=(f"mtx{m}",), writes=("argdw",), dma="d_arg")
                        def _(e, m=m):
                            return e.dma_start(argd[:], s_mtx[m][:])

                    @E("ac", reads=(f"mtx{m}",), writes=(f"mtx{m}",))
                    def _(e, m=m):
                        return e.activation(s_mtx[m][:], s_mtx[m][:], AF.Exp)

                    @E("dv", reads=(f"mtx{m}", "c_um"), writes=(f"mtm{m}",))
                    def _(e, m=m):
                        return e.tensor_tensor(s_mtm[m][:], s_mtx[m][:], s_umask[:], ALU.mult)

                    @E("pe", reads=("bc", "cc"), writes=(f"bk{gb}",))
                    def _(e, h=h, toff=toff, gb=gb):
                        return e.matmul(pb[gb][:, 0:128],
                                        A3[:, h * TC + toff:h * TC + toff + 128],
                                        A4[:, h * TC + toff:h * TC + toff + 128],
                                        start=True, stop=True)

                    @E("dv", reads=(f"bk{gb}", f"mtm{m}"), writes=(f"gm{m}",))
                    def _(e, m=m, gb=gb):
                        return e.tensor_tensor(s_gm[m][:], pb[gb][:, 0:128], s_mtm[m][:], ALU.mult)

                    @E("pe", reads=("bc", "c_id"), writes=("bkf0",))
                    def _(e, h=h, toff=toff):
                        return e.transpose(pbf[0][:, 0:128],
                                           A3[:, h * TC + toff:h * TC + toff + 128], s_ident[:])

                    @E("dv", reads=("bkf0", f"mtx{m}"), writes=(f"b2t{m}",))
                    def _(e, m=m):
                        return e.tensor_scalar(s_b2t[m][:], pbf[0][:, 0:128],
                                               s_mtx[m][:, 127:128], None, ALU.mult)

                    if m == 0:
                        @E("pe", reads=("xc", "c_id"), writes=("bkf1",))
                        def _(e, h=h, toff=toff):
                            return e.transpose(pbf[1][:, 0:128],
                                               A5[:, (h // 2) * TC + toff:(h // 2) * TC + toff + 128],
                                               s_ident[:])

                    @E("dv", reads=("bkf1", "dtT"), writes=(f"dxt{m}",))
                    def _(e, h=h, sb_=sb_, m=m):
                        return e.tensor_scalar(s_dxt[m][:, 0:64],
                                               pbf[1][:, (h % 2) * 64:(h % 2) * 64 + 64],
                                               s_dtT[:, sb_ * 32 + h:sb_ * 32 + h + 1], None, ALU.mult)

                    @E("pe", reads=(f"gm{m}", f"dxt{m}"), writes=(f"bk{yb}",))
                    def _(e, m=m, yb=yb, hh=hh):
                        return e.matmul(pb[yb][:, hh * 64:(hh + 1) * 64],
                                        s_gm[m][:], s_dxt[m][:, 0:64], start=True, stop=True)

                    @E("pe", reads=("cc", "hb"), writes=(f"bk{rb}",))
                    def _(e, h=h, toff=toff, rb=rb):
                        return e.matmul(pb[rb][:, 128:192],
                                        A4[:, h * TC + toff:h * TC + toff + 128],
                                        s_hb[:, h * 64:(h + 1) * 64], start=True, stop=True)

                    @E("dv", reads=(f"bk{rb}", "dCT"), writes=("yis",))
                    def _(e, h=h, sb_=sb_, hh=hh, rb=rb):
                        return e.tensor_scalar(s_yis[:, hh * 64:(hh + 1) * 64], pb[rb][:, 128:192],
                                               s_dCT[:, sb_ * 32 + h:sb_ * 32 + h + 1], None, ALU.mult)

                    @E("pe", reads=(f"b2t{m}", f"dxt{m}"), writes=(f"bk{gb}",))
                    def _(e, m=m, gb=gb):
                        return e.matmul(pb[gb][:, 128:192], s_b2t[m][:], s_dxt[m][:, 0:64],
                                        start=True, stop=True)

                    @E("dv", reads=(f"bk{gb}", "drep", "h"), writes=("h",))
                    def _(e, h=h, sb_=sb_, gb=gb):
                        return e.scalar_tensor_tensor(
                            s_h[:, h * 64:(h + 1) * 64], s_h[:, h * 64:(h + 1) * 64],
                            s_drep[:, sb_ * 32 + h:sb_ * 32 + h + 1], pb[gb][:, 128:192],
                            ALU.mult, ALU.add)

                @E("dv", reads=(f"bk{yb}", "yis"), writes=(f"yp{sb_ % 2}",))
                def _(e, grp=grp, yb=yb, sb_=sb_):
                    return e.tensor_tensor(A7[sb_ % 2][:, grp * 512:(grp + 1) * 512],
                                           pb[yb][:, 0:512], s_yis[:], ALU.add)

            @E("ac", reads=("h",), writes=("hb",))
            def _(e):
                return e.copy(s_hb[:], s_h[:])

            @E("sy", reads=(f"yp{sb_ % 2}",), writes=(f"ypdw{sb_ % 2}",), dma=f"d_yp{sb_ % 2}")
            def _(e, c0=c0, sb_=sb_):
                return e.dma_start(ypd[c0 + sb_ * 128:c0 + (sb_ + 1) * 128, :], A7[sb_ % 2][:])

        if debug and ch == 0:
            for nmsb, dst, srcap in [("drep", drepd, s_drep[:]), ("dCT", dctd, s_dCT[:]),
                                     ("dtT", dttd, s_dtT[:]), ("sT", sTd, s_sT[:, 0:128]),
                                     ("mtx1", mtxd, s_mtx[1][:]), ("gm1", gmd, s_gm[1][:]),
                                     ("dxt1", dxtd, s_dxt[1][:]), ("b2t1", b2td, s_b2t[1][:])]:
                @E("sy", reads=(nmsb,), writes=(f"dbg_{nmsb}",), dma="d_dbg5")
                def _(e, dst=dst, srcap=srcap):
                    return e.dma_start(dst[:] if False else dst[0:srcap.shape[0], 0:srcap.shape[1]], srcap)

        for sb_ in range(NSUB):
            @E("sy", reads=("cc",), writes=("ccdw",), dma="d_ccw")
            def _(e, ch=ch, sb_=sb_):
                rb_ = (ch * 4 + sb_) * 128
                return e.dma_start(
                    ccd[rb_:rb_ + 128, :].rearrange("p (h t) -> p h t", t=128),
                    A4[:].rearrange("p (h tt) -> p h tt", tt=TC)[:, :, sb_ * 128:(sb_ + 1) * 128])


    WOUTC = cast_stream("wout", wout_t, woutb, 16 * 128, 2)
    WGC = cast_stream("wg", wg_t, wgb, 64 * 128, 4)
    WUC = cast_stream("wu", wu_t, wub, 64 * 128, 4)
    WDC = cast_stream("wd", wd_t, wdb, 16 * 128, 4)

    barrier()
    if debug:
        @E("sy", reads=("h",), writes=("hdbgw",), dma="d_hdbg")
        def _(e):
            return e.dma_start(hdbg[:], s_h[:])

    @E("gp", reads=("hb", "c_xi"), writes=("hxw",), dma="d_hxw")
    def _(e):
        return e.indirect_dma_start(
            hx[0:260, :], bass.IndirectOffsetOnAxis(s_xidx[:, 0:1], 0),
            s_hb[:], None)

    @E("gp", reads=("hxw", "c_fi", "c_nn"), writes=("hxfw",), dma="d_hxf")
    def _(e):
        return e.indirect_dma_start(
            hxf.rearrange("r (c one) -> (r c) one", one=1)[0:8, 0:1],
            bass.IndirectOffsetOnAxis(s_fidx[:, 0:1], 0),
            s_nonce[:], None)

    @E("gp", reads=("hxfw", "c_nn"), writes=("hin",), dma="d_hxr")
    def _(e):
        with e.register("spin") as spin, e.register("rf") as rf, e.register("rn") as rn:
            e.reg_load(rn, s_nonce[0:1, 0:1])
            e.reg_mov(spin, 1)
            with ExitStack() as nest:
                for it in range(N_POLL):
                    nest.enter_context(e.If(spin))
                    e.dma_start(s_flag[:], hxf[0:1, 0:1]).then_inc(sp_spin, 16)
                    e.wait_ge(sp_spin, 16 * (it + 1))
                    e.reg_load(rf, s_flag[0:1, 0:1])
                    e.reg_sub(spin, rf, rn)
            with e.While(spin):
                e.nop()
        return e.dma_start(s_hin[:], hx[0:128, :])

    @E("dv", reads=("hin", "c_hm"), writes=("hin",))
    def _(e):
        return e.tensor_scalar(s_hin[:], s_hin[:], s_hmask[:, 0:1], None, ALU.mult)

    if debug:
        @E("gp", reads=("hin",), writes=("flgdw",), dma="d_flgd")
        def _(e):
            e.dma_start(flgd[0:1, 0:2], hxf[0:1, 0:2]).then_inc(sems["d_flgd"][0], 16)
            return e.dma_start(flgd[0:1, 2:3], s_nonce[0:1, 0:1])

    MO = A1
    X2 = A2
    YG = A5
    FU = lambda k: (A3[:, k * TC:(k + 1) * TC] if k < 32 else A4[:, (k - 32) * TC:(k - 31) * TC])
    YF = A7[0]
    YQ = A7[1]
    WD2 = lambda i: A8[i][:]
    WO2 = [SCR[:, 0:1024].bitcast(BF16), s_hb[:, 0:2048]]
    WG2 = [SCR[:, 1024:2048].bitcast(BF16), s_h[:, 0:1024].bitcast(BF16)]
    WU2 = [SCR[:, 2048:3072].bitcast(BF16), s_h[:, 1024:2048].bitcast(BF16)]
    CC2 = A9
    ZS2 = DT3[:, 0:1024].bitcast(BF16)

    for ch in range(nchunk):
        c0 = ch * TC

        @E("ac", reads=("sT",), writes=("expsT",))
        def _(e, ch=ch):
            return e.activation(s_expsT[:], s_sT[:, ch * 128:(ch + 1) * 128], AF.Exp)

        for sb_ in range(NSUB):
            gcol = sb_ * 32

            @E("gp", writes=("yq",), dma="d_yp2")
            def _(e, c0=c0, sb_=sb_):
                return e.dma_start(YQ[:], ypd[c0 + sb_ * 128:c0 + (sb_ + 1) * 128, :])

            @E("gp", writes=("cc2",), dma="d_cc2")
            def _(e, ch=ch, sb_=sb_):
                rb_ = (ch * 4 + sb_) * 128
                return e.dma_start(CC2[:], ccd[rb_:rb_ + 128, :])

            @E("gp", writes=("zs2",), dma="d_zs2")
            def _(e, ch=ch, sb_=sb_):
                rb_ = (ch * 4 + sb_) * 128
                return e.dma_start(ZS2[:, :], zsd[rb_:rb_ + 128, :])

            for h in range(32):
                m = h % 2

                @E("pe", reads=("cc2", "hin"), writes=(f"bk{m}",))
                def _(e, h=h, m=m):
                    return e.matmul(pb[m][:, 0:64], CC2[:, h * 128:(h + 1) * 128],
                                    s_hin[:, h * 64:(h + 1) * 64], start=True, stop=True)

                @E("dv", reads=(f"bk{m}", "expsT", "yq"), writes=("yf",))
                def _(e, h=h, m=m, gcol=gcol):
                    return e.scalar_tensor_tensor(
                        YF[:, h * 64:(h + 1) * 64], pb[m][:, 0:64],
                        s_expsT[:, gcol + h:gcol + h + 1],
                        YQ[:, h * 64:(h + 1) * 64], ALU.mult, ALU.add)

            @E("ac", reads=("yf",), writes=("yq",))
            def _(e):
                return e.square(YQ[:], YF[:])

            @E("dv", reads=("yq",), writes=("rms",))
            def _(e):
                return e.reduce_sum(s_rms[:, :], YQ[:].rearrange("p (h q) -> p h q", q=64), axis=AX.X)

            @E("ac", reads=("rms", "c_ct"), writes=("rms",))
            def _(e):
                return e.activation(s_rms[:, :], s_rms[:, :], AF.Ln, scale=C64(128), bias=CR(128))

            @E("ac", reads=("rms", "c_ct"), writes=("rms",))
            def _(e):
                return e.activation(s_rms[:, :], s_rms[:, :], AF.Exp, scale=CH_(128))

            @E("dv", reads=("rms", "yf"), writes=("yf",))
            def _(e):
                return e.tensor_tensor(
                    YF[:].rearrange("p (h q) -> p h q", q=64),
                    YF[:].rearrange("p (h q) -> p h q", q=64),
                    s_rms[:, :].unsqueeze(2).broadcast_to([128, 32, 64]), ALU.mult)

            for pr in range(16):
                fb = pr % 2

                @E("pe", reads=("yf", "c_id"), writes=(f"bkf{fb}",))
                def _(e, pr=pr, fb=fb):
                    return e.transpose(pbf[fb][:, 0:128], YF[:, pr * 128:(pr + 1) * 128], s_ident[:])

                @E("dv", reads=(f"bkf{fb}", "zs2"), writes=("yg",))
                def _(e, pr=pr, fb=fb, sb_=sb_):
                    return e.tensor_tensor(YG[:, pr * TC + sb_ * 128:pr * TC + (sb_ + 1) * 128],
                                           pbf[fb][:, 0:128],
                                           ZS2[:, pr * 128:(pr + 1) * 128], ALU.mult)

        for di in range(16):
            ws = di % 2

            @E("sy", reads=WOUTC + ("hxw",), writes=(f"wo{ws}",), dma=f"d_wo{ws}")
            def _(e, di=di, ws=ws):
                return e.dma_start(WO2[ws][:, :], woutb[di * 128:(di + 1) * 128, :])

            bank = di % 3
            for k in range(16):
                @E("pe", reads=(f"wo{ws}", "yg"), writes=(f"bk{bank}",))
                def _(e, di=di, k=k, bank=bank, ws=ws):
                    return e.matmul(pb[bank][:, 0:512], WO2[ws][:, k * 128:(k + 1) * 128],
                                    YG[:, k * TC:(k + 1) * TC], start=(k == 0), stop=(k == 15))

            @E("ac", reads=(f"bk{bank}",), writes=("mo",))
            def _(e, di=di, bank=bank):
                return e.copy(MO[:, di * TC:(di + 1) * TC], pb[bank][:, 0:512])

        ln_stats_rows("mo", lambda k: MO[:, k * TC:(k + 1) * TC], 16)
        for di in range(16):
            xs = di % 2

            @E("sy", writes=(f"xf{xs}",), dma=f"d_xf{xs}")
            def _(e, c0=c0, di=di, xs=xs):
                return e.dma_start(s_xf[xs][:], xT[di * 128:(di + 1) * 128, c0:c0 + TC])

            @E("dv", reads=("mo", "bk4"), writes=("x2",))
            def _(e, di=di):
                return e.tensor_tensor(X2[:, di * TC:(di + 1) * TC], MO[:, di * TC:(di + 1) * TC],
                                       pb[4][:, 0:512], ALU.mult)

            @E("dv", reads=("bk5",), writes=("x2",))
            def _(e, di=di):
                return e.tensor_tensor(X2[:, di * TC:(di + 1) * TC], X2[:, di * TC:(di + 1) * TC],
                                       pb[5][:, 0:512], ALU.add)

            @E("dv", reads=(f"xf{xs}",), writes=("x2",))
            def _(e, di=di, xs=xs):
                return e.tensor_tensor(X2[:, di * TC:(di + 1) * TC], X2[:, di * TC:(di + 1) * TC],
                                       s_xf[xs][:], ALU.add)

        ln_stats_rows("x2", lambda k: X2[:, k * TC:(k + 1) * TC], 16)
        for k in range(16):
            @E("dv", reads=("x2", "bk4"), writes=("mo",))
            def _(e, k=k):
                return e.tensor_tensor(MO[:, k * TC:(k + 1) * TC], X2[:, k * TC:(k + 1) * TC],
                                       pb[4][:, 0:512], ALU.mult)

            @E("dv", reads=("bk5",), writes=("mo",))
            def _(e, k=k):
                return e.tensor_tensor(MO[:, k * TC:(k + 1) * TC], MO[:, k * TC:(k + 1) * TC],
                                       pb[5][:, 0:512], ALU.add)

        for ii in range(64):
            gb = ii % 2
            ub = 2 + (ii % 2)

            ws = ii % 2

            @E("sy", reads=WGC + ("hxw",), writes=(f"wg2{ws}",), dma=f"d_wg2{ws}")
            def _(e, ii=ii, ws=ws):
                return e.dma_start(WG2[ws][:, :], wgb[ii * 128:(ii + 1) * 128, :])

            @E("sy", reads=WUC + ("hxw",), writes=(f"wu2{ws}",), dma=f"d_wu2{ws}")
            def _(e, ii=ii, ws=ws):
                return e.dma_start(WU2[ws][:, :], wub[ii * 128:(ii + 1) * 128, :])

            for k in range(16):
                @E("pe", reads=(f"wg2{ws}", "mo"), writes=(f"bk{gb}",))
                def _(e, k=k, gb=gb, ws=ws):
                    return e.matmul(pb[gb][:, 0:512], WG2[ws][:, k * 128:(k + 1) * 128],
                                    MO[:, k * TC:(k + 1) * TC], start=(k == 0), stop=(k == 15))
            for k in range(16):
                @E("pe", reads=(f"wu2{ws}", "mo"), writes=(f"bk{ub}",))
                def _(e, k=k, ub=ub, ws=ws):
                    return e.matmul(pb[ub][:, 0:512], WU2[ws][:, k * 128:(k + 1) * 128],
                                    MO[:, k * TC:(k + 1) * TC], start=(k == 0), stop=(k == 15))

            @E("ac", reads=(f"bk{gb}", "c_sg"), writes=("gt",))
            def _(e, ii=ii, gb=gb):
                return e.activation(s_gt[:], pb[gb][:, 0:512], AF.Silu, scale=s_sg[:, ii:ii + 1])

            @E("dv", reads=(f"bk{ub}", "gt", "c_su"), writes=("fu",))
            def _(e, ii=ii, ub=ub):
                return e.scalar_tensor_tensor(FU(ii), pb[ub][:, 0:512], s_su[:, ii:ii + 1],
                                              s_gt[:], ALU.mult, ALU.mult)

        for di in range(16):
            dbk = 4 + (di % 2)
            for q in range(4):
                @E("sy", reads=WDC, writes=(f"wdl{q % 2}",), dma=f"d_wd{q % 2}")
                def _(e, di=di, q=q):
                    return e.dma_start(WD2(q % 2),
                                       wdb[di * 128:(di + 1) * 128, q * 2048:(q + 1) * 2048])
                for k in range(q * 16, (q + 1) * 16):
                    @E("pe", reads=(f"wdl{q % 2}", "fu"), writes=(f"bk{dbk}",))
                    def _(e, k=k, q=q, dbk=dbk):
                        return e.matmul(pb[dbk][:, 0:512],
                                        WD2(q % 2)[:, (k % 16) * 128:(k % 16 + 1) * 128],
                                        FU(k), start=(k == 0), stop=(k == 63))

            @E("ac", reads=(f"bk{dbk}", "c_sd", "yg"), writes=("yg",))
            def _(e, di=di, dbk=dbk):
                return e.mul(YG[:, di * TC:(di + 1) * TC], pb[dbk][:, 0:512], s_sd[:, di:di + 1])

        ln_stats_rows("yg", lambda k: YG[:, k * TC:(k + 1) * TC], 16)
        for di in range(16):
            os_ = di % 2

            @E("dv", reads=("yg", "bk4"), writes=(f"ot{os_}",))
            def _(e, di=di, os_=os_):
                return e.tensor_tensor(s_otile[os_][:], YG[:, di * TC:(di + 1) * TC],
                                       pb[4][:, 0:512], ALU.mult)

            @E("dv", reads=("bk5",), writes=(f"ot{os_}",))
            def _(e, di=di, os_=os_):
                return e.tensor_tensor(s_otile[os_][:], s_otile[os_][:],
                                       pb[5][:, 0:512], ALU.add)

            @E("dv", reads=("x2",), writes=(f"ot{os_}",))
            def _(e, di=di, os_=os_):
                return e.tensor_tensor(s_otile[os_][:], s_otile[os_][:],
                                       X2[:, di * TC:(di + 1) * TC], ALU.add)

            @E("sy", reads=(f"ot{os_}",), writes=(f"otd{os_}",), dma=f"d_ot{os_}")
            def _(e, c0=c0, di=di, os_=os_):
                return e.dma_start(out_d[di * 128:(di + 1) * 128, c0:c0 + TC], s_otile[os_][:])

    if trunc is not None:
        cut = {}
        for eng in ops:
            for fn, dd, tok, sq in ops[eng]:
                if sq <= trunc and tok is not None:
                    cut[tok[0]] = max(cut.get(tok[0], 0), tok[1])
        FINAL = list(cut.items())
    else:
        FINAL = all_tokens()
    ops["gp"].append(((lambda e: None), tuple(FINAL), None, 0))

    with nc.Block() as blk:
        blk.tensor(lambda e: replay(e, "pe"))
        blk.vector(lambda e: replay(e, "dv"))
        blk.scalar(lambda e: replay(e, "ac"))
        blk.sync(lambda e: replay(e, "sy"))
        blk.gpsimd(lambda e: replay(e, "gp"))

    stk.close()
    return nc


_NC_CACHE = {}


def _host_inputs(inputs, nchunk=NCHUNK):
    th = nchunk * TC
    f32 = np.float32
    Win = np.asarray(inputs["L0_attn_Win"], f32)
    Wout = np.asarray(inputs["L0_attn_Wout"], f32)
    wg = np.asarray(inputs["L0_ffn_w_gate"])
    wu = np.asarray(inputs["L0_ffn_w_up"])
    wd = np.asarray(inputs["L0_ffn_w_down"])
    cwx = np.asarray(inputs["L0_attn_conv_w_x"], f32)
    cwb = np.asarray(inputs["L0_attn_conv_w_b"], f32)
    cwc = np.asarray(inputs["L0_attn_conv_w_c"], f32)
    x = np.asarray(inputs["L0_attn_x"], f32)

    WinP = np.zeros((2048, NCI * 128), f32)
    WinP[:, :PD] = Win.T
    win_t = np.ascontiguousarray(
        WinP.reshape(16, 128, NCI, 128).transpose(2, 1, 0, 3)).reshape(NCI * 128, 2048)
    wout_t = np.ascontiguousarray(
        Wout.T.reshape(16, 128, 16, 128).transpose(2, 1, 0, 3)).reshape(16 * 128, 2048)
    wg_t = np.ascontiguousarray(
        wg.reshape(16, 128, 64, 128).transpose(2, 1, 0, 3)).reshape(64 * 128, 2048)
    wu_t = np.ascontiguousarray(
        wu.reshape(16, 128, 64, 128).transpose(2, 1, 0, 3)).reshape(64 * 128, 2048)
    wd_t = np.ascontiguousarray(
        wd.reshape(64, 128, 16, 128).transpose(2, 1, 0, 3)).reshape(16 * 128, 8192)

    wvec = np.concatenate([cwx.reshape(2048, 4), cwb.reshape(4096, 4), cwc.reshape(4096, 4)], 0)
    conv_t = np.zeros((80, 128, 4, 128), f32)
    rr = np.arange(128)
    for ti in range(80):
        for k in range(4):
            conv_t[ti, rr, k, rr] = wvec[ti * 128 + rr, k]
    conv_t = conv_t.reshape(80 * 128, 512)

    consts = np.zeros((128, 8), f32)
    consts[:, 0] = 1e-5
    consts[:, 1] = 1e-6
    consts[:, 2] = -0.5
    consts[:, 3] = 1.0 / 2048.0
    consts[:, 4] = 1.0 / 64.0
    consts[:, 5] = 1.0

    com = {
        "win_t": win_t, "wout_t": wout_t, "wg_t": wg_t, "wu_t": wu_t, "wd_t": wd_t,
        "conv_t": conv_t,
        "alog": np.asarray(inputs["L0_attn_A_log"], f32).reshape(32, 1),
        "dtb": np.asarray(inputs["L0_attn_dt_bias"], f32).reshape(32, 1),
        "sg_i": np.ascontiguousarray(np.asarray(inputs["L0_ffn_s_gate"], f32).reshape(64, 128).T),
        "su_i": np.ascontiguousarray(np.asarray(inputs["L0_ffn_s_up"], f32).reshape(64, 128).T),
        "sd_i": np.ascontiguousarray(np.asarray(inputs["L0_ffn_s_down"], f32).reshape(16, 128).T),
        "ident_bf": np.asarray(np.eye(128, dtype=ml_dtypes.bfloat16)),
        "ident_f": np.eye(128, dtype=f32),
        "umask_i": np.asarray(np.triu(np.ones((128, 128), ml_dtypes.bfloat16))),
        "onesr_i": np.ones((1, 128), f32),
        "onec_i": np.ones((128, 1), f32),
        "onecb_i": np.ones((128, 1), ml_dtypes.bfloat16),
        "negonesr_i": -np.ones((1, 128), f32),
        "consts_i": consts,
    }
    nonce = (np.frombuffer(np.random.bytes(4), dtype=np.uint32)[0] & np.uint32(0x7FFFFFFF)) | np.uint32(1)
    in_maps = []
    for c in range(NCORES):
        b, hf = c // 2, c % 2
        sl = x[b, hf * th:(hf + 1) * th, :]
        xTc = np.ascontiguousarray(sl.T)
        x3 = np.zeros((D, 4), f32)
        if hf == 1:
            x3[:, 0:3] = x[b, th - 3:th, :].T
        even = (hf == 0)
        m = {"xT": xTc, "xh3": x3,
             "xidx_i": (np.arange(128, dtype=np.int32) if even
                        else np.arange(128, dtype=np.int32) + 130).reshape(128, 1),
             "fidx_i": (np.array([[0], [1]], np.int32) if even
                        else np.array([[4], [5]], np.int32)),
             "nonce_i": np.full((2, 1), nonce, np.uint32),
             "hmask_i": np.full((128, 1), 0.0 if even else 1.0, f32)}
        m.update(com)
        in_maps.append(m)
    return in_maps


def kernel(**inputs):
    debug = bool(inputs.pop("_debug", False))
    key = "dbg" if debug else "std"
    if key not in _NC_CACHE:
        _NC_CACHE[key] = build_program(debug=debug)
    nc = _NC_CACHE[key]
    in_maps = _host_inputs(inputs)
    res = run_bass_kernel_spmd(nc, in_maps, core_ids=list(range(NCORES)))
    out = np.empty((B, T, D), np.float32)
    for c in range(NCORES):
        b, hf = c // 2, c % 2
        out[b, hf * THALF:(hf + 1) * THALF, :] = res.results[c]["out_d"].T
    if debug:
        return out, res
    return out


# version 07 (final)
# speedup vs baseline: 1.0659x; 1.0659x over previous
import numpy as np
import ml_dtypes
import concourse.bass as bass
import concourse.mybir as mybir
from concourse.bass_utils import run_bass_kernel_spmd
from contextlib import ExitStack

F32 = mybir.dt.float32
BF16 = mybir.dt.bfloat16
I8 = mybir.dt.int8
I32 = mybir.dt.int32
U32 = mybir.dt.uint32
ALU = mybir.AluOpType
AF = mybir.ActivationFunctionType
AX = mybir.AxisListType

B, T, D = 4, 8192, 2048
PD, NCI = 12320, 97
TC, NCHUNK, THALF = 512, 8, 4096
SPIN_MODE = "while"
N_POLL = 1536
NCORES = 8
NSUB = 4
KCV = 4
CI_Z0, CI_B0, CI_C0, CI_DT = 16, 32, 64, 96


def build_program(debug=False, nchunk=NCHUNK, trunc=None):
    th = nchunk * TC
    nc = bass.Bass("TRN2", target_bir_lowering=False, debug=False,
                   num_devices=NCORES, detect_race_conditions=False)

    dri = lambda n, sh, dt: nc.dram_tensor(n, sh, dt, kind="ExternalInput").ap()
    xT = dri("xT", [D, th], F32)
    xh3 = dri("xh3", [D, 4], F32)
    win_t = dri("win_t", [NCI * 128, 2048], F32)
    wout_t = dri("wout_t", [16 * 128, 2048], F32)
    wg_t = dri("wg_t", [64 * 128, 2048], I8)
    wu_t = dri("wu_t", [64 * 128, 2048], I8)
    wd_t = dri("wd_t", [16 * 128, 8192], I8)
    conv_t = dri("conv_t", [80 * 128, 512], F32)
    alog = dri("alog", [32, 1], F32)
    dtb = dri("dtb", [32, 1], F32)
    sg_i = dri("sg_i", [128, 64], F32)
    su_i = dri("su_i", [128, 64], F32)
    sd_i = dri("sd_i", [128, 16], F32)
    ident_bf = dri("ident_bf", [128, 128], BF16)
    ident_f = dri("ident_f", [128, 128], F32)
    umask_i = dri("umask_i", [128, 128], BF16)
    onesr_i = dri("onesr_i", [1, 128], F32)
    onec_i = dri("onec_i", [128, 1], F32)
    onecb_i = dri("onecb_i", [128, 1], BF16)
    negonesr_i = dri("negonesr_i", [1, 128], F32)
    consts_i = dri("consts_i", [128, 8], F32)
    xidx_i = dri("xidx_i", [128, 1], I32)
    fidx_i = dri("fidx_i", [2, 1], I32)
    nonce_i = dri("nonce_i", [2, 1], U32)
    hmask_i = dri("hmask_i", [128, 1], F32)

    out_d = nc.dram_tensor("out_d", [D, th], F32, kind="ExternalOutput").ap()

    okind = "ExternalOutput" if debug else "Internal"
    drs = lambda n, sh, dt, kind="Internal": nc.dram_tensor(n, sh, dt, kind=kind).ap()
    winb = drs("winb", [NCI * 128, 2048], BF16)
    woutb = drs("woutb", [16 * 128, 2048], BF16)
    wgb = drs("wgb", [64 * 128, 2048], BF16)
    wub = drs("wub", [64 * 128, 2048], BF16)
    wdb = drs("wdb", [16 * 128, 8192], BF16)
    convb = drs("convb", [80 * 128, 512], BF16)
    ypd = drs("ypd", [th, 2048], BF16, okind)
    ccd = drs("ccd", [th, 4096], BF16, okind)
    zsd = drs("zsd", [th, 2048], BF16, okind)
    sdram = drs("sdram", [32, 520], F32)
    hdbg = drs("hdbg", [128, 2048], F32, okind) if debug else None
    xnd = drs("xnd", [128, 8192], BF16, okind) if debug else None
    flgd = drs("flgd", [1, 4], U32, okind) if debug else None
    drepd = drs("drepd", [128, 128], F32, okind) if debug else None
    dctd = drs("dctd", [128, 128], F32, okind) if debug else None
    dttd = drs("dttd", [128, 128], F32, okind) if debug else None
    sTd = drs("sTd", [128, 128], F32, okind) if debug else None
    mtxd = drs("mtxd", [128, 128], F32, okind) if debug else None
    gmd = drs("gmd", [128, 128], BF16, okind) if debug else None
    dxtd = drs("dxtd", [128, 64], BF16, okind) if debug else None
    b2td = drs("b2td", [128, 128], BF16, okind) if debug else None
    argd = drs("argd", [128, 128], F32, okind) if debug else None
    sfd2 = drs("sfd2", [1, 4128], F32, okind) if debug else None
    trad = drs("trad", [128, 128], F32, okind) if debug else None
    psd = drs("psd", [128, 128], F32, okind) if debug else None
    idd = drs("idd", [128, 128], F32, okind) if debug else None
    umd = drs("umd", [128, 128], F32, okind) if debug else None
    dtd = drs("dtd", [32, 512], F32, okind) if debug else None
    sfd = drs("sfd", [1, 4128], F32, okind) if debug else None
    xcd = drs("xcd", [128, 8192], BF16, okind) if debug else None
    hx = nc.dram_tensor("hx", [260, 2048], BF16, addr_space="Shared").ap()
    hxf = nc.dram_tensor("hxf", [4, 2], U32, addr_space="Shared").ap()

    stk = ExitStack()
    SB = lambda n, sh, dt: stk.enter_context(nc.sbuf_tensor(n, sh, dt))
    s_ident = SB("s_ident", [128, 128], BF16)
    s_identf = SB("s_identf", [128, 128], F32)
    s_umask = SB("s_umask", [128, 128], BF16)
    s_ones = SB("s_ones", [1, 128], F32)
    s_onec = SB("s_onec", [128, 1], F32)
    s_onecb = SB("s_onecb", [128, 1], BF16)
    s_nones = SB("s_nones", [1, 128], F32)
    s_consts = SB("s_consts", [128, 8], F32)
    s_alog = SB("s_alog", [32, 1], F32)
    s_A = SB("s_A", [32, 1], F32)
    s_dtbv = SB("s_dtbv", [32, 1], F32)
    s_sg = SB("s_sg", [128, 64], F32)
    s_su = SB("s_su", [128, 64], F32)
    s_sd = SB("s_sd", [128, 16], F32)
    s_sT = SB("s_sT", [128, 1024], F32)
    s_h = SB("s_h", [128, 2048], F32)
    s_hb = SB("s_hb", [128, 2048], BF16)
    s_sbuf = SB("s_sbuf", [32, 514], F32)
    SCR = SB("SCR", [128, 4160], F32)
    s_sflat = SCR[0:1, 0:4128]
    s_dtT = SB("s_dtT", [128, 128], F32)
    s_dCT = SB("s_dCT", [128, 128], F32)
    s_expsT = SB("s_expsT", [128, 128], F32)
    s_drep = SB("s_drep", [128, 128], F32)
    s_halo = SB("s_halo", [128, 240], BF16)
    s_xh3 = SB("s_xh3", [128, 64], BF16)
    s_x3f = SB("s_x3f", [128, 64], F32)
    s_xidx = SB("s_xidx", [128, 1], I32)
    s_fidx = SB("s_fidx", [2, 1], I32)
    s_nonce = SB("s_nonce", [2, 1], U32)
    s_hmask = SB("s_hmask", [128, 1], F32)
    s_flag = SB("s_flag", [1, 1], U32)
    s_row = SB("s_row", [1, 512], F32)
    s_row2 = SB("s_row2", [1, 512], F32)
    s_rowt = SB("s_rowt", [1, 512], F32)
    s_rms = SB("s_rms", [128, 32], F32)
    DT3 = SB("DT3", [128, 1536], F32)
    s_dt = DT3[0:32, 0:512]
    s_dt2 = DT3[0:32, 512:1024]
    s_dcr = DT3[0:32, 1024:1536]
    s_trA = SB("s_trA", [128, 128], BF16)
    s_trB = SB("s_trB", [128, 128], BF16)
    s_yis = SB("s_yis", [128, 512], BF16)
    s_otile1 = SB("s_otl1", [128, 512], F32)
    s_sq0 = SB("s_sq0", [128, 512], BF16)
    s_sq = [s_sq0, s_sq0]
    s_gm = [SB(f"s_gm{i}", [128, 128], BF16) for i in range(2)]
    s_mtx = [SB(f"s_mtx{i}", [128, 128], F32) for i in range(2)]
    s_mtm = [SB(f"s_mtm{i}", [128, 128], BF16) for i in range(2)]
    s_dxt = [SB(f"s_dxt{i}", [128, 64], BF16) for i in range(2)]
    s_b2t = [SB(f"s_b2t{i}", [128, 128], BF16) for i in range(2)]
    A1 = SB("A1", [128, 16 * TC], BF16)
    A2 = SB("A2", [128, 16 * TC], BF16)
    A3 = SB("A3", [128, 32 * TC], BF16)
    A4 = SB("A4", [128, 32 * TC], BF16)
    A5 = SB("A5", [128, 16 * TC], BF16)
    A6 = SB("A6", [128, 2048], BF16)
    A7 = [SB(f"A7_{i}", [128, 2048], BF16) for i in range(2)]
    A8 = [SB(f"A8_{i}", [128, 2048], BF16) for i in range(2)]
    A9 = SB("A9", [128, 4096], BF16)
    s_pc = [SB(f"s_pc{i}", [128, 4 + TC], BF16) for i in range(2)]
    s_hin = A6
    s_xf = [SCR[:, 3072:3584], SCR[:, 3584:4096]]
    s_otile = [DT3[:, 1024:1536], s_otile1]
    s_gt = s_pc[0][:, 0:512]

    pb = [stk.enter_context(nc.psum_tensor(f"pb{i}", [128, 512], F32)) for i in range(6)]
    pbf = [stk.enter_context(nc.psum_tensor(f"pbf{i}", [128, 1024], BF16)) for i in range(2)]

    sp_spin = stk.enter_context(nc.semaphore("sp_spin"))

    sems = {}
    hw = {}
    ops = {"pe": [], "dv": [], "ac": [], "sy": [], "gp": []}
    seqc = [0]
    ENGSEM = {"pe": "Epe", "dv": "Edv", "ac": "Eac"}

    def sem(name):
        if name not in sems:
            sems[name] = [stk.enter_context(nc.semaphore(name)), 0]
        return sems[name]

    for s in ENGSEM.values():
        sem(s)

    class Buf:
        __slots__ = ("w", "r")
        def __init__(self):
            self.w = None
            self.r = []
    bufs = {}

    def bget(name):
        if name not in bufs:
            bufs[name] = Buf()
        return bufs[name]

    def E(eng, reads=(), writes=(), deps=(), dma=None):
        def deco(fn):
            dd = list(deps)
            for r in reads:
                b = bget(r)
                if b.w is not None:
                    dd.append(b.w)
            for w in writes:
                b = bget(w)
                if b.w is not None:
                    dd.append(b.w)
                dd.extend(b.r)
            if dma is None:
                s = sem(ENGSEM[eng])
                s[1] += 1
                tok = (ENGSEM[eng], s[1])
            else:
                s = sem(dma)
                s[1] += 16
                tok = (dma, s[1])
            seqc[0] += 1
            ops[eng].append((fn, tuple(set(dd)), tok, seqc[0]))
            for r in reads:
                bget(r).r.append(tok)
            for w in writes:
                b = bget(w)
                b.w = tok
                b.r = []
            return tok
        return deco

    def all_tokens():
        return [(n, s[1]) for n, s in sems.items() if s[1] > 0]

    def barrier():
        toks = all_tokens()
        for eng in ops:
            if eng in ENGSEM:
                seqc[0] += 1
                ops[eng].append(((lambda e: e.nop()), tuple(toks), None, seqc[0]))
            else:
                seqc[0] += 1
                ops[eng].append(((lambda e: None), tuple(toks), None, seqc[0]))

    def replay(eng_obj, eng):
        for fn, dd, tok, sq in ops[eng]:
            if trunc is not None and sq > trunc:
                continue
            best = {}
            for sname, val in dd:
                if best.get(sname, 0) < val:
                    best[sname] = val
            for sname, val in sorted(best.items()):
                if eng == "pe" and sname == "Epe":
                    continue
                key = (eng, sname)
                if hw.get(key, 0) >= val:
                    continue
                eng_obj.wait_ge(sems[sname][0], val)
                hw[key] = val
            inst = fn(eng_obj)
            if inst is not None and tok is not None:
                inst.then_inc(sems[tok[0]][0], 16 if tok[0].startswith("d_") else 1)

    CL = lambda p=1: s_consts[0:p, 0:1]
    CR = lambda p=1: s_consts[0:p, 1:2]
    CH_ = lambda p=1: s_consts[0:p, 2:3]
    CD = lambda p=1: s_consts[0:p, 3:4]
    C64 = lambda p=1: s_consts[0:p, 4:5]
    C1 = lambda p=1: s_consts[0:p, 5:6]

    small = [(s_ident, ident_bf, "c_id"), (s_identf, ident_f, "c_idf"),
             (s_umask, umask_i, "c_um"), (s_ones, onesr_i, "c_on"),
             (s_nones, negonesr_i, "c_no"), (s_onec, onec_i, "c_oc"), (s_onecb, onecb_i, "c_ob"), (s_consts, consts_i, "c_ct"),
             (s_alog, alog, "c_al"), (s_dtbv, dtb, "c_db"),
             (s_sg, sg_i, "c_sg"), (s_su, su_i, "c_su"),
             (s_sd, sd_i, "c_sd"), (s_xidx, xidx_i, "c_xi"),
             (s_fidx, fidx_i, "c_fi"), (s_nonce, nonce_i, "c_nn"),
             (s_hmask, hmask_i, "c_hm")]
    for dst, src, nm in small:
        @E("sy", writes=(nm,), dma="d_small")
        def _(e, dst=dst, src=src):
            return e.dma_start(dst[:], src[:])
    for _, _, nm in small:
        bget(nm).w = ("d_small", sems["d_small"][1])
    SMALLT = all_tokens()

    @E("gp", writes=("xb",), dma="d_xb")
    def _(e):
        return e.dma_start(
            A1[:].rearrange("p (k t) -> p k t", t=TC),
            xT[:, 0:TC].rearrange("(k p) t -> p k t", p=128))

    @E("gp", writes=("x3f",), dma="d_x3")
    def _(e):
        return e.dma_start(
            s_x3f[:].rearrange("p (k f) -> p k f", f=4),
            xh3.rearrange("(k p) f -> p k f", p=128))

    def cast_stream(name, src, dst, nrows, pieces):
        wn = []
        rp = nrows // pieces
        for i in range(pieces):
            @E("gp", writes=(f"{name}cast{i}",), dma="d_" + name)
            def _(e, i=i):
                return e.dma_start(dst[i * rp:(i + 1) * rp, :], src[i * rp:(i + 1) * rp, :])
            wn.append(f"{name}cast{i}")
        for n_ in wn:
            bget(n_).w = ("d_" + name, sems["d_" + name][1])
        return tuple(wn)

    def cast_rows(name, src, dst, ranges):
        wn = []
        for i, (r0, r1) in enumerate(ranges):
            @E("gp", writes=(f"{name}cast{i}",), dma="d_" + name)
            def _(e, r0=r0, r1=r1):
                return e.dma_start(dst[r0:r1, :], src[r0:r1, :])
            wn.append(f"{name}cast{i}")
        for n_ in wn:
            bget(n_).w = ("d_" + name, sems["d_" + name][1])
        return tuple(wn)

    WINGRP = {}
    CVGRP = {}
    prev_tok = [()]

    def chain_cast(name, src, dst, ranges):
        toks = cast_rows_dep(name, src, dst, ranges, prev_tok[0])
        prev_tok[0] = toks
        return toks

    def cast_rows_dep(name, src, dst, ranges, dep):
        wn = []
        for i, (r0, r1) in enumerate(ranges):
            @E("gp", reads=dep if i == 0 else (), writes=(f"{name}cast{i}",), dma="d_" + name)
            def _(e, r0=r0, r1=r1):
                return e.dma_start(dst[r0:r1, :], src[r0:r1, :])
            wn.append(f"{name}cast{i}")
        for n_ in wn:
            bget(n_).w = ("d_" + name, sems["d_" + name][1])
        return tuple(wn)

    wa = chain_cast("wA", win_t, winb, [(96 * 128, 97 * 128), (16 * 128, 32 * 128)])
    for ci in [96] + list(range(16, 32)):
        WINGRP[ci] = wa
    cvt = chain_cast("cv0", conv_t, convb, [(0, 20 * 128)])
    for gi in range(5):
        CVGRP[gi] = cvt
    wb_ = chain_cast("wB", win_t, winb, [(0, 16 * 128)])
    for ci in range(16):
        WINGRP[ci] = wb_
    cvt = chain_cast("cv1", conv_t, convb, [(20 * 128, 40 * 128)])
    for gi in range(5, 10):
        CVGRP[gi] = cvt
    wc = chain_cast("wC0", win_t, winb, [(32 * 128, 48 * 128)])
    for ci in range(32, 48):
        WINGRP[ci] = wc
    cvt = chain_cast("cv2", conv_t, convb, [(40 * 128, 60 * 128)])
    for gi in range(10, 15):
        CVGRP[gi] = cvt
    wc = chain_cast("wC1", win_t, winb, [(48 * 128, 64 * 128)])
    for ci in range(48, 64):
        WINGRP[ci] = wc
    cvt = chain_cast("cv3", conv_t, convb, [(60 * 128, 80 * 128)])
    for gi in range(15, 20):
        CVGRP[gi] = cvt
    wc = chain_cast("wC2", win_t, winb, [(64 * 128, 80 * 128)])
    for ci in range(64, 80):
        WINGRP[ci] = wc
    wc = chain_cast("wC3", win_t, winb, [(80 * 128, 96 * 128)])
    for ci in range(80, 96):
        WINGRP[ci] = wc
    WOUTC = chain_cast("wout", wout_t, woutb, [(0, 16 * 128)])
    WGC = chain_cast("wg", wg_t, wgb, [(i * 16 * 128, (i + 1) * 16 * 128) for i in range(4)])
    WUC = chain_cast("wu", wu_t, wub, [(i * 16 * 128, (i + 1) * 16 * 128) for i in range(4)])
    WDC = chain_cast("wd", wd_t, wdb, [(i * 4 * 128, (i + 1) * 4 * 128) for i in range(4)])

    @E("ac", reads=("c_al",), writes=("A",), deps=SMALLT)
    def _(e):
        return e.activation(s_A[0:32, :], s_alog[0:32, :], AF.Exp)

    @E("dv", reads=("A",), writes=("A",))
    def _(e):
        return e.tensor_scalar(s_A[0:32, :], s_A[0:32, :], -1.0, None, ALU.mult)

    @E("dv", writes=("h",))
    def _(e):
        return e.memset(s_h[:], 0.0)

    @E("dv", writes=("hb",))
    def _(e):
        return e.memset(s_hb[:], 0.0)

    @E("dv", writes=("halo",))
    def _(e):
        return e.memset(s_halo[:], 0.0)

    @E("dv", writes=("s",))
    def _(e):
        return e.memset(s_sbuf[:, 0:1], 0.0)

    @E("dv", writes=("trA",))
    def _(e):
        return e.memset(s_trA[:], 0.0)

    @E("dv", writes=("trB",))
    def _(e):
        return e.memset(s_trB[:], 0.0)

    for k in range(16):
        @E("pe", reads=("x3f", "c_oc"), writes=("bk4",), deps=SMALLT)
        def _(e, k=k):
            return e.matmul(pb[4][0:1, 0:4], s_onec[:, 0:1], s_x3f[:, k * 4:(k + 1) * 4],
                            start=(k == 0), stop=(k == 15))

    @E("ac", reads=("x3f",), writes=("x3n",))
    def _(e):
        return e.square(s_xh3[:, :], s_x3f[:, :])

    for k in range(16):
        @E("pe", reads=("x3n", "c_ob"), writes=("bk5",), deps=SMALLT)
        def _(e, k=k):
            return e.matmul(pb[5][0:1, 0:4], s_onecb[:, 0:1], s_xh3[:, k * 4:(k + 1) * 4],
                            start=(k == 0), stop=(k == 15))

    @E("dv", reads=("bk4", "c_ct"), writes=("rowA",))
    def _(e):
        return e.tensor_scalar(s_row[0:1, 0:4], pb[4][0:1, 0:4], CD(), None, ALU.mult)

    @E("dv", reads=("bk5", "c_ct"), writes=("rowB",))
    def _(e):
        return e.tensor_scalar(s_row2[0:1, 0:4], pb[5][0:1, 0:4], CD(), None, ALU.mult)

    @E("dv", reads=("rowA",), writes=("rowT",))
    def _(e):
        return e.tensor_tensor(s_rowt[0:1, 0:4], s_row[0:1, 0:4], s_row[0:1, 0:4], ALU.mult)

    @E("dv", reads=("rowT", "rowB"), writes=("rowB",))
    def _(e):
        return e.tensor_tensor(s_row2[0:1, 0:4], s_row2[0:1, 0:4], s_rowt[0:1, 0:4], ALU.subtract)

    @E("ac", reads=("rowB", "c_ct"), writes=("rowB",))
    def _(e):
        return e.activation(s_row2[0:1, 0:4], s_row2[0:1, 0:4], AF.Ln, bias=CL())

    @E("ac", reads=("rowB", "c_ct"), writes=("rowB",))
    def _(e):
        return e.activation(s_row2[0:1, 0:4], s_row2[0:1, 0:4], AF.Exp, scale=CH_())

    @E("dv", reads=("rowA", "rowB"), writes=("rowA",))
    def _(e):
        return e.tensor_tensor(s_row[0:1, 0:4], s_row[0:1, 0:4], s_row2[0:1, 0:4], ALU.mult)

    @E("dv", reads=("rowA",), writes=("rowA",))
    def _(e):
        return e.tensor_scalar(s_row[0:1, 0:4], s_row[0:1, 0:4], -1.0, None, ALU.mult)

    @E("pe", reads=("rowB", "c_on"), writes=("bk4",))
    def _(e):
        return e.matmul(pb[4][:, 0:4], s_ones[0:1, :], s_row2[0:1, 0:4], start=True, stop=True)

    @E("pe", reads=("rowA", "c_on"), writes=("bk5",))
    def _(e):
        return e.matmul(pb[5][:, 4:8], s_ones[0:1, :], s_row[0:1, 0:4], start=True, stop=True)

    for k in range(16):
        @E("dv", reads=("x3f", "bk4", "x3n"), writes=("x3n",))
        def _(e, k=k):
            return e.tensor_tensor(s_xh3[:, k * 4:(k + 1) * 4], s_x3f[:, k * 4:(k + 1) * 4],
                                   pb[4][:, 0:4], ALU.mult)
    for k in range(16):
        @E("dv", reads=("bk5", "x3n"), writes=("x3n",))
        def _(e, k=k):
            return e.tensor_tensor(s_xh3[:, k * 4:(k + 1) * 4], s_xh3[:, k * 4:(k + 1) * 4],
                                   pb[5][:, 4:8], ALU.add)

    NWIN = 2
    NCV = 2

    def ln_stats_rows(src_name, src_tile_fn, n_tiles):
        for k in range(n_tiles):
            @E("pe", reads=(src_name, "c_ob"), writes=("bk4",), deps=SMALLT)
            def _(e, k=k):
                return e.matmul(pb[4][0:1, 0:512], s_onecb[:, 0:1], src_tile_fn(k),
                                start=(k == 0), stop=(k == n_tiles - 1))
        for k in range(n_tiles):
            @E("ac", reads=(src_name,), writes=("sq",))
            def _(e, k=k):
                return e.square(s_sq[k % 2][:, :], src_tile_fn(k))

            @E("pe", reads=("sq", "c_ob"), writes=("bk5",))
            def _(e, k=k):
                return e.matmul(pb[5][0:1, 0:512], s_onecb[:, 0:1], s_sq[k % 2][:, :],
                                start=(k == 0), stop=(k == n_tiles - 1))

        @E("dv", reads=("bk4", "c_ct"), writes=("rowA",))
        def _(e):
            return e.tensor_scalar(s_row[0:1, :], pb[4][0:1, 0:512], CD(), None, ALU.mult)

        @E("dv", reads=("bk5", "c_ct"), writes=("rowB",))
        def _(e):
            return e.tensor_scalar(s_row2[0:1, :], pb[5][0:1, 0:512], CD(), None, ALU.mult)

        @E("dv", reads=("rowA",), writes=("rowT",))
        def _(e):
            return e.tensor_tensor(s_rowt[0:1, :], s_row[0:1, :], s_row[0:1, :], ALU.mult)

        @E("dv", reads=("rowT", "rowB"), writes=("rowB",))
        def _(e):
            return e.tensor_tensor(s_row2[0:1, :], s_row2[0:1, :], s_rowt[0:1, :], ALU.subtract)

        @E("ac", reads=("rowB", "c_ct"), writes=("rowB",))
        def _(e):
            return e.activation(s_row2[0:1, :], s_row2[0:1, :], AF.Ln, bias=CL())

        @E("ac", reads=("rowB", "c_ct"), writes=("rowB",))
        def _(e):
            return e.activation(s_row2[0:1, :], s_row2[0:1, :], AF.Exp, scale=CH_())

        @E("dv", reads=("rowA", "rowB"), writes=("rowA",))
        def _(e):
            return e.tensor_tensor(s_row[0:1, :], s_row[0:1, :], s_row2[0:1, :], ALU.mult)

        @E("dv", reads=("rowA",), writes=("rowA",))
        def _(e):
            return e.tensor_scalar(s_row[0:1, :], s_row[0:1, :], -1.0, None, ALU.mult)

        @E("pe", reads=("rowB", "c_on"), writes=("bk4",))
        def _(e):
            return e.matmul(pb[4][:, 0:512], s_ones[0:1, :], s_row2[0:1, :], start=True, stop=True)

        @E("pe", reads=("rowA", "c_on"), writes=("bk5",))
        def _(e):
            return e.matmul(pb[5][:, 0:512], s_ones[0:1, :], s_row[0:1, :], start=True, stop=True)

    for ch in range(nchunk):
        c0 = ch * TC

        if ch > 0:
            @E("gp", writes=("xb",), dma="d_xb")
            def _(e, c0=c0):
                return e.dma_start(
                    A1[:].rearrange("p (k t) -> p k t", t=TC),
                    xT[:, c0:c0 + TC].rearrange("(k p) t -> p k t", p=128))

        ln_stats_rows("xb", lambda k: A1[:, k * TC:(k + 1) * TC], 16)
        for k in range(16):
            @E("dv", reads=("xb", "bk4"), writes=("xn",))
            def _(e, k=k):
                return e.tensor_tensor(A2[:, k * TC:(k + 1) * TC], A1[:, k * TC:(k + 1) * TC],
                                       pb[4][:, 0:512], ALU.mult)

            @E("dv", reads=("bk5",), writes=("xn",))
            def _(e, k=k):
                return e.tensor_tensor(A2[:, k * TC:(k + 1) * TC], A2[:, k * TC:(k + 1) * TC],
                                       pb[5][:, 0:512], ALU.add)

        if debug and ch == 0:
            @E("sy", reads=("xn",), writes=("xnd",), dma="d_dbg1")
            def _(e):
                return e.dma_start(xnd[:], A2[:])

        ci_order = [96] + list(range(16, 32)) + list(range(0, 16)) + list(range(32, 96))
        strips = [(ci,) for ci in ci_order]

        nb = 0
        cv_loaded = set()
        for si, pair in enumerate(strips):
            slot = si % NWIN

            @E("sy", reads=WINGRP[pair[0]], writes=(f"win{slot}",), dma=f"d_win{slot}")
            def _(e, pair=pair, slot=slot):
                return e.dma_start(A8[slot][:], winb[pair[0] * 128:(pair[0] + 1) * 128, :])

            for ci in pair:
                if ci == CI_DT or (CI_Z0 <= ci < CI_B0):
                    continue
                ti = ci if ci < 16 else ci - 16
                gi = ti // 4
                if gi not in cv_loaded:
                    cv_loaded.add(gi)
                    cslot = gi % NCV

                    @E("sy", reads=CVGRP[gi], writes=(f"cvl{cslot}",), dma=f"d_cvl{cslot}")
                    def _(e, gi=gi, cslot=cslot):
                        return e.dma_start(
                            A9[:, cslot * 2048:(cslot + 1) * 2048].rearrange("p (g m) -> p g m", g=4),
                            convb[gi * 4 * 128:(gi + 1) * 4 * 128, :]
                            .rearrange("(g p) m -> p g m", p=128))

            for j, ci in enumerate(pair):
                bank = nb % 3
                nb += 1
                for k in range(16):
                    @E("pe", reads=(f"win{slot}", "xn"), writes=(f"bk{bank}",))
                    def _(e, slot=slot, k=k, bank=bank):
                        return e.matmul(pb[bank][:, 0:512],
                                        A8[slot][:, k * 128:(k + 1) * 128],
                                        A2[:, k * TC:(k + 1) * TC],
                                        start=(k == 0), stop=(k == 15))
                if ch == 0 and (ci < 16 or CI_B0 <= ci < CI_DT):
                    ti = ci if ci < 16 else ci - 16
                    for k in range(16):
                        @E("pe", reads=(f"win{slot}", "x3n"), writes=("bk5",))
                        def _(e, slot=slot, k=k):
                            return e.matmul(pb[5][:, 8:12],
                                            A8[slot][:, k * 128:(k + 1) * 128],
                                            s_xh3[:, k * 4:(k + 1) * 4],
                                            start=(k == 0), stop=(k == 15))

                    @E("dv", reads=("bk5",), writes=("halo",))
                    def _(e, ti=ti):
                        return e.tensor_copy(s_halo[:, ti * 3:(ti + 1) * 3], pb[5][:, 8:11])

                if ci == CI_DT:
                    @E("dv", reads=(f"bk{bank}",), writes=("rdt",))
                    def _(e, bank=bank):
                        return e.tensor_scalar(s_dt[0:32, :], pb[bank][0:32, 0:512], 1.0, None, ALU.mult)
                elif CI_Z0 <= ci < CI_B0:
                    zi = ci - CI_Z0
                    hbuf, zo = (zi // 2) % 2, zi % 2

                    @E("ac", reads=(f"bk{bank}",), writes=(f"zs{hbuf}",))
                    def _(e, bank=bank, hbuf=hbuf, zo=zo):
                        return e.activation(A6[:, hbuf * 1024 + zo * TC:hbuf * 1024 + (zo + 1) * TC],
                                            pb[bank][:, 0:512], AF.Silu)

                    if zi % 2 == 1:
                        zp = zi // 2

                        for sb_ in range(NSUB):
                            @E("sy", reads=(f"zs{hbuf}",), writes=(f"zsdw{hbuf}",), dma=f"d_zs{hbuf}")
                            def _(e, ch=ch, zp=zp, hbuf=hbuf, sb_=sb_):
                                rb_ = (ch * 4 + sb_) * 128
                                return e.dma_start(
                                    zsd[rb_:rb_ + 128, zp * 256:(zp + 1) * 256]
                                    .rearrange("p (zo t) -> p zo t", t=128),
                                    A6[:, hbuf * 1024:(hbuf + 1) * 1024]
                                    .rearrange("p (zo tt) -> p zo tt", tt=TC)[:, :, sb_ * 128:(sb_ + 1) * 128])
                else:
                    ti = ci if ci < 16 else ci - 16
                    pcs = ti % 2

                    @E("dv", reads=("halo",), writes=(f"pc{pcs}",))
                    def _(e, ti=ti, pcs=pcs):
                        return e.tensor_copy(s_pc[pcs][:, 0:3], s_halo[:, ti * 3:(ti + 1) * 3])

                    @E("ac", reads=(f"bk{bank}",), writes=(f"pc{pcs}",))
                    def _(e, bank=bank, pcs=pcs):
                        return e.copy(s_pc[pcs][:, 3:3 + TC], pb[bank][:, 0:512])

                    @E("dv", reads=(f"pc{pcs}",), writes=("halo",))
                    def _(e, ti=ti, pcs=pcs):
                        return e.tensor_copy(s_halo[:, ti * 3:(ti + 1) * 3], s_pc[pcs][:, TC:TC + 3])

                    cslot = (ti // 4) % NCV
                    for kk in range(KCV):
                        @E("pe", reads=(f"cvl{cslot}", f"pc{pcs}"), writes=("bk3",))
                        def _(e, ti=ti, kk=kk, pcs=pcs, cslot=cslot):
                            return e.matmul(
                                pb[3][:, 0:512],
                                A9[:, cslot * 2048 + (ti % 4) * 512 + kk * 128:
                                   cslot * 2048 + (ti % 4) * 512 + (kk + 1) * 128],
                                s_pc[pcs][:, kk:kk + TC],
                                start=(kk == 0), stop=(kk == KCV - 1))
                    if ci < 16:
                        dstt, dsti, wname = A5, ci, "xc"
                    elif ci < CI_C0:
                        dstt, dsti, wname = A3, ci - CI_B0, "bc"
                    else:
                        dstt, dsti, wname = A4, ci - CI_C0, "cc"

                    @E("ac", reads=("bk3",), writes=(wname,))
                    def _(e, dstt=dstt, dsti=dsti):
                        return e.activation(dstt[:, dsti * TC:(dsti + 1) * TC],
                                            pb[3][:, 0:512], AF.Silu)

        @E("dv", reads=("rdt", "c_db"), writes=("rdt",))
        def _(e):
            return e.tensor_scalar(s_dt[0:32, :], s_dt[0:32, :], s_dtbv[0:32, :], None, ALU.add)

        @E("dv", reads=("rdt",), writes=("rdt2",))
        def _(e):
            return e.tensor_scalar(s_dt2[0:32, :], s_dt[0:32, :], -1.0, None, ALU.mult)

        @E("dv", reads=("rdt", "rdt2"), writes=("rdt2",))
        def _(e):
            return e.tensor_tensor(s_dt2[0:32, :], s_dt[0:32, :], s_dt2[0:32, :], ALU.min)

        @E("ac", reads=("rdt2",), writes=("rdt2",))
        def _(e):
            return e.activation(s_dt2[0:32, :], s_dt2[0:32, :], AF.Exp)

        @E("ac", reads=("rdt2", "c_ct"), writes=("rdt2",))
        def _(e):
            return e.activation(s_dt2[0:32, :], s_dt2[0:32, :], AF.Ln, bias=C1(32))

        @E("dv", reads=("rdt",), writes=("rdt",))
        def _(e):
            return e.tensor_scalar(s_dt[0:32, :], s_dt[0:32, :], 0.0, None, ALU.max)

        @E("dv", reads=("rdt", "rdt2"), writes=("rdt",))
        def _(e):
            return e.tensor_tensor(s_dt[0:32, :], s_dt[0:32, :], s_dt2[0:32, :], ALU.add)

        @E("dv", reads=("rdt", "A"), writes=("rdt2",))
        def _(e):
            return e.tensor_scalar(s_dt2[0:32, :], s_dt[0:32, :], s_A[0:32, :], None, ALU.mult)

        @E("dv", reads=("rdt2", "s", "c_ct"), writes=("s",))
        def _(e, c0=c0):
            return e.tensor_tensor_scan(
                s_sbuf[0:32, 1:1 + TC],
                s_consts[0:32, 5:6].broadcast_to([32, TC]),
                s_dt2[0:32, :], 0.0, ALU.mult, ALU.add)

        @E("dv", reads=("s",), writes=("s",))
        def _(e, c0=c0):
            return e.tensor_scalar(
                s_sbuf[0:32, 1:1 + TC], s_sbuf[0:32, 1:1 + TC],
                s_sbuf[0:32, 0:1], None, ALU.add)

        for sb_ in range(NSUB):
            @E("dv", reads=("s",), writes=("rdcr",))
            def _(e, c0=c0, sb_=sb_):
                return e.tensor_scalar(
                    s_dcr[0:32, sb_ * 128:(sb_ + 1) * 128],
                    s_sbuf[0:32, 1 + sb_ * 128:1 + (sb_ + 1) * 128],
                    s_sbuf[0:32, sb_ * 128:sb_ * 128 + 1], None, ALU.subtract)

        @E("ac", reads=("rdcr",), writes=("rdcr",))
        def _(e):
            return e.activation(s_dcr[0:32, :], s_dcr[0:32, :], AF.Exp)

        for sb_ in range(NSUB):
            @E("ac", reads=("s",), writes=("trA",))
            def _(e, sb_=sb_):
                return e.copy(s_trA[0:32, :], s_sbuf[0:32, 1 + sb_ * 128:1 + (sb_ + 1) * 128])

            @E("dv", reads=("s", "trA"), writes=("trB",))
            def _(e, sb_=sb_):
                return e.tensor_tensor(s_trB[0:32, :],
                                       s_sbuf[0:32, 1 + sb_ * 128:1 + (sb_ + 1) * 128],
                                       s_trA[0:32, :], ALU.subtract)

            if debug and ch == 0 and sb_ == 3:
                @E("sy", reads=("trA",), writes=("tradw",), dma="d_trad")
                def _(e):
                    return e.dma_start(trad[:].bitcast(BF16)[:, 0:128], s_trA[:])

            @E("pe", reads=("trA", "c_id"), writes=("bkf0",))
            def _(e):
                return e.transpose(pbf[0][:, 0:128], s_trA[:], s_ident[:])

            @E("dv", reads=("bkf0",), writes=("sT",))
            def _(e, ch=ch, sb_=sb_):
                return e.tensor_copy(s_sT[:, (ch * 4 + sb_) * 32:(ch * 4 + sb_ + 1) * 32],
                                     pbf[0][:, 0:32])

            if debug and ch == 0 and sb_ == 3:
                @E("dv", reads=("bkf0",), writes=("psdw0",))
                def _(e):
                    return e.tensor_copy(s_mtx[0][:], pbf[0][:, 0:128])

                @E("sy", reads=("psdw0",), writes=("psdw",), dma="d_psd")
                def _(e):
                    return e.dma_start(psd[:], s_mtx[0][:])

                @E("sy", reads=("c_id",), writes=("iddw",), dma="d_idd")
                def _(e):
                    return e.dma_start(idd[:].bitcast(BF16)[:, 0:128], s_ident[:])

                @E("sy", reads=("c_um",), writes=("umdw",), dma="d_umd")
                def _(e):
                    return e.dma_start(umd[:].bitcast(BF16)[:, 0:128], s_umask[:])

            @E("pe", reads=("trB", "c_id"), writes=("bkf0",))
            def _(e):
                return e.transpose(pbf[0][:, 0:128], s_trB[:], s_ident[:])

            @E("dv", reads=("bkf0", "sT"), writes=("sT",))
            def _(e, ch=ch, sb_=sb_):
                return e.tensor_tensor(s_sT[:, (ch * 4 + sb_) * 32:(ch * 4 + sb_ + 1) * 32],
                                       s_sT[:, (ch * 4 + sb_) * 32:(ch * 4 + sb_ + 1) * 32],
                                       pbf[0][:, 0:32], ALU.add)

            @E("ac", reads=("rdt",), writes=("trA",))
            def _(e, sb_=sb_):
                return e.copy(s_trA[0:32, :], s_dt[0:32, sb_ * 128:(sb_ + 1) * 128])

            @E("pe", reads=("trA", "c_id"), writes=("bkf0",))
            def _(e):
                return e.transpose(pbf[0][:, 0:128], s_trA[:], s_ident[:])

            @E("dv", reads=("bkf0",), writes=("dtT",))
            def _(e, sb_=sb_):
                return e.tensor_copy(s_dtT[:, sb_ * 32:(sb_ + 1) * 32], pbf[0][:, 0:32])

            @E("ac", reads=("rdcr",), writes=("trB",))
            def _(e, sb_=sb_):
                return e.copy(s_trB[0:32, :], s_dcr[0:32, sb_ * 128:(sb_ + 1) * 128])

            @E("pe", reads=("trB", "c_id"), writes=("bkf0",))
            def _(e):
                return e.transpose(pbf[0][:, 0:128], s_trB[:], s_ident[:])

            @E("dv", reads=("bkf0",), writes=("dCT",))
            def _(e, sb_=sb_):
                return e.tensor_copy(s_dCT[:, sb_ * 32:(sb_ + 1) * 32], pbf[0][:, 0:32])

        @E("sy", reads=("s",), writes=("sdram",), dma="d_sb")
        def _(e, c0=c0):
            return e.dma_start(sdram[:, 0:513], s_sbuf[0:32, 0:513])

        if debug and ch == 0:
            @E("sy", reads=("rdt",), writes=("dtd",), dma="d_dbg2")
            def _(e):
                return e.dma_start(dtd[:], s_dt[0:32, :])

            @E("sy", reads=("xc",), writes=("xcd",), dma="d_dbg4")
            def _(e):
                return e.dma_start(xcd[:], A5[:])

        @E("dv", reads=("s",), writes=("s",))
        def _(e):
            return e.tensor_copy(s_sbuf[0:32, 0:1], s_sbuf[0:32, 512:513])

        for sb_ in range(NSUB):
            toff = sb_ * 128
            gcol = (ch * 4 + sb_) * 32

            @E("sy", reads=("sdram",), writes=("sflat",), dma="d_sb2")
            def _(e, sb_=sb_):
                return e.dma_start(
                    s_sflat[0:1, 0:4128].rearrange("one (h t) -> one h t", t=129),
                    sdram[:, sb_ * 128:sb_ * 128 + 129].unsqueeze(0))

            @E("pe", reads=("sflat", "c_on", "c_no"), writes=("bk3",))
            def _(e):
                sf = s_sflat[0:1, :].rearrange("one (h t) -> one h t", t=129)
                e.matmul(pb[3][:, 0:32], s_ones[0:1, :],
                         sf[0:1, :, 128:129].rearrange("one h t -> one (h t)"),
                         start=True, stop=False)
                return e.matmul(pb[3][:, 0:32], s_nones[0:1, :],
                                sf[0:1, :, 0:1].rearrange("one h t -> one (h t)"),
                                start=False, stop=True)

            @E("ac", reads=("bk3",), writes=("drep",))
            def _(e, sb_=sb_):
                return e.activation(s_drep[:, sb_ * 32:(sb_ + 1) * 32], pb[3][:, 0:32], AF.Exp)

            if debug and ch == 0 and sb_ == 0:
                @E("sy", reads=("sflat",), writes=("sfd",), dma="d_dbg3")
                def _(e):
                    return e.dma_start(sfd[:], s_sflat[0:1, 0:4128])
            if debug and ch == 0 and sb_ == 3:
                @E("sy", reads=("sflat",), writes=("sfd2w",), dma="d_dbg6")
                def _(e):
                    return e.dma_start(sfd2[:], s_sflat[0:1, 0:4128])
            for grp in range(4):
                yb = 4 + (grp % 2)
                for hh in range(8):
                    h = grp * 8 + hh
                    m = h % 2
                    rb = m
                    gb = 2 + m

                    @E("pe", reads=("sflat", "c_on"), writes=(f"bk{rb}",))
                    def _(e, h=h, sb_=sb_, rb=rb):
                        sf = s_sflat[0:1, :].rearrange("one (h t) -> one h t", t=129)
                        return e.matmul(pb[rb][:, 0:128], s_ones[0:1, :],
                                        sf[0:1, h:h + 1, 1:129]
                                        .rearrange("one h t -> one (h t)"),
                                        start=True, stop=True)

                    @E("dv", reads=(f"bk{rb}", "sT"), writes=(f"mtx{m}",))
                    def _(e, h=h, gcol=gcol, m=m, rb=rb):
                        return e.tensor_scalar(s_mtx[m][:], pb[rb][:, 0:128],
                                               s_sT[:, gcol + h:gcol + h + 1],
                                               0.0, ALU.subtract, ALU.min)

                    if debug and ch == 0 and h == 31 and sb_ == 3:
                        @E("sy", reads=(f"mtx{m}",), writes=("argdw",), dma="d_arg")
                        def _(e, m=m):
                            return e.dma_start(argd[:], s_mtx[m][:])

                    @E("ac", reads=(f"mtx{m}",), writes=(f"mtx{m}",))
                    def _(e, m=m):
                        return e.activation(s_mtx[m][:], s_mtx[m][:], AF.Exp)

                    @E("dv", reads=(f"mtx{m}", "c_um"), writes=(f"mtm{m}",))
                    def _(e, m=m):
                        return e.tensor_tensor(s_mtm[m][:], s_mtx[m][:], s_umask[:], ALU.mult)

                    @E("pe", reads=("bc", "cc"), writes=(f"bk{gb}",))
                    def _(e, h=h, toff=toff, gb=gb):
                        return e.matmul(pb[gb][:, 0:128],
                                        A3[:, h * TC + toff:h * TC + toff + 128],
                                        A4[:, h * TC + toff:h * TC + toff + 128],
                                        start=True, stop=True)

                    @E("dv", reads=(f"bk{gb}", f"mtm{m}"), writes=(f"gm{m}",))
                    def _(e, m=m, gb=gb):
                        return e.tensor_tensor(s_gm[m][:], pb[gb][:, 0:128], s_mtm[m][:], ALU.mult)

                    @E("pe", reads=("bc", "c_id"), writes=("bkf0",))
                    def _(e, h=h, toff=toff):
                        return e.transpose(pbf[0][:, 0:128],
                                           A3[:, h * TC + toff:h * TC + toff + 128], s_ident[:])

                    @E("dv", reads=("bkf0", f"mtx{m}"), writes=(f"b2t{m}",))
                    def _(e, m=m):
                        return e.tensor_scalar(s_b2t[m][:], pbf[0][:, 0:128],
                                               s_mtx[m][:, 127:128], None, ALU.mult)

                    if m == 0:
                        @E("pe", reads=("xc", "c_id"), writes=("bkf1",))
                        def _(e, h=h, toff=toff):
                            return e.transpose(pbf[1][:, 0:128],
                                               A5[:, (h // 2) * TC + toff:(h // 2) * TC + toff + 128],
                                               s_ident[:])

                    @E("dv", reads=("bkf1", "dtT"), writes=(f"dxt{m}",))
                    def _(e, h=h, sb_=sb_, m=m):
                        return e.tensor_scalar(s_dxt[m][:, 0:64],
                                               pbf[1][:, (h % 2) * 64:(h % 2) * 64 + 64],
                                               s_dtT[:, sb_ * 32 + h:sb_ * 32 + h + 1], None, ALU.mult)

                    @E("pe", reads=(f"gm{m}", f"dxt{m}"), writes=(f"bk{yb}",))
                    def _(e, m=m, yb=yb, hh=hh):
                        return e.matmul(pb[yb][:, hh * 64:(hh + 1) * 64],
                                        s_gm[m][:], s_dxt[m][:, 0:64], start=True, stop=True)

                    @E("pe", reads=("cc", "hb"), writes=(f"bk{rb}",))
                    def _(e, h=h, toff=toff, rb=rb):
                        return e.matmul(pb[rb][:, 128:192],
                                        A4[:, h * TC + toff:h * TC + toff + 128],
                                        s_hb[:, h * 64:(h + 1) * 64], start=True, stop=True)

                    @E("dv", reads=(f"bk{rb}", "dCT"), writes=("yis",))
                    def _(e, h=h, sb_=sb_, hh=hh, rb=rb):
                        return e.tensor_scalar(s_yis[:, hh * 64:(hh + 1) * 64], pb[rb][:, 128:192],
                                               s_dCT[:, sb_ * 32 + h:sb_ * 32 + h + 1], None, ALU.mult)

                    @E("pe", reads=(f"b2t{m}", f"dxt{m}"), writes=(f"bk{gb}",))
                    def _(e, m=m, gb=gb):
                        return e.matmul(pb[gb][:, 128:192], s_b2t[m][:], s_dxt[m][:, 0:64],
                                        start=True, stop=True)

                    @E("dv", reads=(f"bk{gb}", "drep", "h"), writes=("h",))
                    def _(e, h=h, sb_=sb_, gb=gb):
                        return e.scalar_tensor_tensor(
                            s_h[:, h * 64:(h + 1) * 64], s_h[:, h * 64:(h + 1) * 64],
                            s_drep[:, sb_ * 32 + h:sb_ * 32 + h + 1], pb[gb][:, 128:192],
                            ALU.mult, ALU.add)

                @E("dv", reads=(f"bk{yb}", "yis"), writes=(f"yp{sb_ % 2}",))
                def _(e, grp=grp, yb=yb, sb_=sb_):
                    return e.tensor_tensor(A7[sb_ % 2][:, grp * 512:(grp + 1) * 512],
                                           pb[yb][:, 0:512], s_yis[:], ALU.add)

            @E("ac", reads=("h",), writes=("hb",))
            def _(e):
                return e.copy(s_hb[:], s_h[:])

            @E("sy", reads=(f"yp{sb_ % 2}",), writes=(f"ypdw{sb_ % 2}",), dma=f"d_yp{sb_ % 2}")
            def _(e, c0=c0, sb_=sb_):
                return e.dma_start(ypd[c0 + sb_ * 128:c0 + (sb_ + 1) * 128, :], A7[sb_ % 2][:])

        if debug and ch == 0:
            for nmsb, dst, srcap in [("drep", drepd, s_drep[:]), ("dCT", dctd, s_dCT[:]),
                                     ("dtT", dttd, s_dtT[:]), ("sT", sTd, s_sT[:, 0:128]),
                                     ("mtx1", mtxd, s_mtx[1][:]), ("gm1", gmd, s_gm[1][:]),
                                     ("dxt1", dxtd, s_dxt[1][:]), ("b2t1", b2td, s_b2t[1][:])]:
                @E("sy", reads=(nmsb,), writes=(f"dbg_{nmsb}",), dma="d_dbg5")
                def _(e, dst=dst, srcap=srcap):
                    return e.dma_start(dst[:] if False else dst[0:srcap.shape[0], 0:srcap.shape[1]], srcap)

        for sb_ in range(NSUB):
            @E("sy", reads=("cc",), writes=("ccdw",), dma="d_ccw")
            def _(e, ch=ch, sb_=sb_):
                rb_ = (ch * 4 + sb_) * 128
                return e.dma_start(
                    ccd[rb_:rb_ + 128, :].rearrange("p (h t) -> p h t", t=128),
                    A4[:].rearrange("p (h tt) -> p h tt", tt=TC)[:, :, sb_ * 128:(sb_ + 1) * 128])


    barrier()
    if debug:
        @E("sy", reads=("h",), writes=("hdbgw",), dma="d_hdbg")
        def _(e):
            return e.dma_start(hdbg[:], s_h[:])

    @E("gp", reads=("hb", "c_xi"), writes=("hxw",), dma="d_hxw")
    def _(e):
        return e.indirect_dma_start(
            hx[0:260, :], bass.IndirectOffsetOnAxis(s_xidx[:, 0:1], 0),
            s_hb[:], None)

    @E("gp", reads=("hxw", "c_fi", "c_nn"), writes=("hxfw",), dma="d_hxf")
    def _(e):
        return e.indirect_dma_start(
            hxf.rearrange("r (c one) -> (r c) one", one=1)[0:8, 0:1],
            bass.IndirectOffsetOnAxis(s_fidx[:, 0:1], 0),
            s_nonce[:], None)

    @E("gp", reads=("hxfw", "c_nn"), writes=("hin",), dma="d_hxr")
    def _(e):
        with e.register("spin") as spin, e.register("rf") as rf, e.register("rn") as rn:
            e.reg_load(rn, s_nonce[0:1, 0:1])
            e.reg_mov(spin, 1)
            with ExitStack() as nest:
                for it in range(N_POLL):
                    nest.enter_context(e.If(spin))
                    e.dma_start(s_flag[:], hxf[0:1, 0:1]).then_inc(sp_spin, 16)
                    e.wait_ge(sp_spin, 16 * (it + 1))
                    e.reg_load(rf, s_flag[0:1, 0:1])
                    e.reg_sub(spin, rf, rn)
            with e.While(spin):
                e.nop()
        return e.dma_start(s_hin[:], hx[0:128, :])

    @E("dv", reads=("hin", "c_hm"), writes=("hin",))
    def _(e):
        return e.tensor_scalar(s_hin[:], s_hin[:], s_hmask[:, 0:1], None, ALU.mult)

    if debug:
        @E("gp", reads=("hin",), writes=("flgdw",), dma="d_flgd")
        def _(e):
            e.dma_start(flgd[0:1, 0:2], hxf[0:1, 0:2]).then_inc(sems["d_flgd"][0], 16)
            return e.dma_start(flgd[0:1, 2:3], s_nonce[0:1, 0:1])

    MO = A1
    X2 = A2
    YG = A5
    FU = lambda k: (A3[:, k * TC:(k + 1) * TC] if k < 32 else A4[:, (k - 32) * TC:(k - 31) * TC])
    YF = A7[0]
    YQ = A7[1]
    WD2 = lambda i: A8[i][:]
    WO2 = [SCR[:, 0:1024].bitcast(BF16), s_hb[:, 0:2048]]
    WG2 = [SCR[:, 1024:2048].bitcast(BF16), s_h[:, 0:1024].bitcast(BF16)]
    WU2 = [SCR[:, 2048:3072].bitcast(BF16), s_h[:, 1024:2048].bitcast(BF16)]
    CC2 = A9
    ZS2 = DT3[:, 0:1024].bitcast(BF16)

    pending_out = [None]
    for ch in range(nchunk):
        c0 = ch * TC

        @E("ac", reads=("sT",), writes=("expsT",))
        def _(e, ch=ch):
            return e.activation(s_expsT[:], s_sT[:, ch * 128:(ch + 1) * 128], AF.Exp)

        for sb_ in range(NSUB):
            gcol = sb_ * 32
            if pending_out[0] is not None:
                for di_ in range(sb_ * 4, (sb_ + 1) * 4):
                    pending_out[0](di_)
                if sb_ == NSUB - 1:
                    pending_out[0] = None

            @E("gp", writes=("yq",), dma="d_yp2")
            def _(e, c0=c0, sb_=sb_):
                return e.dma_start(YQ[:], ypd[c0 + sb_ * 128:c0 + (sb_ + 1) * 128, :])

            @E("gp", writes=("cc2",), dma="d_cc2")
            def _(e, ch=ch, sb_=sb_):
                rb_ = (ch * 4 + sb_) * 128
                return e.dma_start(CC2[:], ccd[rb_:rb_ + 128, :])

            @E("gp", writes=("zs2",), dma="d_zs2")
            def _(e, ch=ch, sb_=sb_):
                rb_ = (ch * 4 + sb_) * 128
                return e.dma_start(ZS2[:, :], zsd[rb_:rb_ + 128, :])

            for h in range(32):
                m = h % 2

                @E("pe", reads=("cc2", "hin"), writes=(f"bk{m}",))
                def _(e, h=h, m=m):
                    return e.matmul(pb[m][:, 0:64], CC2[:, h * 128:(h + 1) * 128],
                                    s_hin[:, h * 64:(h + 1) * 64], start=True, stop=True)

                @E("dv", reads=(f"bk{m}", "expsT", "yq"), writes=("yf",))
                def _(e, h=h, m=m, gcol=gcol):
                    return e.scalar_tensor_tensor(
                        YF[:, h * 64:(h + 1) * 64], pb[m][:, 0:64],
                        s_expsT[:, gcol + h:gcol + h + 1],
                        YQ[:, h * 64:(h + 1) * 64], ALU.mult, ALU.add)

            @E("ac", reads=("yf",), writes=("yq",))
            def _(e):
                return e.square(YQ[:], YF[:])

            @E("dv", reads=("yq",), writes=("rms",))
            def _(e):
                return e.reduce_sum(s_rms[:, :], YQ[:].rearrange("p (h q) -> p h q", q=64), axis=AX.X)

            @E("ac", reads=("rms", "c_ct"), writes=("rms",))
            def _(e):
                return e.activation(s_rms[:, :], s_rms[:, :], AF.Ln, scale=C64(128), bias=CR(128))

            @E("ac", reads=("rms", "c_ct"), writes=("rms",))
            def _(e):
                return e.activation(s_rms[:, :], s_rms[:, :], AF.Exp, scale=CH_(128))

            @E("dv", reads=("rms", "yf"), writes=("yf",))
            def _(e):
                return e.tensor_tensor(
                    YF[:].rearrange("p (h q) -> p h q", q=64),
                    YF[:].rearrange("p (h q) -> p h q", q=64),
                    s_rms[:, :].unsqueeze(2).broadcast_to([128, 32, 64]), ALU.mult)

            for pr in range(16):
                fb = pr % 2

                @E("pe", reads=("yf", "c_id"), writes=(f"bkf{fb}",))
                def _(e, pr=pr, fb=fb):
                    return e.transpose(pbf[fb][:, 0:128], YF[:, pr * 128:(pr + 1) * 128], s_ident[:])

                @E("dv", reads=(f"bkf{fb}", "zs2"), writes=("yg",))
                def _(e, pr=pr, fb=fb, sb_=sb_):
                    return e.tensor_tensor(YG[:, pr * TC + sb_ * 128:pr * TC + (sb_ + 1) * 128],
                                           pbf[fb][:, 0:128],
                                           ZS2[:, pr * 128:(pr + 1) * 128], ALU.mult)

        for di in range(16):
            ws = di % 2

            @E("sy", reads=WOUTC + ("hxw",), writes=(f"wo{ws}",), dma=f"d_wo{ws}")
            def _(e, di=di, ws=ws):
                return e.dma_start(WO2[ws][:, :], woutb[di * 128:(di + 1) * 128, :])

            bank = di % 3
            for k in range(16):
                @E("pe", reads=(f"wo{ws}", "yg"), writes=(f"bk{bank}",))
                def _(e, di=di, k=k, bank=bank, ws=ws):
                    return e.matmul(pb[bank][:, 0:512], WO2[ws][:, k * 128:(k + 1) * 128],
                                    YG[:, k * TC:(k + 1) * TC], start=(k == 0), stop=(k == 15))

            @E("ac", reads=(f"bk{bank}",), writes=("mo",))
            def _(e, di=di, bank=bank):
                return e.copy(MO[:, di * TC:(di + 1) * TC], pb[bank][:, 0:512])

        ln_stats_rows("mo", lambda k: MO[:, k * TC:(k + 1) * TC], 16)
        for di in range(16):
            xs = di % 2

            @E("sy", writes=(f"xf{xs}",), dma=f"d_xf{xs}")
            def _(e, c0=c0, di=di, xs=xs):
                return e.dma_start(s_xf[xs][:], xT[di * 128:(di + 1) * 128, c0:c0 + TC])

            @E("dv", reads=("mo", "bk4"), writes=("x2",))
            def _(e, di=di):
                return e.tensor_tensor(X2[:, di * TC:(di + 1) * TC], MO[:, di * TC:(di + 1) * TC],
                                       pb[4][:, 0:512], ALU.mult)

            @E("dv", reads=("bk5",), writes=("x2",))
            def _(e, di=di):
                return e.tensor_tensor(X2[:, di * TC:(di + 1) * TC], X2[:, di * TC:(di + 1) * TC],
                                       pb[5][:, 0:512], ALU.add)

            @E("dv", reads=(f"xf{xs}",), writes=("x2",))
            def _(e, di=di, xs=xs):
                return e.tensor_tensor(X2[:, di * TC:(di + 1) * TC], X2[:, di * TC:(di + 1) * TC],
                                       s_xf[xs][:], ALU.add)

        ln_stats_rows("x2", lambda k: X2[:, k * TC:(k + 1) * TC], 16)
        for k in range(16):
            @E("dv", reads=("x2", "bk4"), writes=("mo",))
            def _(e, k=k):
                return e.tensor_tensor(MO[:, k * TC:(k + 1) * TC], X2[:, k * TC:(k + 1) * TC],
                                       pb[4][:, 0:512], ALU.mult)

            @E("dv", reads=("bk5",), writes=("mo",))
            def _(e, k=k):
                return e.tensor_tensor(MO[:, k * TC:(k + 1) * TC], MO[:, k * TC:(k + 1) * TC],
                                       pb[5][:, 0:512], ALU.add)

        for ii in range(64):
            gb = ii % 2
            ub = 2 + (ii % 2)

            ws = ii % 2

            @E("sy", reads=WGC + ("hxw",), writes=(f"wg2{ws}",), dma=f"d_wg2{ws}")
            def _(e, ii=ii, ws=ws):
                return e.dma_start(WG2[ws][:, :], wgb[ii * 128:(ii + 1) * 128, :])

            @E("sy", reads=WUC + ("hxw",), writes=(f"wu2{ws}",), dma=f"d_wu2{ws}")
            def _(e, ii=ii, ws=ws):
                return e.dma_start(WU2[ws][:, :], wub[ii * 128:(ii + 1) * 128, :])

            for k in range(16):
                @E("pe", reads=(f"wg2{ws}", "mo"), writes=(f"bk{gb}",))
                def _(e, k=k, gb=gb, ws=ws):
                    return e.matmul(pb[gb][:, 0:512], WG2[ws][:, k * 128:(k + 1) * 128],
                                    MO[:, k * TC:(k + 1) * TC], start=(k == 0), stop=(k == 15))
            for k in range(16):
                @E("pe", reads=(f"wu2{ws}", "mo"), writes=(f"bk{ub}",))
                def _(e, k=k, ub=ub, ws=ws):
                    return e.matmul(pb[ub][:, 0:512], WU2[ws][:, k * 128:(k + 1) * 128],
                                    MO[:, k * TC:(k + 1) * TC], start=(k == 0), stop=(k == 15))

            @E("ac", reads=(f"bk{gb}", "c_sg"), writes=("gt",))
            def _(e, ii=ii, gb=gb):
                return e.activation(s_gt[:], pb[gb][:, 0:512], AF.Silu, scale=s_sg[:, ii:ii + 1])

            @E("dv", reads=(f"bk{ub}", "gt", "c_su"), writes=("fu",))
            def _(e, ii=ii, ub=ub):
                return e.scalar_tensor_tensor(FU(ii), pb[ub][:, 0:512], s_su[:, ii:ii + 1],
                                              s_gt[:], ALU.mult, ALU.mult)

        for di in range(16):
            dbk = 4 + (di % 2)
            for q in range(4):
                @E("sy", reads=WDC, writes=(f"wdl{q % 2}",), dma=f"d_wd{q % 2}")
                def _(e, di=di, q=q):
                    return e.dma_start(WD2(q % 2),
                                       wdb[di * 128:(di + 1) * 128, q * 2048:(q + 1) * 2048])
                for k in range(q * 16, (q + 1) * 16):
                    @E("pe", reads=(f"wdl{q % 2}", "fu"), writes=(f"bk{dbk}",))
                    def _(e, k=k, q=q, dbk=dbk):
                        return e.matmul(pb[dbk][:, 0:512],
                                        WD2(q % 2)[:, (k % 16) * 128:(k % 16 + 1) * 128],
                                        FU(k), start=(k == 0), stop=(k == 63))

            @E("ac", reads=(f"bk{dbk}", "c_sd"), writes=("mo",))
            def _(e, di=di, dbk=dbk):
                return e.mul(MO[:, di * TC:(di + 1) * TC], pb[dbk][:, 0:512], s_sd[:, di:di + 1])

        ln_stats_rows("mo", lambda k: MO[:, k * TC:(k + 1) * TC], 16)

        def emit_out_tile(di, c0=c0):
            os_ = di % 2

            @E("dv", reads=("mo", "bk4"), writes=(f"ot{os_}",))
            def _(e, di=di, os_=os_):
                return e.tensor_tensor(s_otile[os_][:], MO[:, di * TC:(di + 1) * TC],
                                       pb[4][:, 0:512], ALU.mult)

            @E("dv", reads=("bk5",), writes=(f"ot{os_}",))
            def _(e, di=di, os_=os_):
                return e.tensor_tensor(s_otile[os_][:], s_otile[os_][:],
                                       pb[5][:, 0:512], ALU.add)

            @E("dv", reads=("x2",), writes=(f"ot{os_}",))
            def _(e, di=di, os_=os_):
                return e.tensor_tensor(s_otile[os_][:], s_otile[os_][:],
                                       X2[:, di * TC:(di + 1) * TC], ALU.add)

            @E("sy", reads=(f"ot{os_}",), writes=(f"otd{os_}",), dma=f"d_ot{os_}")
            def _(e, c0=c0, di=di, os_=os_):
                return e.dma_start(out_d[di * 128:(di + 1) * 128, c0:c0 + TC], s_otile[os_][:])
        pending_out[0] = emit_out_tile

    for di in range(16):
        pending_out[0](di)
    pending_out[0] = None

    if trunc is not None:
        cut = {}
        for eng in ops:
            for fn, dd, tok, sq in ops[eng]:
                if sq <= trunc and tok is not None:
                    cut[tok[0]] = max(cut.get(tok[0], 0), tok[1])
        FINAL = list(cut.items())
    else:
        FINAL = all_tokens()
    ops["gp"].append(((lambda e: None), tuple(FINAL), None, 0))

    with nc.Block() as blk:
        blk.tensor(lambda e: replay(e, "pe"))
        blk.vector(lambda e: replay(e, "dv"))
        blk.scalar(lambda e: replay(e, "ac"))
        blk.sync(lambda e: replay(e, "sy"))
        blk.gpsimd(lambda e: replay(e, "gp"))

    stk.close()
    return nc


_NC_CACHE = {}


def _host_inputs(inputs, nchunk=NCHUNK):
    th = nchunk * TC
    f32 = np.float32
    Win = np.asarray(inputs["L0_attn_Win"], f32)
    Wout = np.asarray(inputs["L0_attn_Wout"], f32)
    wg = np.asarray(inputs["L0_ffn_w_gate"])
    wu = np.asarray(inputs["L0_ffn_w_up"])
    wd = np.asarray(inputs["L0_ffn_w_down"])
    cwx = np.asarray(inputs["L0_attn_conv_w_x"], f32)
    cwb = np.asarray(inputs["L0_attn_conv_w_b"], f32)
    cwc = np.asarray(inputs["L0_attn_conv_w_c"], f32)
    x = np.asarray(inputs["L0_attn_x"], f32)

    WinP = np.zeros((2048, NCI * 128), f32)
    WinP[:, :PD] = Win.T
    win_t = np.ascontiguousarray(
        WinP.reshape(16, 128, NCI, 128).transpose(2, 1, 0, 3)).reshape(NCI * 128, 2048)
    wout_t = np.ascontiguousarray(
        Wout.T.reshape(16, 128, 16, 128).transpose(2, 1, 0, 3)).reshape(16 * 128, 2048)
    wg_t = np.ascontiguousarray(
        wg.reshape(16, 128, 64, 128).transpose(2, 1, 0, 3)).reshape(64 * 128, 2048)
    wu_t = np.ascontiguousarray(
        wu.reshape(16, 128, 64, 128).transpose(2, 1, 0, 3)).reshape(64 * 128, 2048)
    wd_t = np.ascontiguousarray(
        wd.reshape(64, 128, 16, 128).transpose(2, 1, 0, 3)).reshape(16 * 128, 8192)

    wvec = np.concatenate([cwx.reshape(2048, 4), cwb.reshape(4096, 4), cwc.reshape(4096, 4)], 0)
    conv_t = np.zeros((80, 128, 4, 128), f32)
    rr = np.arange(128)
    for ti in range(80):
        for k in range(4):
            conv_t[ti, rr, k, rr] = wvec[ti * 128 + rr, k]
    conv_t = conv_t.reshape(80 * 128, 512)

    consts = np.zeros((128, 8), f32)
    consts[:, 0] = 1e-5
    consts[:, 1] = 1e-6
    consts[:, 2] = -0.5
    consts[:, 3] = 1.0 / 2048.0
    consts[:, 4] = 1.0 / 64.0
    consts[:, 5] = 1.0

    com = {
        "win_t": win_t, "wout_t": wout_t, "wg_t": wg_t, "wu_t": wu_t, "wd_t": wd_t,
        "conv_t": conv_t,
        "alog": np.asarray(inputs["L0_attn_A_log"], f32).reshape(32, 1),
        "dtb": np.asarray(inputs["L0_attn_dt_bias"], f32).reshape(32, 1),
        "sg_i": np.ascontiguousarray(np.asarray(inputs["L0_ffn_s_gate"], f32).reshape(64, 128).T),
        "su_i": np.ascontiguousarray(np.asarray(inputs["L0_ffn_s_up"], f32).reshape(64, 128).T),
        "sd_i": np.ascontiguousarray(np.asarray(inputs["L0_ffn_s_down"], f32).reshape(16, 128).T),
        "ident_bf": np.asarray(np.eye(128, dtype=ml_dtypes.bfloat16)),
        "ident_f": np.eye(128, dtype=f32),
        "umask_i": np.asarray(np.triu(np.ones((128, 128), ml_dtypes.bfloat16))),
        "onesr_i": np.ones((1, 128), f32),
        "onec_i": np.ones((128, 1), f32),
        "onecb_i": np.ones((128, 1), ml_dtypes.bfloat16),
        "negonesr_i": -np.ones((1, 128), f32),
        "consts_i": consts,
    }
    nonce = (np.frombuffer(np.random.bytes(4), dtype=np.uint32)[0] & np.uint32(0x7FFFFFFF)) | np.uint32(1)
    in_maps = []
    for c in range(NCORES):
        b, hf = c // 2, c % 2
        sl = x[b, hf * th:(hf + 1) * th, :]
        xTc = np.ascontiguousarray(sl.T)
        x3 = np.zeros((D, 4), f32)
        if hf == 1:
            x3[:, 0:3] = x[b, th - 3:th, :].T
        even = (hf == 0)
        m = {"xT": xTc, "xh3": x3,
             "xidx_i": (np.arange(128, dtype=np.int32) if even
                        else np.arange(128, dtype=np.int32) + 130).reshape(128, 1),
             "fidx_i": (np.array([[0], [1]], np.int32) if even
                        else np.array([[4], [5]], np.int32)),
             "nonce_i": np.full((2, 1), nonce, np.uint32),
             "hmask_i": np.full((128, 1), 0.0 if even else 1.0, f32)}
        m.update(com)
        in_maps.append(m)
    return in_maps


def kernel(**inputs):
    debug = bool(inputs.pop("_debug", False))
    key = "dbg" if debug else "std"
    if key not in _NC_CACHE:
        _NC_CACHE[key] = build_program(debug=debug)
    nc = _NC_CACHE[key]
    in_maps = _host_inputs(inputs)
    res = run_bass_kernel_spmd(nc, in_maps, core_ids=list(range(NCORES)))
    out = np.empty((B, T, D), np.float32)
    for c in range(NCORES):
        b, hf = c // 2, c % 2
        out[b, hf * THALF:(hf + 1) * THALF, :] = res.results[c]["out_d"].T
    if debug:
        return out, res
    return out


# version 09
# speedup vs baseline: 1.0679x; 1.0019x over previous
import numpy as np
import ml_dtypes
import concourse.bass as bass
import concourse.mybir as mybir
from concourse.bass_utils import run_bass_kernel_spmd
from contextlib import ExitStack

F32 = mybir.dt.float32
BF16 = mybir.dt.bfloat16
I8 = mybir.dt.int8
I32 = mybir.dt.int32
U32 = mybir.dt.uint32
ALU = mybir.AluOpType
AF = mybir.ActivationFunctionType
AX = mybir.AxisListType

B, T, D = 4, 8192, 2048
PD, NCI = 12320, 97
TC, NCHUNK, THALF = 512, 8, 4096
SPIN_MODE = "while"
N_POLL = 1536
NCORES = 8
NSUB = 4
KCV = 4
CI_Z0, CI_B0, CI_C0, CI_DT = 16, 32, 64, 96


def build_program(debug=False, nchunk=NCHUNK, trunc=None):
    th = nchunk * TC
    nc = bass.Bass("TRN2", target_bir_lowering=False, debug=False,
                   num_devices=NCORES, detect_race_conditions=False)

    dri = lambda n, sh, dt: nc.dram_tensor(n, sh, dt, kind="ExternalInput").ap()
    xT = dri("xT", [D, th], F32)
    xh3 = dri("xh3", [D, 4], F32)
    win_t = dri("win_t", [NCI * 128, 2048], F32)
    wout_t = dri("wout_t", [16 * 128, 2048], F32)
    wg_t = dri("wg_t", [64 * 128, 2048], I8)
    wu_t = dri("wu_t", [64 * 128, 2048], I8)
    wd_t = dri("wd_t", [16 * 128, 8192], I8)
    conv_t = dri("conv_t", [80 * 128, 512], F32)
    alog = dri("alog", [32, 1], F32)
    dtb = dri("dtb", [32, 1], F32)
    sg_i = dri("sg_i", [128, 64], F32)
    su_i = dri("su_i", [128, 64], F32)
    sd_i = dri("sd_i", [128, 16], F32)
    ident_bf = dri("ident_bf", [128, 128], BF16)
    ident_f = dri("ident_f", [128, 128], F32)
    umask_i = dri("umask_i", [128, 128], BF16)
    onesr_i = dri("onesr_i", [1, 128], F32)
    onec_i = dri("onec_i", [128, 1], F32)
    onecb_i = dri("onecb_i", [128, 1], BF16)
    negonesr_i = dri("negonesr_i", [1, 128], F32)
    consts_i = dri("consts_i", [128, 8], F32)
    xidx_i = dri("xidx_i", [128, 1], I32)
    fidx_i = dri("fidx_i", [2, 1], I32)
    nonce_i = dri("nonce_i", [2, 1], U32)
    hmask_i = dri("hmask_i", [128, 1], F32)

    out_d = nc.dram_tensor("out_d", [D, th], F32, kind="ExternalOutput").ap()

    okind = "ExternalOutput" if debug else "Internal"
    drs = lambda n, sh, dt, kind="Internal": nc.dram_tensor(n, sh, dt, kind=kind).ap()
    winb = drs("winb", [NCI * 128, 2048], BF16)
    woutb = drs("woutb", [16 * 128, 2048], BF16)
    wgb = drs("wgb", [64 * 128, 2048], BF16)
    wub = drs("wub", [64 * 128, 2048], BF16)
    wdb = drs("wdb", [16 * 128, 8192], BF16)
    convb = drs("convb", [80 * 128, 512], BF16)
    ypd = drs("ypd", [th, 2048], BF16, okind)
    ccd = drs("ccd", [th, 4096], BF16, okind)
    zsd = drs("zsd", [th, 2048], BF16, okind)
    sdram = drs("sdram", [32, 520], F32)
    hdbg = drs("hdbg", [128, 2048], F32, okind) if debug else None
    xnd = drs("xnd", [128, 8192], BF16, okind) if debug else None
    flgd = drs("flgd", [1, 4], U32, okind) if debug else None
    drepd = drs("drepd", [128, 128], F32, okind) if debug else None
    dctd = drs("dctd", [128, 128], F32, okind) if debug else None
    dttd = drs("dttd", [128, 128], F32, okind) if debug else None
    sTd = drs("sTd", [128, 128], F32, okind) if debug else None
    mtxd = drs("mtxd", [128, 128], F32, okind) if debug else None
    gmd = drs("gmd", [128, 128], BF16, okind) if debug else None
    dxtd = drs("dxtd", [128, 64], BF16, okind) if debug else None
    b2td = drs("b2td", [128, 128], BF16, okind) if debug else None
    argd = drs("argd", [128, 128], F32, okind) if debug else None
    sfd2 = drs("sfd2", [1, 4128], F32, okind) if debug else None
    trad = drs("trad", [128, 128], F32, okind) if debug else None
    psd = drs("psd", [128, 128], F32, okind) if debug else None
    idd = drs("idd", [128, 128], F32, okind) if debug else None
    umd = drs("umd", [128, 128], F32, okind) if debug else None
    dtd = drs("dtd", [32, 512], F32, okind) if debug else None
    sfd = drs("sfd", [1, 4128], F32, okind) if debug else None
    xcd = drs("xcd", [128, 8192], BF16, okind) if debug else None
    hx = nc.dram_tensor("hx", [260, 2048], BF16, addr_space="Shared").ap()
    hxf = nc.dram_tensor("hxf", [4, 2], U32, addr_space="Shared").ap()

    stk = ExitStack()
    SB = lambda n, sh, dt: stk.enter_context(nc.sbuf_tensor(n, sh, dt))
    s_ident = SB("s_ident", [128, 128], BF16)
    s_identf = SB("s_identf", [128, 128], F32)
    s_umask = SB("s_umask", [128, 128], BF16)
    s_ones = SB("s_ones", [1, 128], F32)
    s_onec = SB("s_onec", [128, 1], F32)
    s_onecb = SB("s_onecb", [128, 1], BF16)
    s_nones = SB("s_nones", [1, 128], F32)
    s_consts = SB("s_consts", [128, 8], F32)
    s_alog = SB("s_alog", [32, 1], F32)
    s_A = SB("s_A", [32, 1], F32)
    s_dtbv = SB("s_dtbv", [32, 1], F32)
    s_sg = SB("s_sg", [128, 64], F32)
    s_su = SB("s_su", [128, 64], F32)
    s_sd = SB("s_sd", [128, 16], F32)
    s_sT = SB("s_sT", [128, 1024], F32)
    s_h = SB("s_h", [128, 2048], F32)
    s_hb = SB("s_hb", [128, 2048], BF16)
    s_sbuf = SB("s_sbuf", [32, 514], F32)
    SCR = SB("SCR", [128, 4160], F32)
    s_sflat = SCR[0:1, 0:4128]
    s_dtT = SB("s_dtT", [128, 128], F32)
    s_dCT = SB("s_dCT", [128, 128], F32)
    s_expsT = SB("s_expsT", [128, 128], F32)
    s_drep = SB("s_drep", [128, 128], F32)
    s_halo = SB("s_halo", [128, 240], BF16)
    s_xh3 = SB("s_xh3", [128, 64], BF16)
    s_x3f = SB("s_x3f", [128, 64], F32)
    s_xidx = SB("s_xidx", [128, 1], I32)
    s_fidx = SB("s_fidx", [2, 1], I32)
    s_nonce = SB("s_nonce", [2, 1], U32)
    s_hmask = SB("s_hmask", [128, 1], F32)
    s_flag = SB("s_flag", [1, 1], U32)
    s_row = SB("s_row", [1, 512], F32)
    s_row2 = SB("s_row2", [1, 512], F32)
    s_rowt = SB("s_rowt", [1, 512], F32)
    s_rms = SB("s_rms", [128, 32], F32)
    DT3 = SB("DT3", [128, 1536], F32)
    s_dt = DT3[0:32, 0:512]
    s_dt2 = DT3[0:32, 512:1024]
    s_dcr = DT3[0:32, 1024:1536]
    s_trA = SB("s_trA", [128, 128], BF16)
    s_trB = SB("s_trB", [128, 128], BF16)
    s_yis = SB("s_yis", [128, 512], BF16)
    s_otile1 = SB("s_otl1", [128, 512], F32)
    s_sq0 = SB("s_sq0", [128, 512], BF16)
    s_sq = [s_sq0, s_sq0]
    s_gm = [SB(f"s_gm{i}", [128, 128], BF16) for i in range(2)]
    s_mtx = [SB(f"s_mtx{i}", [128, 128], F32) for i in range(2)]
    s_mtm = [SB(f"s_mtm{i}", [128, 128], BF16) for i in range(2)]
    s_dxt = [SB(f"s_dxt{i}", [128, 64], BF16) for i in range(2)]
    s_b2t = [SB(f"s_b2t{i}", [128, 128], BF16) for i in range(2)]
    A1 = SB("A1", [128, 16 * TC], BF16)
    A2 = SB("A2", [128, 16 * TC], BF16)
    A3 = SB("A3", [128, 32 * TC], BF16)
    A4 = SB("A4", [128, 32 * TC], BF16)
    A5 = SB("A5", [128, 16 * TC], BF16)
    A6 = SB("A6", [128, 2048], BF16)
    A7 = [SB(f"A7_{i}", [128, 2048], BF16) for i in range(2)]
    A8 = [SB(f"A8_{i}", [128, 2048], BF16) for i in range(2)]
    A9 = SB("A9", [128, 4096], BF16)
    s_pc = [SB(f"s_pc{i}", [128, 4 + TC], BF16) for i in range(2)]
    s_hin = A6
    s_xf = [SCR[:, 3072:3584], SCR[:, 3584:4096]]
    s_otile = [DT3[:, 1024:1536], s_otile1]
    s_gt = s_pc[0][:, 0:512]

    pb = [stk.enter_context(nc.psum_tensor(f"pb{i}", [128, 512], F32)) for i in range(6)]
    pbf = [stk.enter_context(nc.psum_tensor(f"pbf{i}", [128, 1024], BF16)) for i in range(2)]

    sp_spin = stk.enter_context(nc.semaphore("sp_spin"))

    sems = {}
    hw = {}
    ops = {"pe": [], "dv": [], "ac": [], "sy": [], "gp": []}
    seqc = [0]
    ENGSEM = {"pe": "Epe", "dv": "Edv", "ac": "Eac"}

    def sem(name):
        if name not in sems:
            sems[name] = [stk.enter_context(nc.semaphore(name)), 0]
        return sems[name]

    for s in ENGSEM.values():
        sem(s)

    class Buf:
        __slots__ = ("w", "r")
        def __init__(self):
            self.w = None
            self.r = []
    bufs = {}

    def bget(name):
        if name not in bufs:
            bufs[name] = Buf()
        return bufs[name]

    def E(eng, reads=(), writes=(), deps=(), dma=None):
        def deco(fn):
            dd = list(deps)
            for r in reads:
                b = bget(r)
                if b.w is not None:
                    dd.append(b.w)
            for w in writes:
                b = bget(w)
                if b.w is not None:
                    dd.append(b.w)
                dd.extend(b.r)
            if dma is None:
                s = sem(ENGSEM[eng])
                s[1] += 1
                tok = (ENGSEM[eng], s[1])
            else:
                s = sem(dma)
                s[1] += 16
                tok = (dma, s[1])
            seqc[0] += 1
            ops[eng].append((fn, tuple(set(dd)), tok, seqc[0]))
            for r in reads:
                bget(r).r.append(tok)
            for w in writes:
                b = bget(w)
                b.w = tok
                b.r = []
            return tok
        return deco

    def all_tokens():
        return [(n, s[1]) for n, s in sems.items() if s[1] > 0]

    def barrier():
        toks = all_tokens()
        for eng in ops:
            if eng in ENGSEM:
                seqc[0] += 1
                ops[eng].append(((lambda e: e.nop()), tuple(toks), None, seqc[0]))
            else:
                seqc[0] += 1
                ops[eng].append(((lambda e: None), tuple(toks), None, seqc[0]))

    def replay(eng_obj, eng):
        for fn, dd, tok, sq in ops[eng]:
            if trunc is not None and sq > trunc:
                continue
            best = {}
            for sname, val in dd:
                if best.get(sname, 0) < val:
                    best[sname] = val
            for sname, val in sorted(best.items()):
                if eng == "pe" and sname == "Epe":
                    continue
                key = (eng, sname)
                if hw.get(key, 0) >= val:
                    continue
                eng_obj.wait_ge(sems[sname][0], val)
                hw[key] = val
            inst = fn(eng_obj)
            if inst is not None and tok is not None:
                inst.then_inc(sems[tok[0]][0], 16 if tok[0].startswith("d_") else 1)

    CL = lambda p=1: s_consts[0:p, 0:1]
    CR = lambda p=1: s_consts[0:p, 1:2]
    CH_ = lambda p=1: s_consts[0:p, 2:3]
    CD = lambda p=1: s_consts[0:p, 3:4]
    C64 = lambda p=1: s_consts[0:p, 4:5]
    C1 = lambda p=1: s_consts[0:p, 5:6]

    small = [(s_ident, ident_bf, "c_id"), (s_identf, ident_f, "c_idf"),
             (s_umask, umask_i, "c_um"), (s_ones, onesr_i, "c_on"),
             (s_nones, negonesr_i, "c_no"), (s_onec, onec_i, "c_oc"), (s_onecb, onecb_i, "c_ob"), (s_consts, consts_i, "c_ct"),
             (s_alog, alog, "c_al"), (s_dtbv, dtb, "c_db"),
             (s_sg, sg_i, "c_sg"), (s_su, su_i, "c_su"),
             (s_sd, sd_i, "c_sd"), (s_xidx, xidx_i, "c_xi"),
             (s_fidx, fidx_i, "c_fi"), (s_nonce, nonce_i, "c_nn"),
             (s_hmask, hmask_i, "c_hm")]
    for dst, src, nm in small:
        @E("sy", writes=(nm,), dma="d_small")
        def _(e, dst=dst, src=src):
            return e.dma_start(dst[:], src[:])
    for _, _, nm in small:
        bget(nm).w = ("d_small", sems["d_small"][1])
    SMALLT = all_tokens()

    @E("gp", writes=("xb",), dma="d_xb")
    def _(e):
        return e.dma_start(
            A1[:].rearrange("p (k t) -> p k t", t=TC),
            xT[:, 0:TC].rearrange("(k p) t -> p k t", p=128))

    @E("gp", writes=("x3f",), dma="d_x3")
    def _(e):
        return e.dma_start(
            s_x3f[:].rearrange("p (k f) -> p k f", f=4),
            xh3.rearrange("(k p) f -> p k f", p=128))

    def cast_stream(name, src, dst, nrows, pieces):
        wn = []
        rp = nrows // pieces
        for i in range(pieces):
            @E("gp", writes=(f"{name}cast{i}",), dma="d_" + name)
            def _(e, i=i):
                return e.dma_start(dst[i * rp:(i + 1) * rp, :], src[i * rp:(i + 1) * rp, :])
            wn.append(f"{name}cast{i}")
        for n_ in wn:
            bget(n_).w = ("d_" + name, sems["d_" + name][1])
        return tuple(wn)

    def cast_rows(name, src, dst, ranges):
        wn = []
        for i, (r0, r1) in enumerate(ranges):
            @E("gp", writes=(f"{name}cast{i}",), dma="d_" + name)
            def _(e, r0=r0, r1=r1):
                return e.dma_start(dst[r0:r1, :], src[r0:r1, :])
            wn.append(f"{name}cast{i}")
        for n_ in wn:
            bget(n_).w = ("d_" + name, sems["d_" + name][1])
        return tuple(wn)

    WINGRP = {}
    CVGRP = {}
    prev_tok = [()]

    def chain_cast(name, src, dst, ranges):
        toks = cast_rows_dep(name, src, dst, ranges, prev_tok[0])
        prev_tok[0] = toks
        return toks

    def cast_rows_dep(name, src, dst, ranges, dep):
        wn = []
        for i, (r0, r1) in enumerate(ranges):
            @E("gp", reads=dep if i == 0 else (), writes=(f"{name}cast{i}",), dma="d_" + name)
            def _(e, r0=r0, r1=r1):
                return e.dma_start(dst[r0:r1, :], src[r0:r1, :])
            wn.append(f"{name}cast{i}")
        for n_ in wn:
            bget(n_).w = ("d_" + name, sems["d_" + name][1])
        return tuple(wn)

    wa = chain_cast("wA", win_t, winb, [(96 * 128, 97 * 128), (16 * 128, 32 * 128)])
    for ci in [96] + list(range(16, 32)):
        WINGRP[ci] = wa
    cvt = chain_cast("cv0", conv_t, convb, [(0, 20 * 128)])
    for gi in range(5):
        CVGRP[gi] = cvt
    wb_ = chain_cast("wB", win_t, winb, [(0, 16 * 128)])
    for ci in range(16):
        WINGRP[ci] = wb_
    cvt = chain_cast("cv1", conv_t, convb, [(20 * 128, 40 * 128)])
    for gi in range(5, 10):
        CVGRP[gi] = cvt
    wc = chain_cast("wC0", win_t, winb, [(32 * 128, 48 * 128)])
    for ci in range(32, 48):
        WINGRP[ci] = wc
    cvt = chain_cast("cv2", conv_t, convb, [(40 * 128, 60 * 128)])
    for gi in range(10, 15):
        CVGRP[gi] = cvt
    wc = chain_cast("wC1", win_t, winb, [(48 * 128, 64 * 128)])
    for ci in range(48, 64):
        WINGRP[ci] = wc
    cvt = chain_cast("cv3", conv_t, convb, [(60 * 128, 80 * 128)])
    for gi in range(15, 20):
        CVGRP[gi] = cvt
    wc = chain_cast("wC2", win_t, winb, [(64 * 128, 80 * 128)])
    for ci in range(64, 80):
        WINGRP[ci] = wc
    wc = chain_cast("wC3", win_t, winb, [(80 * 128, 96 * 128)])
    for ci in range(80, 96):
        WINGRP[ci] = wc
    WOUTC = chain_cast("wout", wout_t, woutb, [(0, 16 * 128)])
    WGC = chain_cast("wg", wg_t, wgb, [(i * 16 * 128, (i + 1) * 16 * 128) for i in range(4)])
    WUC = chain_cast("wu", wu_t, wub, [(i * 16 * 128, (i + 1) * 16 * 128) for i in range(4)])
    WDC = chain_cast("wd", wd_t, wdb, [(i * 4 * 128, (i + 1) * 4 * 128) for i in range(4)])

    @E("ac", reads=("c_al",), writes=("A",), deps=SMALLT)
    def _(e):
        return e.activation(s_A[0:32, :], s_alog[0:32, :], AF.Exp)

    @E("dv", reads=("A",), writes=("A",))
    def _(e):
        return e.tensor_scalar(s_A[0:32, :], s_A[0:32, :], -1.0, None, ALU.mult)

    @E("dv", writes=("h",))
    def _(e):
        return e.memset(s_h[:], 0.0)

    @E("dv", writes=("hb",))
    def _(e):
        return e.memset(s_hb[:], 0.0)

    @E("dv", writes=("halo",))
    def _(e):
        return e.memset(s_halo[:], 0.0)

    @E("dv", writes=("s",))
    def _(e):
        return e.memset(s_sbuf[:, 0:1], 0.0)

    @E("dv", writes=("trA",))
    def _(e):
        return e.memset(s_trA[:], 0.0)

    @E("dv", writes=("trB",))
    def _(e):
        return e.memset(s_trB[:], 0.0)

    for k in range(16):
        @E("pe", reads=("x3f", "c_oc"), writes=("bk4",), deps=SMALLT)
        def _(e, k=k):
            return e.matmul(pb[4][0:1, 0:4], s_onec[:, 0:1], s_x3f[:, k * 4:(k + 1) * 4],
                            start=(k == 0), stop=(k == 15))

    @E("ac", reads=("x3f",), writes=("x3n",))
    def _(e):
        return e.square(s_xh3[:, :], s_x3f[:, :])

    for k in range(16):
        @E("pe", reads=("x3n", "c_ob"), writes=("bk5",), deps=SMALLT)
        def _(e, k=k):
            return e.matmul(pb[5][0:1, 0:4], s_onecb[:, 0:1], s_xh3[:, k * 4:(k + 1) * 4],
                            start=(k == 0), stop=(k == 15))

    @E("dv", reads=("bk4", "c_ct"), writes=("rowA",))
    def _(e):
        return e.tensor_scalar(s_row[0:1, 0:4], pb[4][0:1, 0:4], CD(), None, ALU.mult)

    @E("dv", reads=("bk5", "c_ct"), writes=("rowB",))
    def _(e):
        return e.tensor_scalar(s_row2[0:1, 0:4], pb[5][0:1, 0:4], CD(), None, ALU.mult)

    @E("dv", reads=("rowA",), writes=("rowT",))
    def _(e):
        return e.tensor_tensor(s_rowt[0:1, 0:4], s_row[0:1, 0:4], s_row[0:1, 0:4], ALU.mult)

    @E("dv", reads=("rowT", "rowB"), writes=("rowB",))
    def _(e):
        return e.tensor_tensor(s_row2[0:1, 0:4], s_row2[0:1, 0:4], s_rowt[0:1, 0:4], ALU.subtract)

    @E("ac", reads=("rowB", "c_ct"), writes=("rowB",))
    def _(e):
        return e.activation(s_row2[0:1, 0:4], s_row2[0:1, 0:4], AF.Ln, bias=CL())

    @E("ac", reads=("rowB", "c_ct"), writes=("rowB",))
    def _(e):
        return e.activation(s_row2[0:1, 0:4], s_row2[0:1, 0:4], AF.Exp, scale=CH_())

    @E("dv", reads=("rowA", "rowB"), writes=("rowA",))
    def _(e):
        return e.tensor_tensor(s_row[0:1, 0:4], s_row[0:1, 0:4], s_row2[0:1, 0:4], ALU.mult)

    @E("dv", reads=("rowA",), writes=("rowA",))
    def _(e):
        return e.tensor_scalar(s_row[0:1, 0:4], s_row[0:1, 0:4], -1.0, None, ALU.mult)

    @E("pe", reads=("rowB", "c_on"), writes=("bk4",))
    def _(e):
        return e.matmul(pb[4][:, 0:4], s_ones[0:1, :], s_row2[0:1, 0:4], start=True, stop=True)

    @E("pe", reads=("rowA", "c_on"), writes=("bk5",))
    def _(e):
        return e.matmul(pb[5][:, 4:8], s_ones[0:1, :], s_row[0:1, 0:4], start=True, stop=True)

    for k in range(16):
        @E("dv", reads=("x3f", "bk4", "x3n"), writes=("x3n",))
        def _(e, k=k):
            return e.tensor_tensor(s_xh3[:, k * 4:(k + 1) * 4], s_x3f[:, k * 4:(k + 1) * 4],
                                   pb[4][:, 0:4], ALU.mult)
    for k in range(16):
        @E("dv", reads=("bk5", "x3n"), writes=("x3n",))
        def _(e, k=k):
            return e.tensor_tensor(s_xh3[:, k * 4:(k + 1) * 4], s_xh3[:, k * 4:(k + 1) * 4],
                                   pb[5][:, 4:8], ALU.add)

    NWIN = 2
    NCV = 2

    def ln_stats_rows(src_name, src_tile_fn, n_tiles):
        for k in range(n_tiles):
            @E("pe", reads=(src_name, "c_ob"), writes=("bk4",), deps=SMALLT)
            def _(e, k=k):
                return e.matmul(pb[4][0:1, 0:512], s_onecb[:, 0:1], src_tile_fn(k),
                                start=(k == 0), stop=(k == n_tiles - 1))
        for k in range(n_tiles):
            @E("ac", reads=(src_name,), writes=("sq",))
            def _(e, k=k):
                return e.square(s_sq[k % 2][:, :], src_tile_fn(k))

            @E("pe", reads=("sq", "c_ob"), writes=("bk5",))
            def _(e, k=k):
                return e.matmul(pb[5][0:1, 0:512], s_onecb[:, 0:1], s_sq[k % 2][:, :],
                                start=(k == 0), stop=(k == n_tiles - 1))

        @E("dv", reads=("bk4", "c_ct"), writes=("rowA",))
        def _(e):
            return e.tensor_scalar(s_row[0:1, :], pb[4][0:1, 0:512], CD(), None, ALU.mult)

        @E("dv", reads=("bk5", "c_ct"), writes=("rowB",))
        def _(e):
            return e.tensor_scalar(s_row2[0:1, :], pb[5][0:1, 0:512], CD(), None, ALU.mult)

        @E("dv", reads=("rowA",), writes=("rowT",))
        def _(e):
            return e.tensor_tensor(s_rowt[0:1, :], s_row[0:1, :], s_row[0:1, :], ALU.mult)

        @E("dv", reads=("rowT", "rowB"), writes=("rowB",))
        def _(e):
            return e.tensor_tensor(s_row2[0:1, :], s_row2[0:1, :], s_rowt[0:1, :], ALU.subtract)

        @E("ac", reads=("rowB", "c_ct"), writes=("rowB",))
        def _(e):
            return e.activation(s_row2[0:1, :], s_row2[0:1, :], AF.Ln, bias=CL())

        @E("ac", reads=("rowB", "c_ct"), writes=("rowB",))
        def _(e):
            return e.activation(s_row2[0:1, :], s_row2[0:1, :], AF.Exp, scale=CH_())

        @E("dv", reads=("rowA", "rowB"), writes=("rowA",))
        def _(e):
            return e.tensor_tensor(s_row[0:1, :], s_row[0:1, :], s_row2[0:1, :], ALU.mult)

        @E("dv", reads=("rowA",), writes=("rowA",))
        def _(e):
            return e.tensor_scalar(s_row[0:1, :], s_row[0:1, :], -1.0, None, ALU.mult)

        @E("pe", reads=("rowB", "c_on"), writes=("bk4",))
        def _(e):
            return e.matmul(pb[4][:, 0:512], s_ones[0:1, :], s_row2[0:1, :], start=True, stop=True)

        @E("pe", reads=("rowA", "c_on"), writes=("bk5",))
        def _(e):
            return e.matmul(pb[5][:, 0:512], s_ones[0:1, :], s_row[0:1, :], start=True, stop=True)

    for ch in range(nchunk):
        c0 = ch * TC

        if ch > 0:
            @E("gp", writes=("xb",), dma="d_xb")
            def _(e, c0=c0):
                return e.dma_start(
                    A1[:].rearrange("p (k t) -> p k t", t=TC),
                    xT[:, c0:c0 + TC].rearrange("(k p) t -> p k t", p=128))

        ln_stats_rows("xb", lambda k: A1[:, k * TC:(k + 1) * TC], 16)
        for k in range(16):
            @E("dv", reads=("xb", "bk4"), writes=("xn",))
            def _(e, k=k):
                return e.tensor_tensor(A2[:, k * TC:(k + 1) * TC], A1[:, k * TC:(k + 1) * TC],
                                       pb[4][:, 0:512], ALU.mult)

            @E("dv", reads=("bk5",), writes=("xn",))
            def _(e, k=k):
                return e.tensor_tensor(A2[:, k * TC:(k + 1) * TC], A2[:, k * TC:(k + 1) * TC],
                                       pb[5][:, 0:512], ALU.add)

        if debug and ch == 0:
            @E("sy", reads=("xn",), writes=("xnd",), dma="d_dbg1")
            def _(e):
                return e.dma_start(xnd[:], A2[:])

        ci_order = [96] + list(range(16, 32)) + list(range(0, 16)) + list(range(32, 96))
        strips = [(ci,) for ci in ci_order]

        nb = 0
        cv_loaded = set()
        for si, pair in enumerate(strips):
            slot = si % NWIN

            @E("sy", reads=WINGRP[pair[0]], writes=(f"win{slot}",), dma=f"d_win{slot}")
            def _(e, pair=pair, slot=slot):
                return e.dma_start(A8[slot][:], winb[pair[0] * 128:(pair[0] + 1) * 128, :])

            for ci in pair:
                if ci == CI_DT or (CI_Z0 <= ci < CI_B0):
                    continue
                ti = ci if ci < 16 else ci - 16
                gi = ti // 4
                if gi not in cv_loaded:
                    cv_loaded.add(gi)
                    cslot = gi % NCV

                    @E("gp", reads=CVGRP[gi], writes=(f"cvl{cslot}",), dma=f"d_cvl{cslot}")
                    def _(e, gi=gi, cslot=cslot):
                        return e.dma_start(
                            A9[:, cslot * 2048:(cslot + 1) * 2048].rearrange("p (g m) -> p g m", g=4),
                            convb[gi * 4 * 128:(gi + 1) * 4 * 128, :]
                            .rearrange("(g p) m -> p g m", p=128))

            for j, ci in enumerate(pair):
                bank = nb % 3
                nb += 1
                for k in range(16):
                    @E("pe", reads=(f"win{slot}", "xn"), writes=(f"bk{bank}",))
                    def _(e, slot=slot, k=k, bank=bank):
                        return e.matmul(pb[bank][:, 0:512],
                                        A8[slot][:, k * 128:(k + 1) * 128],
                                        A2[:, k * TC:(k + 1) * TC],
                                        start=(k == 0), stop=(k == 15))
                if ch == 0 and (ci < 16 or CI_B0 <= ci < CI_DT):
                    ti = ci if ci < 16 else ci - 16
                    for k in range(16):
                        @E("pe", reads=(f"win{slot}", "x3n"), writes=("bk5",))
                        def _(e, slot=slot, k=k):
                            return e.matmul(pb[5][:, 8:12],
                                            A8[slot][:, k * 128:(k + 1) * 128],
                                            s_xh3[:, k * 4:(k + 1) * 4],
                                            start=(k == 0), stop=(k == 15))

                    @E("dv", reads=("bk5",), writes=("halo",))
                    def _(e, ti=ti):
                        return e.tensor_copy(s_halo[:, ti * 3:(ti + 1) * 3], pb[5][:, 8:11])

                if ci == CI_DT:
                    @E("dv", reads=(f"bk{bank}",), writes=("rdt",))
                    def _(e, bank=bank):
                        return e.tensor_scalar(s_dt[0:32, :], pb[bank][0:32, 0:512], 1.0, None, ALU.mult)
                elif CI_Z0 <= ci < CI_B0:
                    zi = ci - CI_Z0
                    hbuf, zo = (zi // 2) % 2, zi % 2

                    @E("ac", reads=(f"bk{bank}",), writes=(f"zs{hbuf}",))
                    def _(e, bank=bank, hbuf=hbuf, zo=zo):
                        return e.activation(A6[:, hbuf * 1024 + zo * TC:hbuf * 1024 + (zo + 1) * TC],
                                            pb[bank][:, 0:512], AF.Silu)

                    if zi % 2 == 1:
                        zp = zi // 2

                        for sb_ in range(NSUB):
                            @E("gp", reads=(f"zs{hbuf}",), writes=(f"zsdw{hbuf}",), dma=f"d_zs{hbuf}")
                            def _(e, ch=ch, zp=zp, hbuf=hbuf, sb_=sb_):
                                rb_ = (ch * 4 + sb_) * 128
                                return e.dma_start(
                                    zsd[rb_:rb_ + 128, zp * 256:(zp + 1) * 256]
                                    .rearrange("p (zo t) -> p zo t", t=128),
                                    A6[:, hbuf * 1024:(hbuf + 1) * 1024]
                                    .rearrange("p (zo tt) -> p zo tt", tt=TC)[:, :, sb_ * 128:(sb_ + 1) * 128])
                else:
                    ti = ci if ci < 16 else ci - 16
                    pcs = ti % 2

                    @E("dv", reads=("halo",), writes=(f"pc{pcs}",))
                    def _(e, ti=ti, pcs=pcs):
                        return e.tensor_copy(s_pc[pcs][:, 0:3], s_halo[:, ti * 3:(ti + 1) * 3])

                    @E("ac", reads=(f"bk{bank}",), writes=(f"pc{pcs}",))
                    def _(e, bank=bank, pcs=pcs):
                        return e.copy(s_pc[pcs][:, 3:3 + TC], pb[bank][:, 0:512])

                    @E("dv", reads=(f"pc{pcs}",), writes=("halo",))
                    def _(e, ti=ti, pcs=pcs):
                        return e.tensor_copy(s_halo[:, ti * 3:(ti + 1) * 3], s_pc[pcs][:, TC:TC + 3])

                    cslot = (ti // 4) % NCV
                    for kk in range(KCV):
                        @E("pe", reads=(f"cvl{cslot}", f"pc{pcs}"), writes=("bk3",))
                        def _(e, ti=ti, kk=kk, pcs=pcs, cslot=cslot):
                            return e.matmul(
                                pb[3][:, 0:512],
                                A9[:, cslot * 2048 + (ti % 4) * 512 + kk * 128:
                                   cslot * 2048 + (ti % 4) * 512 + (kk + 1) * 128],
                                s_pc[pcs][:, kk:kk + TC],
                                start=(kk == 0), stop=(kk == KCV - 1))
                    if ci < 16:
                        dstt, dsti, wname = A5, ci, "xc"
                    elif ci < CI_C0:
                        dstt, dsti, wname = A3, ci - CI_B0, "bc"
                    else:
                        dstt, dsti, wname = A4, ci - CI_C0, "cc"

                    @E("ac", reads=("bk3",), writes=(wname,))
                    def _(e, dstt=dstt, dsti=dsti):
                        return e.activation(dstt[:, dsti * TC:(dsti + 1) * TC],
                                            pb[3][:, 0:512], AF.Silu)

        @E("dv", reads=("rdt", "c_db"), writes=("rdt",))
        def _(e):
            return e.tensor_scalar(s_dt[0:32, :], s_dt[0:32, :], s_dtbv[0:32, :], None, ALU.add)

        @E("dv", reads=("rdt",), writes=("rdt2",))
        def _(e):
            return e.tensor_scalar(s_dt2[0:32, :], s_dt[0:32, :], -1.0, None, ALU.mult)

        @E("dv", reads=("rdt", "rdt2"), writes=("rdt2",))
        def _(e):
            return e.tensor_tensor(s_dt2[0:32, :], s_dt[0:32, :], s_dt2[0:32, :], ALU.min)

        @E("ac", reads=("rdt2",), writes=("rdt2",))
        def _(e):
            return e.activation(s_dt2[0:32, :], s_dt2[0:32, :], AF.Exp)

        @E("ac", reads=("rdt2", "c_ct"), writes=("rdt2",))
        def _(e):
            return e.activation(s_dt2[0:32, :], s_dt2[0:32, :], AF.Ln, bias=C1(32))

        @E("dv", reads=("rdt",), writes=("rdt",))
        def _(e):
            return e.tensor_scalar(s_dt[0:32, :], s_dt[0:32, :], 0.0, None, ALU.max)

        @E("dv", reads=("rdt", "rdt2"), writes=("rdt",))
        def _(e):
            return e.tensor_tensor(s_dt[0:32, :], s_dt[0:32, :], s_dt2[0:32, :], ALU.add)

        @E("dv", reads=("rdt", "A"), writes=("rdt2",))
        def _(e):
            return e.tensor_scalar(s_dt2[0:32, :], s_dt[0:32, :], s_A[0:32, :], None, ALU.mult)

        @E("dv", reads=("rdt2", "s", "c_ct"), writes=("s",))
        def _(e, c0=c0):
            return e.tensor_tensor_scan(
                s_sbuf[0:32, 1:1 + TC],
                s_consts[0:32, 5:6].broadcast_to([32, TC]),
                s_dt2[0:32, :], 0.0, ALU.mult, ALU.add)

        @E("dv", reads=("s",), writes=("s",))
        def _(e, c0=c0):
            return e.tensor_scalar(
                s_sbuf[0:32, 1:1 + TC], s_sbuf[0:32, 1:1 + TC],
                s_sbuf[0:32, 0:1], None, ALU.add)

        for sb_ in range(NSUB):
            @E("dv", reads=("s",), writes=("rdcr",))
            def _(e, c0=c0, sb_=sb_):
                return e.tensor_scalar(
                    s_dcr[0:32, sb_ * 128:(sb_ + 1) * 128],
                    s_sbuf[0:32, 1 + sb_ * 128:1 + (sb_ + 1) * 128],
                    s_sbuf[0:32, sb_ * 128:sb_ * 128 + 1], None, ALU.subtract)

        @E("ac", reads=("rdcr",), writes=("rdcr",))
        def _(e):
            return e.activation(s_dcr[0:32, :], s_dcr[0:32, :], AF.Exp)

        for sb_ in range(NSUB):
            @E("ac", reads=("s",), writes=("trA",))
            def _(e, sb_=sb_):
                return e.copy(s_trA[0:32, :], s_sbuf[0:32, 1 + sb_ * 128:1 + (sb_ + 1) * 128])

            @E("dv", reads=("s", "trA"), writes=("trB",))
            def _(e, sb_=sb_):
                return e.tensor_tensor(s_trB[0:32, :],
                                       s_sbuf[0:32, 1 + sb_ * 128:1 + (sb_ + 1) * 128],
                                       s_trA[0:32, :], ALU.subtract)

            if debug and ch == 0 and sb_ == 3:
                @E("sy", reads=("trA",), writes=("tradw",), dma="d_trad")
                def _(e):
                    return e.dma_start(trad[:].bitcast(BF16)[:, 0:128], s_trA[:])

            @E("pe", reads=("trA", "c_id"), writes=("bkf0",))
            def _(e):
                return e.transpose(pbf[0][:, 0:128], s_trA[:], s_ident[:])

            @E("dv", reads=("bkf0",), writes=("sT",))
            def _(e, ch=ch, sb_=sb_):
                return e.tensor_copy(s_sT[:, (ch * 4 + sb_) * 32:(ch * 4 + sb_ + 1) * 32],
                                     pbf[0][:, 0:32])

            if debug and ch == 0 and sb_ == 3:
                @E("dv", reads=("bkf0",), writes=("psdw0",))
                def _(e):
                    return e.tensor_copy(s_mtx[0][:], pbf[0][:, 0:128])

                @E("sy", reads=("psdw0",), writes=("psdw",), dma="d_psd")
                def _(e):
                    return e.dma_start(psd[:], s_mtx[0][:])

                @E("sy", reads=("c_id",), writes=("iddw",), dma="d_idd")
                def _(e):
                    return e.dma_start(idd[:].bitcast(BF16)[:, 0:128], s_ident[:])

                @E("sy", reads=("c_um",), writes=("umdw",), dma="d_umd")
                def _(e):
                    return e.dma_start(umd[:].bitcast(BF16)[:, 0:128], s_umask[:])

            @E("pe", reads=("trB", "c_id"), writes=("bkf0",))
            def _(e):
                return e.transpose(pbf[0][:, 0:128], s_trB[:], s_ident[:])

            @E("dv", reads=("bkf0", "sT"), writes=("sT",))
            def _(e, ch=ch, sb_=sb_):
                return e.tensor_tensor(s_sT[:, (ch * 4 + sb_) * 32:(ch * 4 + sb_ + 1) * 32],
                                       s_sT[:, (ch * 4 + sb_) * 32:(ch * 4 + sb_ + 1) * 32],
                                       pbf[0][:, 0:32], ALU.add)

            @E("ac", reads=("rdt",), writes=("trA",))
            def _(e, sb_=sb_):
                return e.copy(s_trA[0:32, :], s_dt[0:32, sb_ * 128:(sb_ + 1) * 128])

            @E("pe", reads=("trA", "c_id"), writes=("bkf0",))
            def _(e):
                return e.transpose(pbf[0][:, 0:128], s_trA[:], s_ident[:])

            @E("dv", reads=("bkf0",), writes=("dtT",))
            def _(e, sb_=sb_):
                return e.tensor_copy(s_dtT[:, sb_ * 32:(sb_ + 1) * 32], pbf[0][:, 0:32])

            @E("ac", reads=("rdcr",), writes=("trB",))
            def _(e, sb_=sb_):
                return e.copy(s_trB[0:32, :], s_dcr[0:32, sb_ * 128:(sb_ + 1) * 128])

            @E("pe", reads=("trB", "c_id"), writes=("bkf0",))
            def _(e):
                return e.transpose(pbf[0][:, 0:128], s_trB[:], s_ident[:])

            @E("dv", reads=("bkf0",), writes=("dCT",))
            def _(e, sb_=sb_):
                return e.tensor_copy(s_dCT[:, sb_ * 32:(sb_ + 1) * 32], pbf[0][:, 0:32])

        @E("sy", reads=("s",), writes=("sdram",), dma="d_sb")
        def _(e, c0=c0):
            return e.dma_start(sdram[:, 0:513], s_sbuf[0:32, 0:513])

        if debug and ch == 0:
            @E("sy", reads=("rdt",), writes=("dtd",), dma="d_dbg2")
            def _(e):
                return e.dma_start(dtd[:], s_dt[0:32, :])

            @E("sy", reads=("xc",), writes=("xcd",), dma="d_dbg4")
            def _(e):
                return e.dma_start(xcd[:], A5[:])

        @E("dv", reads=("s",), writes=("s",))
        def _(e):
            return e.tensor_copy(s_sbuf[0:32, 0:1], s_sbuf[0:32, 512:513])

        for sb_ in range(NSUB):
            toff = sb_ * 128
            gcol = (ch * 4 + sb_) * 32

            @E("sy", reads=("sdram",), writes=("sflat",), dma="d_sb2")
            def _(e, sb_=sb_):
                return e.dma_start(
                    s_sflat[0:1, 0:4128].rearrange("one (h t) -> one h t", t=129),
                    sdram[:, sb_ * 128:sb_ * 128 + 129].unsqueeze(0))

            @E("pe", reads=("sflat", "c_on", "c_no"), writes=("bk3",))
            def _(e):
                sf = s_sflat[0:1, :].rearrange("one (h t) -> one h t", t=129)
                e.matmul(pb[3][:, 0:32], s_ones[0:1, :],
                         sf[0:1, :, 128:129].rearrange("one h t -> one (h t)"),
                         start=True, stop=False)
                return e.matmul(pb[3][:, 0:32], s_nones[0:1, :],
                                sf[0:1, :, 0:1].rearrange("one h t -> one (h t)"),
                                start=False, stop=True)

            @E("ac", reads=("bk3",), writes=("drep",))
            def _(e, sb_=sb_):
                return e.activation(s_drep[:, sb_ * 32:(sb_ + 1) * 32], pb[3][:, 0:32], AF.Exp)

            if debug and ch == 0 and sb_ == 0:
                @E("sy", reads=("sflat",), writes=("sfd",), dma="d_dbg3")
                def _(e):
                    return e.dma_start(sfd[:], s_sflat[0:1, 0:4128])
            if debug and ch == 0 and sb_ == 3:
                @E("sy", reads=("sflat",), writes=("sfd2w",), dma="d_dbg6")
                def _(e):
                    return e.dma_start(sfd2[:], s_sflat[0:1, 0:4128])
            for grp in range(4):
                yb = 4 + (grp % 2)
                for hh in range(8):
                    h = grp * 8 + hh
                    m = h % 2
                    rb = m
                    gb = 2 + m

                    @E("pe", reads=("sflat", "c_on"), writes=(f"bk{rb}",))
                    def _(e, h=h, sb_=sb_, rb=rb):
                        sf = s_sflat[0:1, :].rearrange("one (h t) -> one h t", t=129)
                        return e.matmul(pb[rb][:, 0:128], s_ones[0:1, :],
                                        sf[0:1, h:h + 1, 1:129]
                                        .rearrange("one h t -> one (h t)"),
                                        start=True, stop=True)

                    @E("dv", reads=(f"bk{rb}", "sT"), writes=(f"mtx{m}",))
                    def _(e, h=h, gcol=gcol, m=m, rb=rb):
                        return e.tensor_scalar(s_mtx[m][:], pb[rb][:, 0:128],
                                               s_sT[:, gcol + h:gcol + h + 1],
                                               0.0, ALU.subtract, ALU.min)

                    if debug and ch == 0 and h == 31 and sb_ == 3:
                        @E("sy", reads=(f"mtx{m}",), writes=("argdw",), dma="d_arg")
                        def _(e, m=m):
                            return e.dma_start(argd[:], s_mtx[m][:])

                    @E("ac", reads=(f"mtx{m}",), writes=(f"mtx{m}",))
                    def _(e, m=m):
                        return e.activation(s_mtx[m][:], s_mtx[m][:], AF.Exp)

                    @E("dv", reads=(f"mtx{m}", "c_um"), writes=(f"mtm{m}",))
                    def _(e, m=m):
                        return e.tensor_tensor(s_mtm[m][:], s_mtx[m][:], s_umask[:], ALU.mult)

                    @E("pe", reads=("bc", "cc"), writes=(f"bk{gb}",))
                    def _(e, h=h, toff=toff, gb=gb):
                        return e.matmul(pb[gb][:, 0:128],
                                        A3[:, h * TC + toff:h * TC + toff + 128],
                                        A4[:, h * TC + toff:h * TC + toff + 128],
                                        start=True, stop=True)

                    @E("dv", reads=(f"bk{gb}", f"mtm{m}"), writes=(f"gm{m}",))
                    def _(e, m=m, gb=gb):
                        return e.tensor_tensor(s_gm[m][:], pb[gb][:, 0:128], s_mtm[m][:], ALU.mult)

                    @E("pe", reads=("bc", "c_id"), writes=("bkf0",))
                    def _(e, h=h, toff=toff):
                        return e.transpose(pbf[0][:, 0:128],
                                           A3[:, h * TC + toff:h * TC + toff + 128], s_ident[:])

                    @E("dv", reads=("bkf0", f"mtx{m}"), writes=(f"b2t{m}",))
                    def _(e, m=m):
                        return e.tensor_scalar(s_b2t[m][:], pbf[0][:, 0:128],
                                               s_mtx[m][:, 127:128], None, ALU.mult)

                    if m == 0:
                        @E("pe", reads=("xc", "c_id"), writes=("bkf1",))
                        def _(e, h=h, toff=toff):
                            return e.transpose(pbf[1][:, 0:128],
                                               A5[:, (h // 2) * TC + toff:(h // 2) * TC + toff + 128],
                                               s_ident[:])

                    @E("dv", reads=("bkf1", "dtT"), writes=(f"dxt{m}",))
                    def _(e, h=h, sb_=sb_, m=m):
                        return e.tensor_scalar(s_dxt[m][:, 0:64],
                                               pbf[1][:, (h % 2) * 64:(h % 2) * 64 + 64],
                                               s_dtT[:, sb_ * 32 + h:sb_ * 32 + h + 1], None, ALU.mult)

                    @E("pe", reads=(f"gm{m}", f"dxt{m}"), writes=(f"bk{yb}",))
                    def _(e, m=m, yb=yb, hh=hh):
                        return e.matmul(pb[yb][:, hh * 64:(hh + 1) * 64],
                                        s_gm[m][:], s_dxt[m][:, 0:64], start=True, stop=True)

                    @E("pe", reads=("cc", "hb"), writes=(f"bk{rb}",))
                    def _(e, h=h, toff=toff, rb=rb):
                        return e.matmul(pb[rb][:, 128:192],
                                        A4[:, h * TC + toff:h * TC + toff + 128],
                                        s_hb[:, h * 64:(h + 1) * 64], start=True, stop=True)

                    @E("dv", reads=(f"bk{rb}", "dCT"), writes=("yis",))
                    def _(e, h=h, sb_=sb_, hh=hh, rb=rb):
                        return e.tensor_scalar(s_yis[:, hh * 64:(hh + 1) * 64], pb[rb][:, 128:192],
                                               s_dCT[:, sb_ * 32 + h:sb_ * 32 + h + 1], None, ALU.mult)

                    @E("pe", reads=(f"b2t{m}", f"dxt{m}"), writes=(f"bk{gb}",))
                    def _(e, m=m, gb=gb):
                        return e.matmul(pb[gb][:, 128:192], s_b2t[m][:], s_dxt[m][:, 0:64],
                                        start=True, stop=True)

                    @E("dv", reads=(f"bk{gb}", "drep", "h"), writes=("h",))
                    def _(e, h=h, sb_=sb_, gb=gb):
                        return e.scalar_tensor_tensor(
                            s_h[:, h * 64:(h + 1) * 64], s_h[:, h * 64:(h + 1) * 64],
                            s_drep[:, sb_ * 32 + h:sb_ * 32 + h + 1], pb[gb][:, 128:192],
                            ALU.mult, ALU.add)

                @E("dv", reads=(f"bk{yb}", "yis"), writes=(f"yp{sb_ % 2}",))
                def _(e, grp=grp, yb=yb, sb_=sb_):
                    return e.tensor_tensor(A7[sb_ % 2][:, grp * 512:(grp + 1) * 512],
                                           pb[yb][:, 0:512], s_yis[:], ALU.add)

            @E("ac", reads=("h",), writes=("hb",))
            def _(e):
                return e.copy(s_hb[:], s_h[:])

            @E("gp", reads=(f"yp{sb_ % 2}",), writes=(f"ypdw{sb_ % 2}",), dma=f"d_yp{sb_ % 2}")
            def _(e, c0=c0, sb_=sb_):
                return e.dma_start(ypd[c0 + sb_ * 128:c0 + (sb_ + 1) * 128, :], A7[sb_ % 2][:])

        if debug and ch == 0:
            for nmsb, dst, srcap in [("drep", drepd, s_drep[:]), ("dCT", dctd, s_dCT[:]),
                                     ("dtT", dttd, s_dtT[:]), ("sT", sTd, s_sT[:, 0:128]),
                                     ("mtx1", mtxd, s_mtx[1][:]), ("gm1", gmd, s_gm[1][:]),
                                     ("dxt1", dxtd, s_dxt[1][:]), ("b2t1", b2td, s_b2t[1][:])]:
                @E("sy", reads=(nmsb,), writes=(f"dbg_{nmsb}",), dma="d_dbg5")
                def _(e, dst=dst, srcap=srcap):
                    return e.dma_start(dst[:] if False else dst[0:srcap.shape[0], 0:srcap.shape[1]], srcap)

        for sb_ in range(NSUB):
            @E("gp", reads=("cc",), writes=("ccdw",), dma="d_ccw")
            def _(e, ch=ch, sb_=sb_):
                rb_ = (ch * 4 + sb_) * 128
                return e.dma_start(
                    ccd[rb_:rb_ + 128, :].rearrange("p (h t) -> p h t", t=128),
                    A4[:].rearrange("p (h tt) -> p h tt", tt=TC)[:, :, sb_ * 128:(sb_ + 1) * 128])


    barrier()
    if debug:
        @E("sy", reads=("h",), writes=("hdbgw",), dma="d_hdbg")
        def _(e):
            return e.dma_start(hdbg[:], s_h[:])

    @E("gp", reads=("hb", "c_xi"), writes=("hxw",), dma="d_hxw")
    def _(e):
        return e.indirect_dma_start(
            hx[0:260, :], bass.IndirectOffsetOnAxis(s_xidx[:, 0:1], 0),
            s_hb[:], None)

    @E("gp", reads=("hxw", "c_fi", "c_nn"), writes=("hxfw",), dma="d_hxf")
    def _(e):
        return e.indirect_dma_start(
            hxf.rearrange("r (c one) -> (r c) one", one=1)[0:8, 0:1],
            bass.IndirectOffsetOnAxis(s_fidx[:, 0:1], 0),
            s_nonce[:], None)

    @E("gp", reads=("hxfw", "c_nn"), writes=("hin",), dma="d_hxr")
    def _(e):
        with e.register("spin") as spin, e.register("rf") as rf, e.register("rn") as rn:
            e.reg_load(rn, s_nonce[0:1, 0:1])
            e.reg_mov(spin, 1)
            with ExitStack() as nest:
                for it in range(N_POLL):
                    nest.enter_context(e.If(spin))
                    e.dma_start(s_flag[:], hxf[0:1, 0:1]).then_inc(sp_spin, 16)
                    e.wait_ge(sp_spin, 16 * (it + 1))
                    e.reg_load(rf, s_flag[0:1, 0:1])
                    e.reg_sub(spin, rf, rn)
            with e.While(spin):
                e.nop()
        return e.dma_start(s_hin[:], hx[0:128, :])

    @E("dv", reads=("hin", "c_hm"), writes=("hin",))
    def _(e):
        return e.tensor_scalar(s_hin[:], s_hin[:], s_hmask[:, 0:1], None, ALU.mult)

    if debug:
        @E("gp", reads=("hin",), writes=("flgdw",), dma="d_flgd")
        def _(e):
            e.dma_start(flgd[0:1, 0:2], hxf[0:1, 0:2]).then_inc(sems["d_flgd"][0], 16)
            return e.dma_start(flgd[0:1, 2:3], s_nonce[0:1, 0:1])

    MO = A1
    X2 = A2
    YG = A5
    FU = lambda k: (A3[:, k * TC:(k + 1) * TC] if k < 32 else A4[:, (k - 32) * TC:(k - 31) * TC])
    YF = A7[0]
    YQ = A7[1]
    WD2 = lambda i: A8[i][:]
    WO2 = [SCR[:, 0:1024].bitcast(BF16), s_hb[:, 0:2048]]
    WG2 = [SCR[:, 1024:2048].bitcast(BF16), s_h[:, 0:1024].bitcast(BF16)]
    WU2 = [SCR[:, 2048:3072].bitcast(BF16), s_h[:, 1024:2048].bitcast(BF16)]
    CC2 = A9
    ZS2 = DT3[:, 0:1024].bitcast(BF16)

    pending_out = [None]
    for ch in range(nchunk):
        c0 = ch * TC

        @E("ac", reads=("sT",), writes=("expsT",))
        def _(e, ch=ch):
            return e.activation(s_expsT[:], s_sT[:, ch * 128:(ch + 1) * 128], AF.Exp)

        for sb_ in range(NSUB):
            gcol = sb_ * 32
            if pending_out[0] is not None:
                for di_ in range(sb_ * 4, (sb_ + 1) * 4):
                    pending_out[0](di_)
                if sb_ == NSUB - 1:
                    pending_out[0] = None

            @E("gp", writes=("yq",), dma="d_yp2")
            def _(e, c0=c0, sb_=sb_):
                return e.dma_start(YQ[:], ypd[c0 + sb_ * 128:c0 + (sb_ + 1) * 128, :])

            @E("gp", writes=("cc2",), dma="d_cc2")
            def _(e, ch=ch, sb_=sb_):
                rb_ = (ch * 4 + sb_) * 128
                return e.dma_start(CC2[:], ccd[rb_:rb_ + 128, :])

            @E("gp", writes=("zs2",), dma="d_zs2")
            def _(e, ch=ch, sb_=sb_):
                rb_ = (ch * 4 + sb_) * 128
                return e.dma_start(ZS2[:, :], zsd[rb_:rb_ + 128, :])

            for h in range(32):
                m = h % 2

                @E("pe", reads=("cc2", "hin"), writes=(f"bk{m}",))
                def _(e, h=h, m=m):
                    return e.matmul(pb[m][:, 0:64], CC2[:, h * 128:(h + 1) * 128],
                                    s_hin[:, h * 64:(h + 1) * 64], start=True, stop=True)

                @E("dv", reads=(f"bk{m}", "expsT", "yq"), writes=("yf",))
                def _(e, h=h, m=m, gcol=gcol):
                    return e.scalar_tensor_tensor(
                        YF[:, h * 64:(h + 1) * 64], pb[m][:, 0:64],
                        s_expsT[:, gcol + h:gcol + h + 1],
                        YQ[:, h * 64:(h + 1) * 64], ALU.mult, ALU.add)

            @E("ac", reads=("yf",), writes=("yq",))
            def _(e):
                return e.square(YQ[:], YF[:])

            @E("dv", reads=("yq",), writes=("rms",))
            def _(e):
                return e.reduce_sum(s_rms[:, :], YQ[:].rearrange("p (h q) -> p h q", q=64), axis=AX.X)

            @E("ac", reads=("rms", "c_ct"), writes=("rms",))
            def _(e):
                return e.activation(s_rms[:, :], s_rms[:, :], AF.Ln, scale=C64(128), bias=CR(128))

            @E("ac", reads=("rms", "c_ct"), writes=("rms",))
            def _(e):
                return e.activation(s_rms[:, :], s_rms[:, :], AF.Exp, scale=CH_(128))

            @E("dv", reads=("rms", "yf"), writes=("yf",))
            def _(e):
                return e.tensor_tensor(
                    YF[:].rearrange("p (h q) -> p h q", q=64),
                    YF[:].rearrange("p (h q) -> p h q", q=64),
                    s_rms[:, :].unsqueeze(2).broadcast_to([128, 32, 64]), ALU.mult)

            for pr in range(16):
                fb = pr % 2

                @E("pe", reads=("yf", "c_id"), writes=(f"bkf{fb}",))
                def _(e, pr=pr, fb=fb):
                    return e.transpose(pbf[fb][:, 0:128], YF[:, pr * 128:(pr + 1) * 128], s_ident[:])

                @E("dv", reads=(f"bkf{fb}", "zs2"), writes=("yg",))
                def _(e, pr=pr, fb=fb, sb_=sb_):
                    return e.tensor_tensor(YG[:, pr * TC + sb_ * 128:pr * TC + (sb_ + 1) * 128],
                                           pbf[fb][:, 0:128],
                                           ZS2[:, pr * 128:(pr + 1) * 128], ALU.mult)

        for di in range(16):
            ws = di % 2

            @E("sy", reads=WOUTC + ("hxw",), writes=(f"wo{ws}",), dma=f"d_wo{ws}")
            def _(e, di=di, ws=ws):
                return e.dma_start(WO2[ws][:, :], woutb[di * 128:(di + 1) * 128, :])

            bank = di % 3
            for k in range(16):
                @E("pe", reads=(f"wo{ws}", "yg"), writes=(f"bk{bank}",))
                def _(e, di=di, k=k, bank=bank, ws=ws):
                    return e.matmul(pb[bank][:, 0:512], WO2[ws][:, k * 128:(k + 1) * 128],
                                    YG[:, k * TC:(k + 1) * TC], start=(k == 0), stop=(k == 15))

            @E("ac", reads=(f"bk{bank}",), writes=("mo",))
            def _(e, di=di, bank=bank):
                return e.copy(MO[:, di * TC:(di + 1) * TC], pb[bank][:, 0:512])

        ln_stats_rows("mo", lambda k: MO[:, k * TC:(k + 1) * TC], 16)
        for di in range(16):
            xs = di % 2

            @E("sy", writes=(f"xf{xs}",), dma=f"d_xf{xs}")
            def _(e, c0=c0, di=di, xs=xs):
                return e.dma_start(s_xf[xs][:], xT[di * 128:(di + 1) * 128, c0:c0 + TC])

            @E("dv", reads=("mo", "bk4"), writes=("x2",))
            def _(e, di=di):
                return e.tensor_tensor(X2[:, di * TC:(di + 1) * TC], MO[:, di * TC:(di + 1) * TC],
                                       pb[4][:, 0:512], ALU.mult)

            @E("dv", reads=("bk5",), writes=("x2",))
            def _(e, di=di):
                return e.tensor_tensor(X2[:, di * TC:(di + 1) * TC], X2[:, di * TC:(di + 1) * TC],
                                       pb[5][:, 0:512], ALU.add)

            @E("dv", reads=(f"xf{xs}",), writes=("x2",))
            def _(e, di=di, xs=xs):
                return e.tensor_tensor(X2[:, di * TC:(di + 1) * TC], X2[:, di * TC:(di + 1) * TC],
                                       s_xf[xs][:], ALU.add)

        ln_stats_rows("x2", lambda k: X2[:, k * TC:(k + 1) * TC], 16)
        for k in range(16):
            @E("dv", reads=("x2", "bk4"), writes=("mo",))
            def _(e, k=k):
                return e.tensor_tensor(MO[:, k * TC:(k + 1) * TC], X2[:, k * TC:(k + 1) * TC],
                                       pb[4][:, 0:512], ALU.mult)

            @E("dv", reads=("bk5",), writes=("mo",))
            def _(e, k=k):
                return e.tensor_tensor(MO[:, k * TC:(k + 1) * TC], MO[:, k * TC:(k + 1) * TC],
                                       pb[5][:, 0:512], ALU.add)

        for ii in range(64):
            gb = ii % 2
            ub = 2 + (ii % 2)

            ws = ii % 2

            @E("sy", reads=WGC + ("hxw",), writes=(f"wg2{ws}",), dma=f"d_wg2{ws}")
            def _(e, ii=ii, ws=ws):
                return e.dma_start(WG2[ws][:, :], wgb[ii * 128:(ii + 1) * 128, :])

            @E("sy", reads=WUC + ("hxw",), writes=(f"wu2{ws}",), dma=f"d_wu2{ws}")
            def _(e, ii=ii, ws=ws):
                return e.dma_start(WU2[ws][:, :], wub[ii * 128:(ii + 1) * 128, :])

            for k in range(16):
                @E("pe", reads=(f"wg2{ws}", "mo"), writes=(f"bk{gb}",))
                def _(e, k=k, gb=gb, ws=ws):
                    return e.matmul(pb[gb][:, 0:512], WG2[ws][:, k * 128:(k + 1) * 128],
                                    MO[:, k * TC:(k + 1) * TC], start=(k == 0), stop=(k == 15))
            for k in range(16):
                @E("pe", reads=(f"wu2{ws}", "mo"), writes=(f"bk{ub}",))
                def _(e, k=k, ub=ub, ws=ws):
                    return e.matmul(pb[ub][:, 0:512], WU2[ws][:, k * 128:(k + 1) * 128],
                                    MO[:, k * TC:(k + 1) * TC], start=(k == 0), stop=(k == 15))

            @E("ac", reads=(f"bk{gb}", "c_sg"), writes=("gt",))
            def _(e, ii=ii, gb=gb):
                return e.activation(s_gt[:], pb[gb][:, 0:512], AF.Silu, scale=s_sg[:, ii:ii + 1])

            @E("dv", reads=(f"bk{ub}", "gt", "c_su"), writes=("fu",))
            def _(e, ii=ii, ub=ub):
                return e.scalar_tensor_tensor(FU(ii), pb[ub][:, 0:512], s_su[:, ii:ii + 1],
                                              s_gt[:], ALU.mult, ALU.mult)

        for di in range(16):
            dbk = 4 + (di % 2)
            for q in range(4):
                @E("sy", reads=WDC, writes=(f"wdl{q % 2}",), dma=f"d_wd{q % 2}")
                def _(e, di=di, q=q):
                    return e.dma_start(WD2(q % 2),
                                       wdb[di * 128:(di + 1) * 128, q * 2048:(q + 1) * 2048])
                for k in range(q * 16, (q + 1) * 16):
                    @E("pe", reads=(f"wdl{q % 2}", "fu"), writes=(f"bk{dbk}",))
                    def _(e, k=k, q=q, dbk=dbk):
                        return e.matmul(pb[dbk][:, 0:512],
                                        WD2(q % 2)[:, (k % 16) * 128:(k % 16 + 1) * 128],
                                        FU(k), start=(k == 0), stop=(k == 63))

            @E("ac", reads=(f"bk{dbk}", "c_sd"), writes=("mo",))
            def _(e, di=di, dbk=dbk):
                return e.mul(MO[:, di * TC:(di + 1) * TC], pb[dbk][:, 0:512], s_sd[:, di:di + 1])

        ln_stats_rows("mo", lambda k: MO[:, k * TC:(k + 1) * TC], 16)

        def emit_out_tile(di, c0=c0):
            os_ = di % 2

            @E("dv", reads=("mo", "bk4"), writes=(f"ot{os_}",))
            def _(e, di=di, os_=os_):
                return e.tensor_tensor(s_otile[os_][:], MO[:, di * TC:(di + 1) * TC],
                                       pb[4][:, 0:512], ALU.mult)

            @E("dv", reads=("bk5",), writes=(f"ot{os_}",))
            def _(e, di=di, os_=os_):
                return e.tensor_tensor(s_otile[os_][:], s_otile[os_][:],
                                       pb[5][:, 0:512], ALU.add)

            @E("dv", reads=("x2",), writes=(f"ot{os_}",))
            def _(e, di=di, os_=os_):
                return e.tensor_tensor(s_otile[os_][:], s_otile[os_][:],
                                       X2[:, di * TC:(di + 1) * TC], ALU.add)

            @E("sy", reads=(f"ot{os_}",), writes=(f"otd{os_}",), dma=f"d_ot{os_}")
            def _(e, c0=c0, di=di, os_=os_):
                return e.dma_start(out_d[di * 128:(di + 1) * 128, c0:c0 + TC], s_otile[os_][:])
        pending_out[0] = emit_out_tile

    for di in range(16):
        pending_out[0](di)
    pending_out[0] = None

    if trunc is not None:
        cut = {}
        for eng in ops:
            for fn, dd, tok, sq in ops[eng]:
                if sq <= trunc and tok is not None:
                    cut[tok[0]] = max(cut.get(tok[0], 0), tok[1])
        FINAL = list(cut.items())
    else:
        FINAL = all_tokens()
    ops["gp"].append(((lambda e: None), tuple(FINAL), None, 0))

    with nc.Block() as blk:
        blk.tensor(lambda e: replay(e, "pe"))
        blk.vector(lambda e: replay(e, "dv"))
        blk.scalar(lambda e: replay(e, "ac"))
        blk.sync(lambda e: replay(e, "sy"))
        blk.gpsimd(lambda e: replay(e, "gp"))

    stk.close()
    return nc


_NC_CACHE = {}


def _host_inputs(inputs, nchunk=NCHUNK):
    th = nchunk * TC
    f32 = np.float32
    Win = np.asarray(inputs["L0_attn_Win"], f32)
    Wout = np.asarray(inputs["L0_attn_Wout"], f32)
    wg = np.asarray(inputs["L0_ffn_w_gate"])
    wu = np.asarray(inputs["L0_ffn_w_up"])
    wd = np.asarray(inputs["L0_ffn_w_down"])
    cwx = np.asarray(inputs["L0_attn_conv_w_x"], f32)
    cwb = np.asarray(inputs["L0_attn_conv_w_b"], f32)
    cwc = np.asarray(inputs["L0_attn_conv_w_c"], f32)
    x = np.asarray(inputs["L0_attn_x"], f32)

    WinP = np.zeros((2048, NCI * 128), f32)
    WinP[:, :PD] = Win.T
    win_t = np.ascontiguousarray(
        WinP.reshape(16, 128, NCI, 128).transpose(2, 1, 0, 3)).reshape(NCI * 128, 2048)
    wout_t = np.ascontiguousarray(
        Wout.T.reshape(16, 128, 16, 128).transpose(2, 1, 0, 3)).reshape(16 * 128, 2048)
    wg_t = np.ascontiguousarray(
        wg.reshape(16, 128, 64, 128).transpose(2, 1, 0, 3)).reshape(64 * 128, 2048)
    wu_t = np.ascontiguousarray(
        wu.reshape(16, 128, 64, 128).transpose(2, 1, 0, 3)).reshape(64 * 128, 2048)
    wd_t = np.ascontiguousarray(
        wd.reshape(64, 128, 16, 128).transpose(2, 1, 0, 3)).reshape(16 * 128, 8192)

    wvec = np.concatenate([cwx.reshape(2048, 4), cwb.reshape(4096, 4), cwc.reshape(4096, 4)], 0)
    conv_t = np.zeros((80, 128, 4, 128), f32)
    rr = np.arange(128)
    for ti in range(80):
        for k in range(4):
            conv_t[ti, rr, k, rr] = wvec[ti * 128 + rr, k]
    conv_t = conv_t.reshape(80 * 128, 512)

    consts = np.zeros((128, 8), f32)
    consts[:, 0] = 1e-5
    consts[:, 1] = 1e-6
    consts[:, 2] = -0.5
    consts[:, 3] = 1.0 / 2048.0
    consts[:, 4] = 1.0 / 64.0
    consts[:, 5] = 1.0

    com = {
        "win_t": win_t, "wout_t": wout_t, "wg_t": wg_t, "wu_t": wu_t, "wd_t": wd_t,
        "conv_t": conv_t,
        "alog": np.asarray(inputs["L0_attn_A_log"], f32).reshape(32, 1),
        "dtb": np.asarray(inputs["L0_attn_dt_bias"], f32).reshape(32, 1),
        "sg_i": np.ascontiguousarray(np.asarray(inputs["L0_ffn_s_gate"], f32).reshape(64, 128).T),
        "su_i": np.ascontiguousarray(np.asarray(inputs["L0_ffn_s_up"], f32).reshape(64, 128).T),
        "sd_i": np.ascontiguousarray(np.asarray(inputs["L0_ffn_s_down"], f32).reshape(16, 128).T),
        "ident_bf": np.asarray(np.eye(128, dtype=ml_dtypes.bfloat16)),
        "ident_f": np.eye(128, dtype=f32),
        "umask_i": np.asarray(np.triu(np.ones((128, 128), ml_dtypes.bfloat16))),
        "onesr_i": np.ones((1, 128), f32),
        "onec_i": np.ones((128, 1), f32),
        "onecb_i": np.ones((128, 1), ml_dtypes.bfloat16),
        "negonesr_i": -np.ones((1, 128), f32),
        "consts_i": consts,
    }
    nonce = (np.frombuffer(np.random.bytes(4), dtype=np.uint32)[0] & np.uint32(0x7FFFFFFF)) | np.uint32(1)
    in_maps = []
    for c in range(NCORES):
        b, hf = c // 2, c % 2
        sl = x[b, hf * th:(hf + 1) * th, :]
        xTc = np.ascontiguousarray(sl.T)
        x3 = np.zeros((D, 4), f32)
        if hf == 1:
            x3[:, 0:3] = x[b, th - 3:th, :].T
        even = (hf == 0)
        m = {"xT": xTc, "xh3": x3,
             "xidx_i": (np.arange(128, dtype=np.int32) if even
                        else np.arange(128, dtype=np.int32) + 130).reshape(128, 1),
             "fidx_i": (np.array([[0], [1]], np.int32) if even
                        else np.array([[4], [5]], np.int32)),
             "nonce_i": np.full((2, 1), nonce, np.uint32),
             "hmask_i": np.full((128, 1), 0.0 if even else 1.0, f32)}
        m.update(com)
        in_maps.append(m)
    return in_maps


def kernel(**inputs):
    debug = bool(inputs.pop("_debug", False))
    key = "dbg" if debug else "std"
    if key not in _NC_CACHE:
        _NC_CACHE[key] = build_program(debug=debug)
    nc = _NC_CACHE[key]
    in_maps = _host_inputs(inputs)
    res = run_bass_kernel_spmd(nc, in_maps, core_ids=list(range(NCORES)))
    out = np.empty((B, T, D), np.float32)
    for c in range(NCORES):
        b, hf = c // 2, c % 2
        out[b, hf * THALF:(hf + 1) * THALF, :] = res.results[c]["out_d"].T
    if debug:
        return out, res
    return out
